# version 01
# baseline (speedup 1.0000x reference)
import contextlib
import numpy as np
import concourse.bass as bass
import concourse.mybir as mybir
from concourse.bass_utils import run_bass_kernel_spmd

F32 = mybir.dt.float32
BF16 = mybir.dt.bfloat16
AF = mybir.ActivationFunctionType
ALU = mybir.AluOpType
AX = mybir.AxisListType

NCORES = 8
SEQ = 4096
DM = 1024
NSEQ = 2
TOK = NSEQ * SEQ

ENG_NAMES = ("sync", "tensor", "vector", "scalar", "gpsimd")


def ap(t, off, dims):
    return bass.AP(t, off, [list(d) for d in dims])


class Builder:
    def __init__(self, nc):
        self.nc = nc
        self.sems = {}
        self._stack = contextlib.ExitStack()
        self.reset_queues()

    def reset_queues(self):
        self.q = {e: [] for e in ENG_NAMES}
        self.last_wait = {e: {} for e in ENG_NAMES}

    def sem(self, name, dma=False):
        h = self._stack.enter_context(self.nc.semaphore(name))
        self.sems[name] = [h, 0, 0]
        if not hasattr(self, "dma_sems"):
            self.dma_sems = set()
        if dma:
            self.dma_sems.add(name)

    def op(self, eng, fn, after=(), inc=None, inc_n=1):
        waits = []
        flat = []
        for t in after:
            if t is None:
                continue
            if isinstance(t, list):
                flat.extend(x for x in t if x is not None)
            else:
                flat.append(t)
        for t in flat:
            sname, val = t
            s = self.sems[sname]
            if val <= s[2]:
                continue
            lw = self.last_wait[eng]
            if lw.get(sname, -1) >= val:
                continue
            lw[sname] = val
            waits.append((sname, val))
        tok = None
        if inc is not None:
            s = self.sems[inc]
            s[1] += inc_n
            tok = (inc, s[1])
        self.q[eng].append((waits, fn, inc, inc_n))
        return tok

    def dma(self, queue, out, in_, after=(), inc="dma", serial=False, **kw):
        if serial:
            st = self.sems[inc]
            if st[1] > 0:
                self.op(queue, None, after=[(inc, st[1])])
        return self.op(queue, lambda e: e.dma_start(out=out, in_=in_, **kw),
                       after=after, inc=inc, inc_n=16)

    def wait_all(self, eng, sem):
        s = self.sems[sem]
        if s[1] > s[2]:
            self.op(eng, lambda e: e.wait_ge(s[0], s[1]))

    def flush(self):
        nc = self.nc
        qs = self.q
        sems = self.sems
        with nc.Block() as blk:
            for ename in ENG_NAMES:
                if not qs[ename]:
                    continue
                ops = qs[ename]

                def make(ops=ops):
                    def run(eng):
                        for waits, fn, inc, inc_n in ops:
                            for sname, val in waits:
                                eng.wait_ge(sems[sname][0], val)
                            if fn is None:
                                continue
                            ins = fn(eng)
                            if inc is not None:
                                ins.then_inc(sems[inc][0], inc_n)
                    return run
                getattr(blk, ename)(make())
        for name, s in self.sems.items():
            if name not in getattr(self, "dma_sems", set()):
                s[2] = s[1]
        self.reset_queues()


def host_consts():
    c = {}
    DH, WIN = 64, 512
    half = np.arange(0, DH, 2, dtype=np.float64)
    inv_freq = 1.0 / 10000.0 ** (half / DH)
    t = np.arange(SEQ, dtype=np.float64)
    freqs = t[:, None] * inv_freq[None, :]
    base = (half + 0.4 * DH) / (1.4 * DH)
    power = (t - SEQ // 2) / float(WIN // 2)
    scale = base[None, :] ** power[:, None]
    cos = np.cos(freqs)
    sin = np.sin(freqs)
    signs = np.concatenate([-np.ones(32), np.ones(32)])

    def lay(a):
        return np.ascontiguousarray(
            a.reshape(32, 128, 64).transpose(1, 0, 2).reshape(128, 2048)
        ).astype(np.float32)

    c["tab_csq"] = lay(np.concatenate([cos * scale, cos * scale], 1))
    c["tab_ssq"] = lay(np.concatenate([sin * scale, sin * scale], 1) * signs)
    c["tab_csk"] = lay(np.concatenate([cos / scale, cos / scale], 1))
    c["tab_ssk"] = lay(np.concatenate([sin / scale, sin / scale], 1) * signs)

    fr_b = 10000.0 ** (-np.arange(0, 64, 2, dtype=np.float64) / 64)
    pos = np.arange(SEQ, dtype=np.float64) - 1.0
    ang = pos[:, None] * fr_b[None, :]
    cb = np.cos(ang)
    sb = np.sin(ang)
    cb[0, :] = 1.0
    sb[0, :] = 0.0
    c["tab_csb"] = lay(np.concatenate([cb, cb], 1))
    c["tab_ssb"] = lay(np.concatenate([sb, sb], 1) * signs)

    H = np.array([[1.0]])
    while H.shape[0] < 64:
        H = np.block([[H, H], [H, -H]])
    sig = np.concatenate([np.arange(0, 64, 2), np.arange(1, 64, 2)])
    c["had"] = np.ascontiguousarray(H.astype(np.float32))
    PT = np.zeros((64, 64))
    PT[sig, np.arange(64)] = 1.0
    PT2 = np.zeros((128, 128))
    PT2[0:64, 0:64] = PT
    PT2[64:128, 64:128] = PT
    c["permT"] = np.ascontiguousarray(PT2.astype(np.float32))
    return c


class MK:
    def __init__(self, phases=None, dbg=()):
        self.phases = phases
        self.dbg = set(dbg)
        self.nc = bass.Bass("TRN2", target_bir_lowering=False)
        self.st = contextlib.ExitStack()
        self.B = Builder(self.nc)
        for s in ("pe", "dv", "ac", "pl"):
            self.B.sem(s)
        for s in ("ld0", "ld1", "st", "wld", "sto", "sh0", "sh1", "dma"):
            self.B.sem(s, dma=True)

    def sel(self, p):
        return self.phases is None or p in self.phases

    def sbuf(self, name, shape, dt, stack=None):
        s = stack if stack is not None else self.st
        return s.enter_context(self.nc.sbuf_tensor(name, list(shape), dt))

    def psum(self, name, shape, dt, stack):
        return stack.enter_context(self.nc.psum_tensor(name, list(shape), dt))


WSPECS = {
    "L0_attn_a_ln_w": (1024,), "L0_attn_a_ln_b": (1024,),
    "L0_attn_a_W_in": (1024, 3072), "L0_attn_a_conv_w": (1536, 3),
    "L0_attn_a_A": (16, 16), "L0_attn_a_Bm": (16, 1536),
    "L0_attn_a_Cm": (1536, 16), "L0_attn_a_W_out": (1536, 1024),
    "L0_attn_b_w_qkv": (3072, 1024), "L0_attn_b_w_out": (1024, 1024),
    "L0_attn_b_sorf_signs": (16, 4, 3, 64), "L0_attn_b_sorf_scales": (16, 4, 64),
    "L0_ffn_wqkv": (2, 1024, 1536), "L0_ffn_q_scale": (2, 64),
    "L0_ffn_k_scale": (2, 64), "L0_ffn_wo": (2, 512, 1024),
    "L0_ffn_ln_g": (2, 1024), "L0_ffn_ln_b": (2, 1024),
    "L0_ffn_w1": (2, 1024, 5460), "L0_ffn_b1": (2, 5460),
    "L0_ffn_w2": (2, 2730, 1024), "L0_ffn_b2": (2, 1024),
    "tab_csq": (128, 2048), "tab_ssq": (128, 2048),
    "tab_csk": (128, 2048), "tab_ssk": (128, 2048),
    "tab_csb": (128, 2048), "tab_ssb": (128, 2048),
    "had": (64, 64), "permT": (128, 128),
}


def build_program(phases=None, dbg=()):
    mk = MK(phases, dbg)
    nc, B = mk.nc, mk.B
    sel = mk.sel

    W = {n: nc.dram_tensor(n, list(s), F32, kind="ExternalInput")
         for n, s in WSPECS.items()}
    x_in = nc.dram_tensor("x", [TOK, DM], F32, kind="ExternalInput")

    def stream(name, produced, consumers, dt=F32, shape=(TOK, DM)):
        if phases is None:
            kind = "ExternalOutput" if name == "OUT" else None
        else:
            p_in = sel(produced)
            c_in = any(sel(c) for c in consumers)
            if not p_in and not c_in:
                return None
            if name == "OUT":
                kind = "ExternalOutput"
            elif p_in and c_in:
                kind = "ExternalOutput" if name in mk.dbg else None
            elif p_in:
                kind = "ExternalOutput"
            else:
                kind = "ExternalInput"
        if kind is None:
            return nc.dram_tensor(name, list(shape), dt)
        return nc.dram_tensor(name, list(shape), dt, kind=kind)

    S1 = stream("S1", "p1", ["f0a", "f0d"])
    S2 = stream("S2", "f0a", ["f0b"])
    H1 = stream("H1", "f0a", ["f0b"], BF16)
    S3 = stream("S3", "f0b", ["f0c"])
    S4 = stream("S4", "f0c", ["f0d"])
    H3 = stream("H3", "f0c", ["f0d"], BF16)
    S5 = stream("S5", "f0d", ["ba", "bb"])
    QROT = stream("QROT", "ba", ["bb"], BF16)
    BQ = stream("BQ", "ba", ["bb"], F32, (TOK, 16))
    SKV = stream("SKV", "ba", ["bb"], F32, (NSEQ * 128, 16 * 2 * 65))
    S6 = stream("S6", "bb", ["f1a", "f1d"])
    S7 = stream("S7", "f1a", ["f1b"])
    H7 = stream("H7", "f1a", ["f1b"], BF16)
    S8 = stream("S8", "f1b", ["f1c"])
    S9 = stream("S9", "f1c", ["f1d"])
    H9 = stream("H9", "f1c", ["f1d"], BF16)
    OUT = stream("OUT", "f1d", [])

    mk.cpool = mk.sbuf("cpool", (128, 8), F32)
    CVALS = [1e-5, 1e-10, -2.7725887222397811, 1.0, 0.0, 0.5, -0.5, 2.0]
    mk.cmap = {v: i for i, v in enumerate(CVALS)}
    mk.cbias = lambda v: mk.cpool[:, mk.cmap[v]:mk.cmap[v] + 1]
    for v, i in mk.cmap.items():
        B.op("gpsimd", lambda e, v=v, i=i: e.memset(mk.cpool[:, i:i + 1], v), inc="pl")
    mk.identb = mk.sbuf("identb", (128, 128), BF16)
    mk.identf = mk.sbuf("identf", (128, 128), F32)
    mk.ones = mk.sbuf("onesrow", (1, 2048), F32)
    mk.onesb = mk.sbuf("onesrowb", (1, 2048), BF16)

    def aff_ident(t):
        def f(e, t=t):
            return e.affine_select(out=t[:], in_=t[:], compare_op=ALU.not_equal,
                                   fill=1.0, base=0, pattern=[[-1, 128]],
                                   channel_multiplier=1)
        return f
    t0 = B.op("gpsimd", lambda e: e.memset(mk.identb[:], 0.0), inc="pl")
    B.op("gpsimd", aff_ident(mk.identb), after=[t0], inc="pl")
    t1 = B.op("gpsimd", lambda e: e.memset(mk.identf[:], 0.0), inc="pl")
    B.op("gpsimd", aff_ident(mk.identf), after=[t1], inc="pl")
    B.op("gpsimd", lambda e: e.memset(mk.ones[:], 1.0), inc="pl")
    B.op("gpsimd", lambda e: e.memset(mk.onesb[:], 1.0), inc="pl")
    B.flush()

    if sel("p1"):
        p1_attn_a(mk, W, x_in, S1)
    if sel("f0a"):
        ffn_attn(mk, W, 0, "f0a", S1, S2, H1, outer_ln=True)
    if sel("f0b"):
        ffn_mlp(mk, W, 0, "f0b", S2, H1, S3, extra=None)
    if sel("f0c"):
        ffn_attn(mk, W, 1, "f0c", S3, S4, H3, outer_ln=False)
    if sel("f0d"):
        ffn_mlp(mk, W, 1, "f0d", S4, H3, S5, extra=S1)
    if sel("ba"):
        attn_b_passA(mk, W, S5, QROT, BQ, SKV)
    if sel("bb"):
        attn_b_passB(mk, W, S5, QROT, BQ, SKV, S6)
    if sel("f1a"):
        ffn_attn(mk, W, 0, "f1a", S6, S7, H7, outer_ln=True)
    if sel("f1b"):
        ffn_mlp(mk, W, 0, "f1b", S7, H7, S8, extra=None)
    if sel("f1c"):
        ffn_attn(mk, W, 1, "f1c", S8, S9, H9, outer_ln=False)
    if sel("f1d"):
        ffn_mlp(mk, W, 1, "f1d", S9, H9, OUT, extra=S6)

    return mk



def p1_attn_a(mk, W, x_in, S1):
    nc, B = mk.nc, mk.B
    TILE = 256
    NSUB = TILE // 128
    NT = SEQ // TILE
    ND = 64
    NSEQ_L = NSEQ
    if "nt1" in mk.dbg:
        NT, NSEQ_L = 1, 1
    for f in mk.dbg:
        if f.startswith("ntx"):
            NT, NSEQ_L = int(f[3:]), 1

    stk = contextlib.ExitStack()
    sb = lambda n, shp, dt: mk.sbuf(n, shp, dt, stack=stk)
    idf, idb, ones = mk.identf, mk.identb, mk.ones

    Wp = sb("a_Wp", (128, 8 * 3072), BF16)
    Wo = sb("a_Wo", (128, 12 * 1024), BF16)
    pb = sb("a_pb", (1, 3072), F32)
    bmT = sb("a_bmT", (128, 12 * 16), BF16)
    cmT = sb("a_cmT", (16, 12 * 128), BF16)
    mpow = sb("a_mpow", (16, ND * 16), BF16)
    cw = sb("a_cw", (128, 36), F32)

    skip_setup = "nosetup" in mk.dbg
    sstk = contextlib.ExitStack()
    ssb = lambda n, shp, dt: mk.sbuf(n, shp, dt, stack=sstk)
    wtmp = ssb("a_wtmp", (128, 3072), F32)
    lnw_r = ssb("a_lnwr", (1, 1024), F32)
    lnb_r = ssb("a_lnbr", (1, 1024), F32)
    lnw_c = ssb("a_lnwc", (128, 8), F32)
    lnb_c = ssb("a_lnbc", (128, 8), F32)
    bm_r = ssb("a_bmr", (16, 1536), F32)
    m_sb = ssb("a_msb", (16, 16), F32)
    x_sb = ssb("a_xsb", (16, 16), F32)
    mm_sb = ssb("a_mmsb", (16, 16), F32)
    ps_a = mk.psum("a_ps_a", (128, 512), F32, sstk)
    ps_b = mk.psum("a_ps_b", (128, 512), F32, sstk)
    ps_pb = mk.psum("a_ps_pb", (1, 3072), F32, sstk)

    if skip_setup:
        B.flush()
        sstk.close()
    else:
      tlw = B.dma("gpsimd", lnw_r[:], ap(W["L0_attn_a_ln_w"], 0, [[0, 1], [1, 1024]]), inc="wld", serial=True)
      tlb = B.dma("gpsimd", lnb_r[:], ap(W["L0_attn_a_ln_b"], 0, [[0, 1], [1, 1024]]), inc="wld", serial=True)
      tt_all = []
      for c in range(8):
          tt_all.append(B.op("tensor", lambda e, c=c: e.transpose(
              ps_a[:, c:c + 1], lnw_r[0:1, 128 * c:128 * (c + 1)], idf[0:1, 0:1]),
              after=[tlw], inc="pe"))
          tt_all.append(B.op("tensor", lambda e, c=c: e.transpose(
              ps_b[:, c:c + 1], lnb_r[0:1, 128 * c:128 * (c + 1)], idf[0:1, 0:1]),
              after=[tlb], inc="pe"))
      tw = B.op("vector", lambda e: e.tensor_copy(lnw_c[:], ps_a[:, 0:8]), after=tt_all, inc="dv")
      tb = B.op("vector", lambda e: e.tensor_copy(lnb_c[:], ps_b[:, 0:8]), after=tt_all, inc="dv")

      prev_k = [tw, tb]
      for kc in range(8):
          tl = B.dma("gpsimd", wtmp[:],
                     ap(W["L0_attn_a_W_in"], kc * 128 * 3072, [[3072, 128], [1, 3072]]),
                     after=prev_k, inc="wld", serial=True)
          tv = B.op("vector", lambda e, kc=kc: e.tensor_scalar(
              Wp[:, 3072 * kc:3072 * (kc + 1)], wtmp[:], lnw_c[:, kc:kc + 1], None,
              ALU.mult), after=[tl, tw], inc="dv")
          cur = [tv]
          for ng in range(6):
              cur.append(B.op("tensor", lambda e, kc=kc, ng=ng: e.matmul(
                  ps_pb[0:1, 512 * ng:512 * (ng + 1)], lnb_c[:, kc:kc + 1],
                  wtmp[:, 512 * ng:512 * (ng + 1)], start=(kc == 0), stop=(kc == 7)),
                  after=[tl, tb], inc="pe"))
          prev_k = cur
      tpb = B.op("vector", lambda e: e.tensor_copy(pb[:], ps_pb[:]), after=prev_k, inc="dv")

      two = [B.dma("gpsimd", Wo[:, 1024 * c:1024 * (c + 1)],
                   ap(W["L0_attn_a_W_out"], c * 128 * 1024, [[1024, 128], [1, 1024]]),
                   inc="wld", serial=True) for c in range(12)]
      B.op("vector", lambda e: e.tensor_scalar(Wo[:], Wo[:], 0.5, None, ALU.mult),
           after=two, inc="dv")

      tbm = B.dma("gpsimd", bm_r[:], ap(W["L0_attn_a_Bm"], 0, [[1536, 16], [1, 1536]]), inc="wld", serial=True)
      prev = [tw, tb]
      for c in range(12):
          tt = B.op("tensor", lambda e, c=c: e.transpose(
              ps_a[:, 0:16], bm_r[0:16, 128 * c:128 * (c + 1)], idf[0:16, 0:16]),
              after=[tbm] + prev, inc="pe")
          tv = B.op("vector", lambda e, c=c: e.tensor_scalar(
              bmT[:, 16 * c:16 * (c + 1)], ps_a[:, 0:16], 0.5, None, ALU.mult),
              after=[tt], inc="dv")
          prev = [tv]

      tcm = B.dma("gpsimd", wtmp[:, 0:192],
                  ap(W["L0_attn_a_Cm"], 0, [[16, 128], [2048, 12], [1, 16]]),
                  after=[tpb] + prev_k, inc="wld", serial=True)
      prevc = [tb]
      for c in range(12):
          tt = B.op("tensor", lambda e, c=c: e.transpose(
              ps_b[0:16, 0:128], wtmp[:, 16 * c:16 * (c + 1)], idf[:, 0:128]),
              after=[tcm] + prevc, inc="pe")
          tv = B.op("vector", lambda e, c=c: e.tensor_copy(
              cmT[0:16, 128 * c:128 * (c + 1)], ps_b[0:16, 0:128]), after=[tt], inc="dv")
          prevc = [tv]

      ta = B.dma("gpsimd", m_sb[:], ap(W["L0_attn_a_A"], 0, [[16, 16], [1, 16]]), inc="wld", serial=True)
      tmt = B.op("vector", lambda e: e.tensor_add(m_sb[:], m_sb[:], idf[0:16, 0:16]),
                 after=[ta], inc="dv")
      tmm = B.op("tensor", lambda e: e.transpose(ps_a[0:16, 16:32], m_sb[:], idf[0:16, 0:16]),
                 after=[tmt] + prev, inc="pe")
      tme = B.op("vector", lambda e: e.tensor_copy(mm_sb[:], ps_a[0:16, 16:32]),
                 after=[tmm], inc="dv")
      tx = B.op("vector", lambda e: e.tensor_copy(x_sb[:], idf[0:16, 0:16]), inc="dv")
      prevp = [tx, tme]
      for d in range(ND):
          tt = B.op("tensor", lambda e, d=d: e.transpose(
              ps_b[0:16, 0:16], x_sb[:], idf[0:16, 0:16]), after=prevp, inc="pe")
          te = B.op("vector", lambda e, d=d: e.tensor_copy(
              mpow[0:16, 16 * d:16 * (d + 1)], ps_b[0:16, 0:16]), after=[tt], inc="dv")
          if d < ND - 1:
              tm2 = B.op("tensor", lambda e: e.matmul(
                  ps_a[0:16, 0:16], mm_sb[:], x_sb[:], start=True, stop=True),
                  after=prevp + [tt], inc="pe")
              tx2 = B.op("vector", lambda e: e.tensor_copy(x_sb[:], ps_a[0:16, 0:16]),
                         after=[tm2, te], inc="dv")
              prevp = [tx2, te]
          else:
              prevp = [te]

      tcw = B.dma("gpsimd", cw[:, 0:36],
                  ap(W["L0_attn_a_conv_w"], 0, [[3, 128], [384, 12], [1, 3]]), inc="wld", serial=True)
      B.flush()
      sstk.close()

    xt = [sb(f"a_xt{i}", (128, NSUB * 1024), F32) for i in range(2)]
    xho = sb("a_xho", (128, NSUB * 1024), F32)
    h2n = sb("a_h2n", (128, NSUB * 1024), BF16)
    hfm = sb("a_hfm", (128, 8 * TILE), BF16)
    data = sb("a_data", (128, 12 * TILE), F32)
    dext = sb("a_dext", (128, 12 * (TILE + 2)), F32)
    tnh = sb("a_tnh", (128, 12 * TILE), F32)
    u_sb = sb("a_u", (128, 12 * TILE), BF16)
    gyb = sb("a_gy", (128, 12 * TILE), BF16)
    bu_pad = sb("a_bu", (16, ND + SEQ), BF16)
    st_sb = sb("a_st", (16, TILE), BF16)
    y_scr = sb("a_yscr", (128, TILE), F32)
    tg_scr = sb("a_tg", (128, TILE), F32)
    stats = sb("a_stats", (128, 16), F32)
    rr = sb("a_rr", (128, 2 * NSUB), F32)

    ps_tr = mk.psum("a_pstr", (128, TILE), BF16, stk)
    ps_d0 = mk.psum("a_psd0", (128, TILE), F32, stk)
    ps_d1 = mk.psum("a_psd1", (128, TILE), F32, stk)
    ps_bu = mk.psum("a_psbu", (16, TILE), F32, stk)
    ps_st = mk.psum("a_psst", (16, TILE), F32, stk)
    ps_y = mk.psum("a_psy", (128, TILE), F32, stk)
    ps_o = mk.psum("a_pso", (128, 1024), F32, stk)
    pd = [ps_d0, ps_d1]

    dbg = "p1dbg" in mk.dbg
    if dbg:
        DXHO = nc.dram_tensor("DXHO", [128, NSUB * 1024], F32, kind="ExternalOutput")
        DXT = nc.dram_tensor("DXT", [128, NSUB * 1024], F32, kind="ExternalOutput")
        DSTATS = nc.dram_tensor("DSTATS", [128, 16], F32, kind="ExternalOutput")
        DRR = nc.dram_tensor("DRR", [128, 2 * NSUB], F32, kind="ExternalOutput")
        DH2N = nc.dram_tensor("DH2N", [128, NSUB * 1024], F32, kind="ExternalOutput")
        DDATA = nc.dram_tensor("DDATA", [128, 12 * TILE], F32, kind="ExternalOutput")
        DU = nc.dram_tensor("DU", [128, 12 * TILE], F32, kind="ExternalOutput")
        DBU = nc.dram_tensor("DBU", [16, TILE], F32, kind="ExternalOutput")
        DST = nc.dram_tensor("DST", [16, TILE], F32, kind="ExternalOutput")
        DGY = nc.dram_tensor("DGY", [128, 12 * TILE], F32, kind="ExternalOutput")

    wtok = ("wld", B.sems["wld"][1])
    P = {
        "dbg_xho": None,
        "xt_free": [None, None],
        "xho_free": None,
        "stats_free": None, "rr_free": None, "hfm_free": wtok,
        "pd_free": [wtok, wtok],
        "tr_free": wtok, "y_free": wtok, "tg_free": wtok,
        "o_free": wtok, "bu_rd": wtok, "st_free": wtok,
        "u_free": wtok, "gy_free": wtok, "dext_free": wtok,
        "h2n_free": None, "data_free": wtok, "tnh_free": wtok,
    }

    for s in range(NSEQ_L):
        tz = B.op("vector", lambda e: e.memset(bu_pad[0:16, 0:ND], 0.0),
                  after=[P["bu_rd"]], inc="dv")
        tdz = B.op("vector", lambda e: e.memset(dext[:], 0.0),
                   after=[P["dext_free"]], inc="dv")
        P["dext_free"] = tdz
        for it in range(NT):
            bi = it % 2
            x_t = xt[bi]
            r0 = s * SEQ + it * TILE
            t_ld = B.dma("sync", ap(x_t, 0, [[NSUB * 1024, 128], [1024, NSUB], [1, 1024]]),
                         ap(x_in, r0 * 1024, [[1024, 128], [128 * 1024, NSUB], [1, 1024]]),
                         after=[P["xt_free"][bi]], inc=f"ld{bi}")

            tstats = []
            tprev = [t_ld, P["stats_free"]]
            for si in range(NSUB):
                o = 1024 * si
                b1 = B.op("vector", lambda e, o=o, x_t=x_t: e.bn_stats(
                    stats[:, 0:6], x_t[:, o:o + 512]), after=tprev, inc="dv")
                b2 = B.op("vector", lambda e, o=o, x_t=x_t: e.bn_stats(
                    stats[:, 6:12], x_t[:, o + 512:o + 1024]), after=[t_ld, b1], inc="dv")
                tag = B.op("vector", lambda e, si=si: e.bn_aggr(
                    stats[:, 12 + 2 * si:14 + 2 * si], stats[:, 0:12]),
                    after=[b1, b2], inc="dv")
                tstats.append(tag)
                tprev = [t_ld, tag]
            var_ap = ap(stats, 13, [[16, 128], [2, NSUB]])
            dstat = None
            if dbg and s == 0 and it == 0:
                dstat = B.dma("sync", ap(DSTATS, 0, [[16, 128], [1, 16]]),
                              stats[:], after=tstats, inc="sto", serial=True)
            tl1 = B.op("scalar", lambda e: e.activation(
                out=rr[:, 0:NSUB], in_=var_ap, func=AF.Ln, bias=mk.cbias(1e-5), scale=1.0),
                after=tstats + [P["rr_free"], dstat], inc="ac")
            tr1 = B.op("scalar", lambda e: e.activation(
                out=rr[:, 0:NSUB], in_=rr[:, 0:NSUB], func=AF.Exp, scale=-0.5),
                after=[tl1], inc="ac")
            tl2 = B.op("scalar", lambda e: e.activation(
                out=rr[:, NSUB:2 * NSUB], in_=var_ap, func=AF.Ln, bias=mk.cbias(1e-10),
                scale=1.0 + 1e-5), after=tstats + [P["rr_free"]], inc="ac")
            tr2 = B.op("scalar", lambda e: e.activation(
                out=rr[:, NSUB:2 * NSUB], in_=rr[:, NSUB:2 * NSUB], func=AF.Exp, scale=-0.5), after=[tl2], inc="ac")

            t_ho = []
            for si in range(NSUB):
                o = 1024 * si
                mu = ap(stats, 12 + 2 * si, [[16, 128], [1, 1]])
                ta1 = B.op("vector", lambda e, o=o, si=si, mu=mu, x_t=x_t: e.tensor_scalar(
                    xho[:, o:o + 1024], x_t[:, o:o + 1024], mu, rr[:, si:si + 1],
                    ALU.subtract, ALU.mult), after=[tr1, P["xho_free"]], inc="dv")
                ta2 = B.op("vector", lambda e, o=o, si=si, mu=mu, x_t=x_t: e.tensor_scalar(
                    h2n[:, o:o + 1024], x_t[:, o:o + 1024], mu,
                    rr[:, NSUB + si:NSUB + si + 1], ALU.subtract, ALU.mult),
                    after=[tr2, P["h2n_free"]], inc="dv")
                ta3 = B.op("vector", lambda e, o=o, x_t=x_t: e.tensor_add(
                    xho[:, o:o + 1024], xho[:, o:o + 1024], x_t[:, o:o + 1024]),
                    after=[ta1], inc="dv")
                t_ho += [ta2, ta3]
            P["xt_free"][bi] = t_ho[-1]
            P["stats_free"] = t_ho[-1]
            P["rr_free"] = t_ho[-1]

            if dbg and s == 0 and it == 0:
                d1 = B.dma("sync", ap(DXT, 0, [[NSUB * 1024, 128], [1, NSUB * 1024]]),
                      x_t[:], after=t_ho, inc="sto", serial=True)
                d2 = dstat
                d3 = B.dma("sync", ap(DRR, 0, [[2 * NSUB, 128], [1, 2 * NSUB]]),
                      rr[:], after=t_ho, inc="sto", serial=True)
                d4 = B.dma("sync", ap(DXHO, 0, [[NSUB * 1024, 128], [1, NSUB * 1024]]),
                      xho[:], after=t_ho, inc="sto", serial=True)
                d5 = B.dma("gpsimd", ap(DH2N, 0, [[NSUB * 1024, 128], [1, NSUB * 1024]]),
                      h2n[:], after=t_ho, inc="sto", serial=True)
                P["xt_free"][bi] = d1
                P["stats_free"] = d2
                P["rr_free"] = d3
                P["dbg_xho"] = d4
                P["h2n_free"] = d5

            if "cut1" in mk.dbg:
                continue
            t_tr = []
            tr_guard = [P["data_free"], P["hfm_free"]]
            prev_tt = [P["tr_free"]]
            for c in range(8):
                tts = []
                for si in range(NSUB):
                    tts.append(B.op("tensor", lambda e, c=c, si=si: e.transpose(
                        ps_tr[:, 128 * si:128 * (si + 1)],
                        h2n[:, 1024 * si + 128 * c:1024 * si + 128 * (c + 1)],
                        idb[:]), after=t_ho + prev_tt, inc="pe"))
                te = B.op("vector", lambda e, c=c: e.tensor_copy(
                    hfm[:, TILE * c:TILE * (c + 1)], ps_tr[:, 0:TILE]),
                    after=tts + tr_guard, inc="dv")
                t_tr.append(te)
                prev_tt = [te]
            P["tr_free"] = t_tr[-1]
            P["h2n_free"] = t_tr[-1]

            t_data = []
            for mc in range(12):
                pp = pd[mc % 2]
                guard = [t_data[mc - 2]] if mc >= 2 else [P["pd_free"][mc % 2]]
                tms = []
                for kc in range(8):
                    tms.append(B.op("tensor", lambda e, mc=mc, kc=kc, pp=pp: e.matmul(
                        pp[:], Wp[:, 3072 * kc + 128 * mc:3072 * kc + 128 * (mc + 1)],
                        hfm[:, TILE * kc:TILE * (kc + 1)], start=(kc == 0), stop=False),
                        after=(t_tr + guard if kc == 0 else []), inc="pe"))
                tms.append(B.op("tensor", lambda e, mc=mc, pp=pp: e.matmul(
                    pp[:], pb[0:1, 128 * mc:128 * (mc + 1)], ones[0:1, 0:TILE],
                    start=False, stop=True), inc="pe"))
                te = B.op("scalar", lambda e, mc=mc, pp=pp: e.copy(
                    out=data[:, TILE * mc:TILE * (mc + 1)], in_=pp[:]),
                    after=tms + [P["tnh_free"] if False else None], inc="ac")
                t_data.append(te)

            if dbg and s == 0 and it == 0:
                d6 = B.dma("sync", ap(DDATA, 0, [[12 * TILE, 128], [1, 12 * TILE]]),
                      data[:], after=t_data, inc="sto", serial=True)
                P["tnh_free"] = d6
            tcp = B.op("vector", lambda e: e.tensor_copy(
                ap(dext, 2, [[12 * (TILE + 2), 128], [TILE + 2, 12], [1, TILE]]),
                ap(data, 0, [[12 * TILE, 128], [TILE, 12], [1, TILE]])),
                after=t_data + [P["dext_free"]], inc="dv")
            t_conv = []
            for c in range(12):
                ob = c * (TILE + 2)
                tc1 = B.op("vector", lambda e, ob=ob, c=c: e.tensor_scalar(
                    tnh[:, TILE * c:TILE * (c + 1)], dext[:, ob:ob + TILE],
                    cw[:, 3 * c:3 * c + 1], None, ALU.mult),
                    after=[tcp, P["tnh_free"]], inc="dv")
                tc2 = B.op("vector", lambda e, ob=ob, c=c: e.scalar_tensor_tensor(
                    tnh[:, TILE * c:TILE * (c + 1)], dext[:, ob + 1:ob + 1 + TILE],
                    cw[:, 3 * c + 1:3 * c + 2], tnh[:, TILE * c:TILE * (c + 1)],
                    ALU.mult, ALU.add), after=[tc1], inc="dv")
                tc3 = B.op("vector", lambda e, ob=ob, c=c: e.scalar_tensor_tensor(
                    data[:, TILE * c:TILE * (c + 1)], dext[:, ob + 2:ob + 2 + TILE],
                    cw[:, 3 * c + 2:3 * c + 3], tnh[:, TILE * c:TILE * (c + 1)],
                    ALU.mult, ALU.add), after=[tc2], inc="dv")
                t_conv.append(tc3)
            tcar = B.op("vector", lambda e: e.tensor_copy(
                ap(dext, 0, [[12 * (TILE + 2), 128], [TILE + 2, 12], [1, 2]]),
                ap(dext, TILE, [[12 * (TILE + 2), 128], [TILE + 2, 12], [1, 2]])),
                after=t_conv, inc="dv")
            P["dext_free"] = tcar
            P["data_free"] = t_conv[-1]
            tth = B.op("scalar", lambda e: e.activation(
                out=tnh[:], in_=data[:, 0:12 * TILE], func=AF.Tanh, scale=0.5),
                after=t_conv, inc="ac")
            P["tnh_free"] = tth
            tu = B.op("vector", lambda e: e.scalar_tensor_tensor(
                u_sb[:], tnh[:], 1.0, data[:, 0:12 * TILE], ALU.add, ALU.mult),
                after=[tth, P["u_free"]], inc="dv")
            P["data_free"] = tu
            P["tnh_free"] = tu

            if dbg and s == 0 and it == 0:
                P["u_free"] = B.dma("gpsimd", ap(DU, 0, [[12 * TILE, 128], [1, 12 * TILE]]),
                      u_sb[:], after=[tu], inc="sto", serial=True)
            if "cut2" in mk.dbg:
                continue
            tbu_m = []
            for kc in range(12):
                tbu_m.append(B.op("tensor", lambda e, kc=kc: e.matmul(
                    ps_bu[:], bmT[:, 16 * kc:16 * (kc + 1)],
                    u_sb[:, TILE * kc:TILE * (kc + 1)], start=(kc == 0),
                    stop=(kc == 11)), after=[tu] + ([P["bu_psfree"]] if "bu_psfree" in P else []), inc="pe"))
            o_bu = ND + it * TILE
            tbu = B.op("vector", lambda e, o_bu=o_bu: e.tensor_copy(
                bu_pad[0:16, o_bu:o_bu + TILE], ps_bu[:]), after=tbu_m + [tz], inc="dv")
            P["bu_psfree"] = tbu
            P["u_free"] = tbu_m[-1]

            if dbg and s == 0 and it == 0:
                B.dma("gpsimd", ap(DBU, 0, [[TILE, 16], [1, TILE]]),
                      bu_pad[0:16, o_bu:o_bu + TILE], after=[tbu], inc="sto", serial=True)
            tst_m = []
            for d in range(ND):
                tst_m.append(B.op("tensor", lambda e, d=d, o_bu=o_bu: e.matmul(
                    ps_st[:], mpow[0:16, 16 * d:16 * (d + 1)],
                    bu_pad[0:16, o_bu - d:o_bu - d + TILE],
                    start=(d == 0), stop=(d == ND - 1)),
                    after=[tbu, P["st_free"]], inc="pe"))
            tst = B.op("vector", lambda e: e.tensor_copy(st_sb[:], ps_st[:]),
                       after=tst_m, inc="dv")
            P["st_free"] = tst
            P["bu_rd"] = tst_m[-1]

            if dbg and s == 0 and it == 0:
                P["st_free"] = B.dma("gpsimd", ap(DST, 0, [[TILE, 16], [1, TILE]]),
                      st_sb[:], after=[tst], inc="sto", serial=True)
            t_gy = []
            prev_ye = None
            prev_ge = None
            for mc in range(12):
                ty_m = B.op("tensor", lambda e, mc=mc: e.matmul(
                    ps_y[:], cmT[0:16, 128 * mc:128 * (mc + 1)], st_sb[:],
                    start=True, stop=True), after=[tst, P["y_free"]], inc="pe")
                ty_e = B.op("scalar", lambda e: e.copy(out=y_scr[:], in_=ps_y[:]),
                            after=[ty_m] + t_gy[-1:], inc="ac")
                P["y_free"] = ty_e
                pp = pd[mc % 2]
                guard = [t_gy[mc - 2]] if mc >= 2 else [t_data[10 + mc]]
                tg_m = []
                for kc in range(8):
                    tg_m.append(B.op("tensor", lambda e, mc=mc, kc=kc, pp=pp: e.matmul(
                        pp[:], Wp[:, 3072 * kc + 128 * (12 + mc):3072 * kc + 128 * (13 + mc)],
                        hfm[:, TILE * kc:TILE * (kc + 1)], start=(kc == 0), stop=False),
                        after=(guard if kc == 0 else []), inc="pe"))
                tg_m.append(B.op("tensor", lambda e, mc=mc, pp=pp: e.matmul(
                    pp[:], pb[0:1, 128 * (12 + mc):128 * (13 + mc)], ones[0:1, 0:TILE],
                    start=False, stop=True), inc="pe"))
                tg_e = B.op("scalar", lambda e, pp=pp: e.activation(
                    out=tg_scr[:], in_=pp[:], func=AF.Tanh, scale=0.5),
                    after=tg_m + t_gy[-1:], inc="ac")
                tgy = B.op("vector", lambda e, mc=mc: e.scalar_tensor_tensor(
                    gyb[:, TILE * mc:TILE * (mc + 1)], tg_scr[:], 1.0, y_scr[:],
                    ALU.add, ALU.mult), after=[tg_e, ty_e, P["gy_free"]], inc="dv")
                t_gy.append(tgy)
            P["hfm_free"] = tg_m[-1]
            P["pd_free"] = [t_gy[-2], t_gy[-1]]

            if dbg and s == 0 and it == 0:
                P["gy_free"] = B.dma("gpsimd", ap(DGY, 0, [[12 * TILE, 128], [1, 12 * TILE]]),
                      gyb[:], after=t_gy, inc="sto", serial=True)
            t_nx = []
            for si in range(NSUB):
                to_m = []
                for mc in range(12):
                    for ng in range(2):
                        to_m.append(B.op("tensor", lambda e, mc=mc, ng=ng, si=si: e.matmul(
                            ps_o[:, 512 * ng:512 * (ng + 1)],
                            gyb[:, TILE * mc + 128 * si:TILE * mc + 128 * (si + 1)],
                            Wo[:, 1024 * mc + 512 * ng:1024 * mc + 512 * (ng + 1)],
                            start=(mc == 0), stop=(mc == 11)),
                            after=(t_gy + [P["o_free"]] if mc == 0 else []), inc="pe"))
                tnx = B.op("vector", lambda e, si=si: e.tensor_add(
                    xho[:, 1024 * si:1024 * (si + 1)], xho[:, 1024 * si:1024 * (si + 1)],
                    ps_o[:]), after=to_m + [P.get("dbg_xho")], inc="dv")
                P["o_free"] = tnx
                tso = B.dma("gpsimd", ap(S1, (r0 + 128 * si) * 1024, [[1024, 128], [1, 1024]]),
                            xho[:, 1024 * si:1024 * (si + 1)], after=[tnx], inc="st")
                t_nx.append(tso)
            P["xho_free"] = t_nx[-1]
            P["gy_free"] = to_m[-1]

    B.wait_all("sync", "st")
    B.wait_all("sync", "sto")
    B.flush()
    stk.close()


def ffn_attn(mk, W, l, tag, Sin, Sout, Hout, outer_ln):
    nc, B = mk.nc, mk.B
    TILE = 512
    NSUB = 4
    NT = SEQ // TILE
    NSEQ_L = NSEQ
    if "nt1" in mk.dbg:
        NT, NSEQ_L = 1, 1

    stk = contextlib.ExitStack()
    sb = lambda n, shp, dt: mk.sbuf(tag + n, shp, dt, stack=stk)
    idb = mk.identb

    wqkv = sb("_wqkv", (128, 8 * 1536), BF16)
    wo = sb("_wo", (128, 4 * 1024), BF16)
    csq = sb("_csq", (128, 2048), BF16)
    ssq = sb("_ssq", (128, 2048), BF16)
    csk = sb("_csk", (128, 2048), BF16)
    ssk = sb("_ssk", (128, 2048), BF16)
    mask_l = sb("_mskl", (128, 128), F32)
    mask_r = sb("_mskr", (128, 128), F32)

    sstk = contextlib.ExitStack()
    qs_row = mk.sbuf(tag + "_qsr", (1, 128), F32, stack=sstk)
    rep = mk.sbuf(tag + "_rep", (128, 128), F32, stack=sstk)
    col8 = mk.sbuf(tag + "_col8", (1, 128), F32, stack=sstk)
    tabf = mk.sbuf(tag + "_tabf", (128, 2048), F32, stack=sstk)
    ps_r = mk.psum(tag + "_psr", (128, 128), F32, sstk)

    for kc in range(8):
        B.dma("gpsimd", wqkv[:, 1536 * kc:1536 * (kc + 1)],
              ap(W["L0_ffn_wqkv"], l * 1024 * 1536 + kc * 128 * 1536,
                 [[1536, 128], [1, 1536]]), inc="wld", serial=True)
    for c in range(4):
        B.dma("gpsimd", wo[:, 1024 * c:1024 * (c + 1)],
              ap(W["L0_ffn_wo"], l * 512 * 1024 + c * 128 * 1024,
                 [[1024, 128], [1, 1024]]), inc="wld", serial=True)
    t_qs = B.dma("gpsimd", qs_row[0:1, 0:64],
                 ap(W["L0_ffn_q_scale"], l * 64, [[0, 1], [1, 64]]), inc="wld", serial=True)
    t_ks = B.dma("gpsimd", qs_row[0:1, 64:128],
                 ap(W["L0_ffn_k_scale"], l * 64, [[0, 1], [1, 64]]), inc="wld", serial=True)
    t_c8 = B.op("vector", lambda e: e.memset(col8[:], 8.0), inc="dv")
    t_r1 = B.op("tensor", lambda e: e.matmul(
        ps_r[:, 0:64], col8[0:1, :], qs_row[0:1, 0:64], start=True, stop=True),
        after=[t_qs, t_c8], inc="pe")
    t_r2 = B.op("tensor", lambda e: e.matmul(
        ps_r[:, 64:128], mk.ones[0:1, 0:128], qs_row[0:1, 64:128], start=True, stop=True),
        after=[t_ks], inc="pe")
    t_re = B.op("vector", lambda e: e.tensor_copy(rep[:], ps_r[:]),
                after=[t_r1, t_r2], inc="dv")

    tv_prev = None
    for name, tab, rcol, rot in (("csq", csq, 0, False), ("ssq", ssq, 0, True),
                                 ("csk", csk, 64, False), ("ssk", ssk, 64, True)):
        tl = B.dma("gpsimd", tabf[:], ap(W["tab_" + name], 0, [[2048, 128], [1, 2048]]),
                   after=[tv_prev], inc="wld", serial=True)
        if rot:
            rv = ap(rep, rcol + 32, [[128, 128], [0, 32], [-32, 2], [1, 32]])
        else:
            rv = ap(rep, rcol, [[128, 128], [0, 32], [1, 64]])
        tv_prev = B.op("vector", lambda e, tab=tab, rv=rv: e.tensor_tensor(
            tab[:].rearrange("p (j d) -> p j d", d=64),
            tabf[:].rearrange("p (j d) -> p j d", d=64), rv, ALU.mult),
            after=[tl, t_re], inc="dv")

    tm0 = B.op("gpsimd", lambda e: e.memset(mask_l[:], 0.0), inc="pl")
    B.op("gpsimd", lambda e: e.affine_select(
        out=mask_l[:], in_=mask_l[:], compare_op=ALU.is_ge, fill=-3e38, base=0,
        pattern=[[1, 128]], channel_multiplier=-1), after=[tm0], inc="pl")
    tm2 = B.op("gpsimd", lambda e: e.memset(mask_r[:], 0.0), inc="pl")
    B.op("gpsimd", lambda e: e.affine_select(
        out=mask_r[:], in_=mask_r[:], compare_op=ALU.is_ge, fill=-3e38, base=0,
        pattern=[[-1, 128]], channel_multiplier=1), after=[tm2], inc="pl")
    B.flush()
    sstk.close()

    xt = [sb(f"_xt{i}", (128, NSUB * 1024), F32) for i in range(2)]
    h0 = sb("_h0", (128, NSUB * 1024), F32)
    h0b = sb("_h0b", (128, NSUB * 1024), BF16)
    sh2b = sb("_sh2b", (128, NSUB * 512), BF16)
    carry_h = sb("_carh", (1, 512), BF16)
    carry_y = sb("_cary", (1, 512), F32)
    hs_fm = sb("_hsfm", (128, 8 * 512), BF16)
    qk_sb = sb("_qksb", (128, 8 * 512), BF16)
    v_tm = sb("_vtm", (128, 8 * 512), BF16)
    k_fm = sb("_kfm", (128, 4 * 1024), BF16)
    q_fm = sb("_qfm", (128, 4 * 512), BF16)
    qrot = sb("_qrot", (128, 1536), BF16)
    sq_scr = sb("_sqscr", (128, 512), F32)
    nrm = sb("_nrm", (128, 16), F32)
    nrmb = sb("_nrmb", (128, 8), BF16)
    sc_sb = sb("_scsb", (128, 2 * 640), BF16)
    an_sb = sb("_ansb", (128, 2 * 640), BF16)
    at_sb = sb("_atsb", (128, 2 * 640), BF16)
    mneg = sb("_mneg", (128, 4), F32)
    dens = sb("_dens", (128, 4), F32)
    o_sb = sb("_osb", (128, 4 * 512), BF16)
    y1 = sb("_y1", (128, NSUB * 1024), F32)
    sh2y = sb("_sh2y", (128, NSUB * 512), F32)
    stats = sb("_stats", (128, 24), F32)
    rr = sb("_rr", (128, 8), F32)

    ps_tr = mk.psum(tag + "_pstr", (128, 1024), BF16, stk)
    ps_A = mk.psum(tag + "_psA", (128, 512), F32, stk)
    ps_B = mk.psum(tag + "_psB", (128, 512), F32, stk)
    ps_E = mk.psum(tag + "_psE", (128, 128), F32, stk)
    ps_E2 = mk.psum(tag + "_psE2", (128, 128), F32, stk)
    ps_F = mk.psum(tag + "_psF", (128, 512), F32, stk)
    ps_G = mk.psum(tag + "_psG", (128, 1024), F32, stk)

    wtok = ("wld", B.sems["wld"][1])
    P = {
        "xt_free": [None, None], "h0_free": wtok, "h0b_free": wtok,
        "sh2b_free": wtok, "hs_free": wtok, "qk_free": wtok,
        "vslot": [wtok] * 8, "kslot": [wtok] * 8, "qfm_free": wtok,
        "qrot_free": wtok, "sq_free": wtok, "nrm_free": wtok, "nrmb_free": wtok,
        "sc_free": [wtok, wtok], "an_free": [wtok, wtok], "at_free": [wtok, wtok],
        "mn_free": wtok, "dn_free": wtok,
        "o_free": wtok, "y1_free": wtok, "sh2y_free": wtok, "h1b_free": wtok,
        "stats_free": wtok, "rr_free": wtok, "tr_free": wtok,
        "A_free": wtok, "B_free": wtok, "E_free": wtok, "F_free": wtok,
        "G_free": wtok, "carh_free": wtok, "cary_free": wtok,
        "krd": wtok, "vrd": wtok,
    }
    tvz = B.op("vector", lambda e: e.memset(v_tm[:], 0.0), inc="dv")
    P["vslot"] = [tvz] * 8

    LNE = NSUB * 1024

    for s in range(NSEQ_L):
        tch = B.op("vector", lambda e: e.memset(carry_h[:], 0.0),
                   after=[P["carh_free"]], inc="dv")
        tcy = B.op("vector", lambda e: e.memset(carry_y[:], 0.0),
                   after=[P["cary_free"]], inc="dv")
        P["carh_free"], P["cary_free"] = tch, tcy
        for it in range(NT):
            bi = it % 2
            x_t = xt[bi]
            r0 = s * SEQ + it * TILE
            t_ld = B.dma("sync", ap(x_t, 0, [[LNE, 128], [1024, NSUB], [1, 1024]]),
                         ap(Sin, r0 * 1024, [[1024, 128], [128 * 1024, NSUB], [1, 1024]]),
                         after=[P["xt_free"][bi]], inc=f"ld{bi}")

            t_h0 = []
            if outer_ln:
                tstats = []
                tpr = [t_ld, P["stats_free"]]
                for si in range(NSUB):
                    o = 1024 * si
                    b1 = B.op("vector", lambda e, o=o, x_t=x_t: e.bn_stats(
                        stats[:, 0:6], x_t[:, o:o + 512]), after=tpr, inc="dv")
                    b2 = B.op("vector", lambda e, o=o, x_t=x_t: e.bn_stats(
                        stats[:, 6:12], x_t[:, o + 512:o + 1024]), after=[b1], inc="dv")
                    tg = B.op("vector", lambda e, si=si: e.bn_aggr(
                        stats[:, 12 + 2 * si:14 + 2 * si], stats[:, 0:12]),
                        after=[b1, b2], inc="dv")
                    tstats.append(tg)
                    tpr = [t_ld, tg]
                tl1 = B.op("scalar", lambda e: e.activation(
                    out=rr[:, 0:NSUB], in_=ap(stats, 13, [[24, 128], [2, NSUB]]),
                    func=AF.Ln, bias=mk.cbias(1e-5), scale=1.0),
                    after=tstats + [P["rr_free"]], inc="ac")
                tr1 = B.op("scalar", lambda e: e.activation(
                    out=rr[:, 0:NSUB], in_=rr[:, 0:NSUB], func=AF.Exp, scale=-0.5),
                    after=[tl1], inc="ac")
                for si in range(NSUB):
                    o = 1024 * si
                    mu = ap(stats, 12 + 2 * si, [[24, 128], [1, 1]])
                    th1 = B.op("vector", lambda e, o=o, si=si, mu=mu, x_t=x_t: e.tensor_scalar(
                        h0[:, o:o + 1024], x_t[:, o:o + 1024], mu, rr[:, si:si + 1],
                        ALU.subtract, ALU.mult), after=[tr1, P["h0_free"]], inc="dv")
                    th2 = B.op("vector", lambda e, o=o: e.tensor_copy(
                        h0b[:, o:o + 1024], h0[:, o:o + 1024]),
                        after=[th1, P["h0b_free"]], inc="dv")
                    t_h0 += [th1, th2]
                P["stats_free"] = t_h0[-1]
                P["rr_free"] = t_h0[-1]
                P["xt_free"][bi] = t_h0[-1]
                h0_src = h0
            else:
                for si in range(NSUB):
                    o = 1024 * si
                    t_h0.append(B.op("vector", lambda e, o=o, x_t=x_t: e.tensor_copy(
                        h0b[:, o:o + 1024], x_t[:, o:o + 1024]),
                        after=[t_ld, P["h0b_free"]], inc="dv"))
                h0_src = x_t
            P["h0b_free"] = None

            if "fa1" in mk.dbg:
                continue
            shd = []
            shd.append(B.dma("gpsimd",
                       bass.AP(sh2b, NSUB * 512, [[NSUB * 512, 127], [512, NSUB], [1, 512]]),
                       bass.AP(h0b, 512, [[NSUB * 1024, 127], [1024, NSUB], [1, 512]]),
                       after=t_h0 + [P["sh2b_free"]], inc="sh0", serial=True))
            for si in range(1, NSUB):
                shd.append(B.dma("gpsimd",
                           bass.AP(sh2b, 512 * si, [[NSUB * 512, 1], [1, 512]]),
                           bass.AP(h0b, 127 * NSUB * 1024 + 1024 * (si - 1) + 512,
                                   [[NSUB * 1024, 1], [1, 512]]),
                           after=t_h0, inc="sh0", serial=True))
            shd.append(B.dma("gpsimd",
                       bass.AP(sh2b, 0, [[NSUB * 512, 1], [1, 512]]),
                       bass.AP(carry_h, 0, [[512, 1], [1, 512]]),
                       after=t_h0 + [P["carh_free"]], inc="sh0", serial=True))
            sh_tok = shd[-1]
            tch = B.dma("gpsimd", carry_h[:],
                        h0b[127:128, (NSUB - 1) * 1024 + 512:NSUB * 1024],
                        after=t_h0 + [sh_tok], inc="sh0", serial=True)
            P["carh_free"] = tch

            t_hs = []
            prev_tt = [P["tr_free"]]
            for c in range(8):
                tts = []
                for si in range(NSUB):
                    if c < 4:
                        src = h0b[:, 1024 * si + 128 * c:1024 * si + 128 * (c + 1)]
                        dep = t_h0
                    else:
                        src = sh2b[:, 512 * si + 128 * (c - 4):512 * si + 128 * (c - 3)]
                        dep = [sh_tok]
                    tts.append(B.op("tensor", lambda e, src=src, si=si: e.transpose(
                        ps_tr[:, 128 * si:128 * (si + 1)], src, idb[:]),
                        after=dep + prev_tt, inc="pe"))
                te = B.op("vector", lambda e, c=c: e.tensor_copy(
                    hs_fm[:, 512 * c:512 * (c + 1)], ps_tr[0:128, 0:512]),
                    after=tts + [P["hs_free"]], inc="dv")
                t_hs.append(te)
                prev_tt = [te]
            P["tr_free"] = t_hs[-1]
            P["h0b_free"] = t_hs[-1]
            P["sh2b_free"] = t_hs[-1]

            if "fa2" in mk.dbg:
                continue
            pAB = [ps_A, ps_B]
            gAB = [P["A_free"], P["B_free"]]
            t_q, t_k, t_v = [None] * NSUB, [None] * NSUB, [None] * NSUB
            nmm = 0
            for si in range(NSUB):
                for part in range(3):
                    pp = pAB[nmm % 2]
                    gd = gAB[nmm % 2]
                    tms = []
                    for kc in range(8):
                        tms.append(B.op("tensor",
                            lambda e, si=si, part=part, kc=kc, pp=pp: e.matmul(
                                pp[:], hs_fm[:, 512 * kc + 128 * si:512 * kc + 128 * (si + 1)],
                                wqkv[:, 1536 * kc + 512 * part:1536 * kc + 512 * (part + 1)],
                                start=(kc == 0), stop=(kc == 7)),
                            after=(t_hs + [gd] if kc == 0 else []), inc="pe"))
                    if part < 2:
                        te = B.op("scalar", lambda e, si=si, part=part, pp=pp: e.copy(
                            out=qk_sb[:, 2048 * part + 512 * si:2048 * part + 512 * (si + 1)],
                            in_=pp[:]), after=tms + [P["qk_free"]], inc="ac")
                        (t_q if part == 0 else t_k)[si] = te
                    else:
                        slot = ((r0 // 128) + si) % 8
                        te = B.op("scalar", lambda e, pp=pp, slot=slot: e.copy(
                            out=v_tm[:, 512 * slot:512 * (slot + 1)], in_=pp[:]),
                            after=tms + [P["vslot"][slot], P["vrd"]], inc="ac")
                        P["vslot"][slot] = te
                        t_v[si] = te
                    gAB[nmm % 2] = te
                    nmm += 1
            P["A_free"], P["B_free"] = gAB
            P["qk_free"] = t_k[-1]

            if "fa3" in mk.dbg:
                continue
            t_qready = [None] * NSUB
            t_kready = [None] * NSUB
            for part, (cse, sse) in ((0, (csq, ssq)), (1, (csk, ssk))):
                for si in range(NSUB):
                    base = 2048 * part + 512 * si
                    src = qk_sb[:, base:base + 512]
                    dep = [t_q[si] if part == 0 else t_k[si]]
                    tsq = B.op("scalar", lambda e, src=src: e.activation(
                        out=sq_scr[:], in_=src, func=AF.Square),
                        after=dep + [P["sq_free"]], inc="ac")
                    trd = B.op("vector", lambda e: e.reduce_sum(
                        nrm[:, 0:8], sq_scr[:].rearrange("p (h d) -> p h d", d=64),
                        axis=AX.X), after=[tsq, P["nrm_free"]], inc="dv")
                    tmx = B.op("vector", lambda e: e.tensor_scalar(
                        nrm[:, 8:16], nrm[:, 0:8], 1e-24, None, ALU.max),
                        after=[trd], inc="dv")
                    tln = B.op("scalar", lambda e: e.activation(
                        out=nrm[:, 8:16], in_=nrm[:, 8:16], func=AF.Ln),
                        after=[tmx], inc="ac")
                    tex = B.op("scalar", lambda e: e.activation(
                        out=nrmb[:], in_=nrm[:, 8:16], func=AF.Exp, scale=-0.5),
                        after=[tln, P["nrmb_free"]], inc="ac")
                    P["sq_free"] = trd
                    tqn = B.op("vector", lambda e, src=src: e.tensor_tensor(
                        qrot[:, 0:512].rearrange("p (h d) -> p h d", d=64),
                        src.rearrange("p (h d) -> p h d", d=64),
                        ap(nrmb, 0, [[8, 128], [1, 8], [0, 64]]), ALU.mult),
                        after=[tex, P["qrot_free"]], inc="dv")
                    P["nrm_free"] = tln
                    P["nrmb_free"] = tqn
                    jj = ((r0 // 128) + si) % 32
                    t1 = B.op("vector", lambda e, jj=jj, cse=cse: e.tensor_tensor(
                        qrot[:, 512:1024].rearrange("p (h d) -> p h d", d=64),
                        qrot[:, 0:512].rearrange("p (h d) -> p h d", d=64),
                        ap(cse, 64 * jj, [[2048, 128], [0, 8], [1, 64]]), ALU.mult),
                        after=[tqn], inc="dv")
                    t2 = B.op("vector", lambda e, jj=jj, sse=sse: e.tensor_tensor(
                        qrot[:, 1024:1536].rearrange("p (h s d) -> p h s d", d=32, s=2),
                        ap(qrot, 32, [[1536, 128], [64, 8], [-32, 2], [1, 32]]),
                        ap(sse, 64 * jj, [[2048, 128], [0, 8], [32, 2], [1, 32]]), ALU.mult),
                        after=[tqn, t1], inc="dv")
                    t3 = B.op("vector", lambda e: e.tensor_add(
                        qrot[:, 0:512], qrot[:, 512:1024], qrot[:, 1024:1536]),
                        after=[t1, t2], inc="dv")
                    P["qrot_free"] = t3
                    tts = []
                    for c in range(4):
                        tts.append(B.op("tensor", lambda e, c=c: e.transpose(
                            ps_tr[:, 128 * c:128 * (c + 1)],
                            qrot[:, 128 * c:128 * (c + 1)], idb[:]),
                            after=[t3, P["tr_free"]], inc="pe"))
                    if part == 0:
                        tev = B.op("vector", lambda e, si=si: e.tensor_copy(
                            ap(q_fm, 128 * si, [[4 * 512, 128], [512, 4], [1, 128]]),
                            ps_tr[0:128, 0:512].rearrange("p (c d) -> p c d", d=128)),
                            after=tts + [P["qfm_free"]], inc="dv")
                        P["qfm_free"] = tev
                        t_qready[si] = tev
                    else:
                        slot = ((r0 // 128) + si) % 8
                        col = (128 * slot) % 1024
                        tev = B.op("vector", lambda e, col=col: e.tensor_copy(
                            ap(k_fm, col, [[4 * 1024, 128], [1024, 4], [1, 128]]),
                            ps_tr[0:128, 0:512].rearrange("p (c d) -> p c d", d=128)),
                            after=tts + [P["kslot"][slot], P["krd"]], inc="dv")
                        P["kslot"][slot] = tev
                        t_kready[si] = tev
                    P["tr_free"] = tev
            qall = t_qready[-1]

            if "fa4" in mk.dbg:
                continue
            alv = 9
            for f in mk.dbg:
                if f.startswith("alv"):
                    alv = int(f[3:])
            t_o_ev = [None] * 4
            for hp in range(4):
                for si in range(NSUB):
                    g0 = r0 + 128 * si
                    t_sc = {0: [], 1: []}
                    t_mz = []
                    for par in range(2):
                        pmain = pAB[par]
                        pb0 = 64 * par
                        gmain = P["A_free"] if par == 0 else P["B_free"]
                        for jj in range(5):
                            kc_tok = g0 - 512 + 128 * jj
                            slot = ((kc_tok // 128) % 8 + 8) % 8
                            dstp, dcol = (pmain, 128 * jj) if jj < 4 else ((ps_E if par == 0 else ps_E2), 0)
                            if kc_tok < s * SEQ:
                                t_mz.append(B.op("vector",
                                    lambda e, dstp=dstp, dcol=dcol: e.memset(
                                        dstp[:, dcol:dcol + 128], -1e30),
                                    after=[gmain, P["E_free"]], inc="dv"))
                            else:
                                t_sc[par].append((jj, slot))
                    sc_toks = []
                    for par in range(2):
                        pmain = pAB[par]
                        pb0 = 64 * par
                        gmain = P["A_free"] if par == 0 else P["B_free"]
                        for (jj, slot) in t_sc[par]:
                            dstp, dcol = (pmain, 128 * jj) if jj < 4 else ((ps_E if par == 0 else ps_E2), 0)
                            deps = [qall, t_kready[-1], gmain, P["E_free"]] + t_mz
                            sc_toks.append(B.op("tensor",
                                lambda e, hp=hp, si=si, slot=slot, dstp=dstp, dcol=dcol, pb0=pb0: e.matmul(
                                    dstp[:, dcol:dcol + 128],
                                    q_fm[pb0:pb0 + 64, 512 * hp + 128 * si:512 * hp + 128 * (si + 1)],
                                    k_fm[pb0:pb0 + 64, 1024 * hp + 128 * slot:1024 * hp + 128 * (slot + 1)],
                                    start=True, stop=True),
                                after=deps, inc="pe"))
                    done = sc_toks + t_mz
                    if alv < 2:
                        P["A_free"] = done
                        P["B_free"] = done
                        P["E_free"] = done
                        continue
                    head_toks = []
                    for par in range(2):
                        pmain = pAB[par]
                        tml = B.op("vector", lambda e, pmain=pmain, par=par: e.tensor_add(
                            sc_sb[:, 640 * par:640 * par + 128], pmain[:, 0:128],
                            mask_l[:]), after=done + [P["sc_free"][par]], inc="dv")
                        pse = ps_E if par == 0 else ps_E2
                        tmr = B.op("vector", lambda e, par=par, pse=pse: e.tensor_add(
                            sc_sb[:, 640 * par + 512:640 * par + 640],
                            pse[:, 0:128], mask_r[:]),
                            after=done + [P["sc_free"][par]], inc="dv")
                        te1 = B.op("scalar", lambda e, pmain=pmain, par=par: e.copy(
                            out=sc_sb[:, 640 * par + 128:640 * par + 512],
                            in_=pmain[:, 128:512]),
                            after=done + [tml, tmr, P["sc_free"][par]], inc="ac")
                        if par == 0:
                            P["A_free"] = [te1, tml]
                        else:
                            P["B_free"] = [te1, tml]
                        P["E_free"] = tmr
                        if alv < 3:
                            head_toks.append(tmr)
                            continue
                        tmax = B.op("vector", lambda e, par=par: e.reduce_max(
                            mneg[:, par:par + 1], sc_sb[:, 640 * par:640 * (par + 1)],
                            axis=AX.X, negate=True),
                            after=[te1, tml, tmr, P["mn_free"]], inc="dv")
                        if alv < 4:
                            head_toks.append(tmax)
                            continue
                        texp = B.op("scalar", lambda e, par=par: e.activation(
                            out=an_sb[:, 640 * par:640 * (par + 1)],
                            in_=sc_sb[:, 640 * par:640 * (par + 1)], func=AF.Exp,
                            bias=mneg[:, par:par + 1],
                            accum_out=dens[:, par:par + 1]),
                            after=[tmax, P["an_free"][par], P["dn_free"]], inc="ac")
                        P["sc_free"][par] = texp
                        if alv < 5:
                            head_toks.append(texp)
                            P["mn_free"] = texp
                            continue
                        trc = B.op("vector", lambda e, par=par: e.reciprocal(
                            dens[:, 2 + par:3 + par], dens[:, par:par + 1]),
                            after=[texp], inc="dv")
                        tnr = B.op("vector", lambda e, par=par: e.tensor_scalar(
                            an_sb[:, 640 * par:640 * (par + 1)],
                            an_sb[:, 640 * par:640 * (par + 1)],
                            dens[:, 2 + par:3 + par], None, ALU.mult),
                            after=[trc], inc="dv")
                        head_toks.append(tnr)
                        P["mn_free"] = texp
                        P["dn_free"] = tnr
                    if alv < 6:
                        continue
                    for par in range(2):
                        tnr = head_toks[par]
                        tts = []
                        for jj in range(5):
                            tts.append(B.op("tensor", lambda e, par=par, jj=jj: e.transpose(
                                ps_tr[:, 128 * jj:128 * (jj + 1)],
                                an_sb[:, 640 * par + 128 * jj:640 * par + 128 * (jj + 1)],
                                idb[:]), after=[tnr, P["tr_free"]], inc="pe"))
                        tat = B.op("vector", lambda e, par=par: e.tensor_copy(
                            at_sb[:, 640 * par:640 * (par + 1)], ps_tr[0:128, 0:640]),
                            after=tts + [P["at_free"][par]], inc="dv")
                        P["tr_free"] = tat
                        P["an_free"][par] = tat
                        if alv < 7:
                            continue
                        h = 2 * hp + par
                        av = []
                        for jj in range(5):
                            kc_tok = g0 - 512 + 128 * jj
                            slot = ((kc_tok // 128) % 8 + 8) % 8
                            av.append(B.op("tensor",
                                lambda e, h=h, par=par, jj=jj, slot=slot, si=si: e.matmul(
                                    ps_F[64 * par:64 * par + 64, 128 * si:128 * (si + 1)],
                                    v_tm[:, 512 * slot + 64 * h:512 * slot + 64 * (h + 1)],
                                    at_sb[:, 640 * par + 128 * jj:640 * par + 128 * (jj + 1)],
                                    start=(jj == 0), stop=(jj == 4)),
                                after=[tat] + ([P["F_free"]] if jj == 0 else []), inc="pe"))
                        P["at_free"][par] = av[-1]
                        if si == NSUB - 1:
                            t_o_ev[hp] = av[-1]
                if alv < 8:
                    continue
                tev = B.op("vector", lambda e, hp=hp: e.tensor_copy(
                    o_sb[:, 512 * hp:512 * (hp + 1)], ps_F[:]),
                    after=[t_o_ev[hp], P["o_free"]], inc="dv")
                P["F_free"] = tev
                P["o_free"] = tev
                t_o_ev[hp] = tev

            if sc_toks:
                P["krd"] = sc_toks[-1]
            if alv >= 7:
                P["vrd"] = av[-1]

            if "fa5" in mk.dbg or alv < 9:
                continue
            t_y1 = []
            for si in range(NSUB):
                toms = []
                for hp in range(4):
                    for ng in range(2):
                        toms.append(B.op("tensor",
                            lambda e, hp=hp, ng=ng, si=si: e.matmul(
                                ps_G[:, 512 * ng:512 * (ng + 1)],
                                o_sb[:, 512 * hp + 128 * si:512 * hp + 128 * (si + 1)],
                                wo[:, 1024 * hp + 512 * ng:1024 * hp + 512 * (ng + 1)],
                                start=(hp == 0), stop=(hp == 3)),
                            after=(t_o_ev + [P["G_free"]] if hp == 0 else []), inc="pe"))
                ty = B.op("vector", lambda e, si=si, h0_src=h0_src: e.tensor_add(
                    y1[:, 1024 * si:1024 * (si + 1)], h0_src[:, 1024 * si:1024 * (si + 1)],
                    ps_G[:]), after=toms + [P["y1_free"]], inc="dv")
                P["G_free"] = ty
                t_y1.append(ty)
            P["h0_free"] = t_y1[-1]
            if not outer_ln:
                P["xt_free"][bi] = t_y1[-1]
            tso = B.dma("sync", ap(Sout, r0 * 1024, [[1024, 128], [128 * 1024, NSUB], [1, 1024]]),
                        ap(y1, 0, [[LNE, 128], [1024, NSUB], [1, 1024]]),
                        after=t_y1, inc="st")

            shy = []
            shy.append(B.dma("gpsimd",
                       bass.AP(sh2y, NSUB * 512, [[NSUB * 512, 127], [512, NSUB], [1, 512]]),
                       bass.AP(y1, 512, [[NSUB * 1024, 127], [1024, NSUB], [1, 512]]),
                       after=t_y1 + [P["sh2y_free"]], inc="sh1", serial=True))
            for si in range(1, NSUB):
                shy.append(B.dma("gpsimd",
                           bass.AP(sh2y, 512 * si, [[NSUB * 512, 1], [1, 512]]),
                           bass.AP(y1, 127 * NSUB * 1024 + 1024 * (si - 1) + 512,
                                   [[NSUB * 1024, 1], [1, 512]]),
                           after=t_y1, inc="sh1", serial=True))
            shy.append(B.dma("gpsimd",
                       bass.AP(sh2y, 0, [[NSUB * 512, 1], [1, 512]]),
                       bass.AP(carry_y, 0, [[512, 1], [1, 512]]),
                       after=t_y1 + [P["cary_free"]], inc="sh1", serial=True))
            shy_tok = shy[-1]
            tcy = B.dma("gpsimd", carry_y[:],
                        y1[127:128, (NSUB - 1) * 1024 + 512:NSUB * 1024],
                        after=t_y1 + [shy_tok], inc="sh1", serial=True)
            P["cary_free"] = tcy

            tstats2 = []
            tpr = [shy_tok, P["stats_free"]]
            for si in range(NSUB):
                b1 = B.op("vector", lambda e, si=si: e.bn_stats(
                    stats[:, 0:6], y1[:, 1024 * si:1024 * si + 512]),
                    after=tpr + t_y1, inc="dv")
                b2 = B.op("vector", lambda e, si=si: e.bn_stats(
                    stats[:, 6:12], sh2y[:, 512 * si:512 * (si + 1)]),
                    after=[b1, shy_tok], inc="dv")
                tg = B.op("vector", lambda e, si=si: e.bn_aggr(
                    stats[:, 12 + 2 * si:14 + 2 * si], stats[:, 0:12]),
                    after=[b1, b2], inc="dv")
                tstats2.append(tg)
                tpr = [tg]
            tl2 = B.op("scalar", lambda e: e.activation(
                out=rr[:, 4:4 + NSUB], in_=ap(stats, 13, [[24, 128], [2, NSUB]]),
                func=AF.Ln, bias=mk.cbias(1e-5), scale=1.0),
                after=tstats2 + [P["rr_free"]], inc="ac")
            tr2 = B.op("scalar", lambda e: e.activation(
                out=rr[:, 4:4 + NSUB], in_=rr[:, 4:4 + NSUB], func=AF.Exp, scale=-0.5),
                after=[tl2], inc="ac")
            t_h1 = []
            for si in range(NSUB):
                mu = ap(stats, 12 + 2 * si, [[24, 128], [1, 1]])
                ta = B.op("vector", lambda e, si=si, mu=mu: e.tensor_scalar(
                    h0b[:, 1024 * si:1024 * si + 512], y1[:, 1024 * si:1024 * si + 512],
                    mu, rr[:, 4 + si:5 + si], ALU.subtract, ALU.mult),
                    after=[tr2, P["h1b_free"], P["h0b_free"], P["carh_free"]], inc="dv")
                tb = B.op("vector", lambda e, si=si, mu=mu: e.tensor_scalar(
                    h0b[:, 1024 * si + 512:1024 * (si + 1)], sh2y[:, 512 * si:512 * (si + 1)],
                    mu, rr[:, 4 + si:5 + si], ALU.subtract, ALU.mult),
                    after=[tr2], inc="dv")
                t_h1 += [ta, tb]
            P["stats_free"] = t_h1[-1]
            P["rr_free"] = t_h1[-1]
            P["sh2y_free"] = t_h1[-1]
            P["y1_free"] = (t_h1[-1], tso)[0]
            tso2 = B.dma("sync", ap(Hout, r0 * 1024, [[1024, 128], [128 * 1024, NSUB], [1, 1024]]),
                         ap(h0b, 0, [[LNE, 128], [1024, NSUB], [1, 1024]]),
                         after=t_h1, inc="sto")
            P["h1b_free"] = tso2
            P["h0b_free"] = tso2
            P["y1_free"] = tso

    B.wait_all("sync", "st")
    B.wait_all("sync", "sto")
    B.wait_all("gpsimd", "sh0")
    B.wait_all("gpsimd", "sh1")
    B.flush()
    stk.close()


def ffn_mlp(mk, W, l, tag, Sin, Hin, Sout, extra):
    nc, B = mk.nc, mk.B
    TILE = 256
    NSUB = 2
    NT = SEQ // TILE
    NSEQ_L = NSEQ
    if "nt1" in mk.dbg:
        NT, NSEQ_L = 1, 1
    HID = 2730
    NCH = 22

    stk = contextlib.ExitStack()
    sb = lambda n, shp, dt: mk.sbuf(tag + n, shp, dt, stack=stk)
    idb, ones = mk.identb, mk.ones

    w1p = sb("_w1p", (128, 8 * 5460), BF16)
    w2a = sb("_w2a", (128, NCH * 1024), BF16)
    b1c = sb("_b1c", (128, 2 * NCH), F32)

    sstk = contextlib.ExitStack()
    wtmp = mk.sbuf(tag + "_wtmp", (128, 5460), F32, stack=sstk)
    lg_r = mk.sbuf(tag + "_lgr", (1, 1024), F32, stack=sstk)
    lb_r = mk.sbuf(tag + "_lbr", (1, 1024), F32, stack=sstk)
    lg_c = mk.sbuf(tag + "_lgc", (128, 8), F32, stack=sstk)
    lb_c = mk.sbuf(tag + "_lbc", (128, 8), F32, stack=sstk)
    b1row = mk.sbuf(tag + "_b1row", (1, 5504), F32, stack=sstk)
    ps_a = mk.psum(tag + "_ps_a", (128, 512), F32, sstk)
    ps_b = mk.psum(tag + "_ps_b", (128, 512), F32, sstk)
    ps_c = mk.psum(tag + "_ps_c", (1, 512), F32, sstk)

    idf = mk.identf
    tlg = B.dma("gpsimd", lg_r[:], ap(W["L0_ffn_ln_g"], l * 1024, [[0, 1], [1, 1024]]),
                inc="wld", serial=True)
    tlb = B.dma("gpsimd", lb_r[:], ap(W["L0_ffn_ln_b"], l * 1024, [[0, 1], [1, 1024]]),
                inc="wld", serial=True)
    ttr = []
    for c in range(8):
        ttr.append(B.op("tensor", lambda e, c=c: e.transpose(
            ps_a[:, c:c + 1], lg_r[0:1, 128 * c:128 * (c + 1)], idf[0:1, 0:1]),
            after=[tlg], inc="pe"))
        ttr.append(B.op("tensor", lambda e, c=c: e.transpose(
            ps_b[:, c:c + 1], lb_r[0:1, 128 * c:128 * (c + 1)], idf[0:1, 0:1]),
            after=[tlb], inc="pe"))
    tw = B.op("vector", lambda e: e.tensor_copy(lg_c[:], ps_a[:, 0:8]), after=ttr, inc="dv")
    tb = B.op("vector", lambda e: e.tensor_copy(lb_c[:], ps_b[:, 0:8]), after=ttr, inc="dv")
    tb1 = B.dma("gpsimd", b1row[0:1, 0:5460],
                ap(W["L0_ffn_b1"], l * 5460, [[0, 1], [1, 5460]]), inc="wld", serial=True)
    prev_k = [tw, tb, tb1]
    for kc in range(8):
        tl = B.dma("gpsimd", wtmp[:],
                   ap(W["L0_ffn_w1"], l * 1024 * 5460 + kc * 128 * 5460,
                      [[5460, 128], [1, 5460]]), after=prev_k, inc="wld", serial=True)
        tv = B.op("vector", lambda e, kc=kc: e.tensor_scalar(
            w1p[:, 5460 * kc:5460 * (kc + 1)], wtmp[:], lg_c[:, kc:kc + 1], None,
            ALU.mult), after=[tl, tw], inc="dv")
        cur = [tv]
        for ng in range(11):
            w = min(512, 5460 - 512 * ng)
            tm = B.op("tensor", lambda e, kc=kc, ng=ng, w=w: e.matmul(
                ps_c[0:1, 0:w], lb_c[:, kc:kc + 1],
                wtmp[:, 512 * ng:512 * ng + w], start=True, stop=True),
                after=[tl, tb] + (cur[-1:] if ng > 0 else []), inc="pe")
            ta = B.op("vector", lambda e, ng=ng, w=w: e.tensor_add(
                b1row[0:1, 512 * ng:512 * ng + w], b1row[0:1, 512 * ng:512 * ng + w],
                ps_c[0:1, 0:w]), after=[tm], inc="dv")
            cur.append(ta)
        prev_k = cur
    tcol = []
    for h in range(2):
        for mc in range(NCH):
            off = 2730 * h + 128 * mc
            w = min(128, 2730 - 128 * mc)
            tt = B.op("tensor", lambda e, off=off, w=w: e.transpose(
                ps_a[0:w, 20:21], b1row[0:1, off:off + w], idf[0:1, 0:1]),
                after=prev_k + tcol[-1:], inc="pe")
            tv = B.op("vector", lambda e, h=h, mc=mc, w=w: e.tensor_copy(
                b1c[0:w, NCH * h + mc:NCH * h + mc + 1], ps_a[0:w, 20:21]),
                after=[tt], inc="dv")
            tcol.append(tv)
    tz = B.op("vector", lambda e: e.memset(w2a[:, (NCH - 1) * 1024:NCH * 1024], 0.0), inc="dv")
    for kc in range(NCH):
        rows = min(128, HID - 128 * kc)
        B.dma("gpsimd", w2a[0:rows, 1024 * kc:1024 * (kc + 1)],
              ap(W["L0_ffn_w2"], l * HID * 1024 + kc * 128 * 1024,
                 [[1024, rows], [1, 1024]]),
              after=[tz] if kc == NCH - 1 else [], inc="wld", serial=True)
    B.dma("gpsimd", w2a[42:43, (NCH - 1) * 1024:NCH * 1024],
          ap(W["L0_ffn_b2"], l * 1024, [[0, 1], [1, 1024]]),
          after=[tz], inc="wld", serial=True)
    B.flush()
    sstk.close()

    hb = [sb(f"_hb{i}", (128, NSUB * 1024), BF16) for i in range(2)]
    yb = [sb(f"_yb{i}", (128, NSUB * 1024), F32) for i in range(2)]
    xb = [sb(f"_xb{i}", (128, NSUB * 1024), F32) for i in range(2)] if extra is not None else None
    hfm = sb("_hfm", (128, 8 * TILE), BF16)
    gel = sb("_gel", (128, TILE), F32)
    agb = sb("_agb", (128, NCH * TILE), BF16)
    y2 = sb("_y2", (128, NSUB * 1024), F32)

    ps_tr = mk.psum(tag + "_pstr", (128, TILE), BF16, stk)
    ps_g0 = mk.psum(tag + "_psg0", (128, TILE), F32, stk)
    ps_g1 = mk.psum(tag + "_psg1", (128, TILE), F32, stk)
    ps_a0 = mk.psum(tag + "_psa0", (128, TILE), F32, stk)
    ps_a1 = mk.psum(tag + "_psa1", (128, TILE), F32, stk)
    ps_o = mk.psum(tag + "_pso", (128, 1024), F32, stk)
    pg = [ps_g0, ps_g1]
    pa = [ps_a0, ps_a1]

    wtok = ("wld", B.sems["wld"][1])
    P = {
        "h_free": [None, None], "y_free": [None, None], "x_free": [None, None],
        "hfm_free": wtok, "tr_free": wtok, "gel_free": wtok,
        "pg_free": [wtok, wtok], "pa_free": [wtok, wtok],
        "ag_free": wtok, "o_free": wtok, "y2_free": wtok,
    }
    tz1 = B.op("vector", lambda e: e.memset(
        agb[:, (NCH - 1) * TILE:NCH * TILE], 0.0), inc="dv")
    tz2 = B.dma("gpsimd", agb[42:43, (NCH - 1) * TILE:NCH * TILE],
                mk.ones[0:1, 0:TILE], after=[tz1], inc="wld", serial=True)
    P["ag_free"] = tz2

    LNE = NSUB * 1024

    for s in range(NSEQ_L):
        for it in range(NT):
            bi = it % 2
            h_t, y_t = hb[bi], yb[bi]
            r0 = s * SEQ + it * TILE
            t_lh = B.dma("sync", ap(h_t, 0, [[LNE, 128], [1024, NSUB], [1, 1024]]),
                         ap(Hin, r0 * 1024, [[1024, 128], [128 * 1024, NSUB], [1, 1024]]),
                         after=[P["h_free"][bi]], inc=f"ld{bi}")
            t_ly = B.dma("sync", ap(y_t, 0, [[LNE, 128], [1024, NSUB], [1, 1024]]),
                         ap(Sin, r0 * 1024, [[1024, 128], [128 * 1024, NSUB], [1, 1024]]),
                         after=[P["y_free"][bi]], inc=f"ld{bi}")
            if extra is not None:
                x_t = xb[bi]
                t_lx = B.dma("sync", ap(x_t, 0, [[LNE, 128], [1024, NSUB], [1, 1024]]),
                             ap(extra, r0 * 1024, [[1024, 128], [128 * 1024, NSUB], [1, 1024]]),
                             after=[P["x_free"][bi]], inc=f"ld{bi}")
            ld_tok = (f"ld{bi}", B.sems[f"ld{bi}"][1])

            t_hs = []
            prev_tt = [P["tr_free"]]
            for c in range(8):
                tts = []
                for si in range(NSUB):
                    tts.append(B.op("tensor", lambda e, c=c, si=si, h_t=h_t: e.transpose(
                        ps_tr[:, 128 * si:128 * (si + 1)],
                        h_t[:, 1024 * si + 128 * c:1024 * si + 128 * (c + 1)], idb[:]),
                        after=[ld_tok] + prev_tt, inc="pe"))
                te = B.op("vector", lambda e, c=c: e.tensor_copy(
                    hfm[:, TILE * c:TILE * (c + 1)], ps_tr[0:128, 0:TILE]),
                    after=tts + [P["hfm_free"]], inc="dv")
                t_hs.append(te)
                prev_tt = [te]
            P["tr_free"] = t_hs[-1]
            P["h_free"][bi] = t_hs[-1]

            t_ag = []
            for mc in range(NCH):
                wdt = min(128, HID - 128 * mc)
                ppg, ppa = pg[mc % 2], pa[mc % 2]
                gg, ga = P["pg_free"][mc % 2], P["pa_free"][mc % 2]
                tg_m = []
                ta_m = []
                for kc in range(8):
                    tg_m.append(B.op("tensor",
                        lambda e, mc=mc, kc=kc, ppg=ppg, wdt=wdt: e.matmul(
                            ppg[0:wdt, :],
                            w1p[:, 5460 * kc + 2730 + 128 * mc:5460 * kc + 2730 + 128 * mc + wdt],
                            hfm[:, TILE * kc:TILE * (kc + 1)],
                            start=(kc == 0), stop=(kc == 7)),
                        after=(t_hs + [gg] if kc == 0 else []), inc="pe"))
                for kc in range(8):
                    ta_m.append(B.op("tensor",
                        lambda e, mc=mc, kc=kc, ppa=ppa, wdt=wdt: e.matmul(
                            ppa[0:wdt, :],
                            w1p[:, 5460 * kc + 128 * mc:5460 * kc + 128 * mc + wdt],
                            hfm[:, TILE * kc:TILE * (kc + 1)],
                            start=(kc == 0), stop=(kc == 7)),
                        after=(t_hs + [ga] if kc == 0 else []), inc="pe"))
                tge = B.op("scalar", lambda e, mc=mc, ppg=ppg, wdt=wdt: e.activation(
                    out=gel[0:wdt, :], in_=ppg[0:wdt, :], func=AF.Gelu,
                    bias=b1c[0:wdt, NCH + mc:NCH + mc + 1]),
                    after=tg_m + [P["gel_free"]], inc="ac")
                P["pg_free"][mc % 2] = tge
                tag = B.op("vector", lambda e, mc=mc, ppa=ppa, wdt=wdt: e.scalar_tensor_tensor(
                    agb[0:wdt, TILE * mc:TILE * (mc + 1)], ppa[0:wdt, :],
                    b1c[0:wdt, mc:mc + 1], gel[0:wdt, :], ALU.add, ALU.mult),
                    after=ta_m + [tge, P["ag_free"]], inc="dv")
                P["pa_free"][mc % 2] = tag
                P["gel_free"] = tag
                t_ag.append(tag)
            P["hfm_free"] = ta_m[-1]

            t_y2 = []
            for si in range(NSUB):
                toms = []
                for kc in range(NCH):
                    rows = min(128, HID - 128 * kc)
                    if kc == NCH - 1:
                        rows = 43
                    for ng in range(2):
                        toms.append(B.op("tensor",
                            lambda e, kc=kc, ng=ng, si=si, rows=rows: e.matmul(
                                ps_o[:, 512 * ng:512 * (ng + 1)],
                                agb[0:rows, TILE * kc + 128 * si:TILE * kc + 128 * (si + 1)],
                                w2a[0:rows, 1024 * kc + 512 * ng:1024 * kc + 512 * (ng + 1)],
                                start=(kc == 0), stop=(kc == NCH - 1)),
                            after=(t_ag + [P["o_free"]] if kc == 0 else []), inc="pe"))
                ty = B.op("vector", lambda e, si=si, y_t=y_t: e.tensor_add(
                    y2[:, 1024 * si:1024 * (si + 1)], y_t[:, 1024 * si:1024 * (si + 1)],
                    ps_o[:]), after=toms + [ld_tok, P["y2_free"]], inc="dv")
                if extra is not None:
                    ty = B.op("vector", lambda e, si=si, x_t=x_t: e.tensor_add(
                        y2[:, 1024 * si:1024 * (si + 1)], y2[:, 1024 * si:1024 * (si + 1)],
                        x_t[:, 1024 * si:1024 * (si + 1)]), after=[ty], inc="dv")
                P["o_free"] = ty
                t_y2.append(ty)
            P["y_free"][bi] = t_y2[-1]
            if extra is not None:
                P["x_free"][bi] = t_y2[-1]
            P["ag_free"] = toms[-1]
            tso = B.dma("sync", ap(Sout, r0 * 1024, [[1024, 128], [128 * 1024, NSUB], [1, 1024]]),
                        ap(y2, 0, [[LNE, 128], [1024, NSUB], [1, 1024]]),
                        after=t_y2, inc="st")
            P["y2_free"] = tso

    B.wait_all("sync", "st")
    B.flush()
    stk.close()


def attn_b_setup(mk, W, tag, stk, need_wqkv):
    nc, B = mk.nc, mk.B
    idf = mk.identf
    out = {}
    if need_wqkv:
        out["wqkvT"] = mk.sbuf(tag + "_wqkvT", (128, 8 * 3072), BF16, stack=stk)
    out["wbd"] = mk.sbuf(tag + "_wbd", (128, 8 * 512), BF16, stack=stk)
    sstk = contextlib.ExitStack()
    wtmp = mk.sbuf(tag + "_wtmp", (128, 1024), F32, stack=sstk)
    hsb = mk.sbuf(tag + "_hsb", (64, 64), F32, stack=sstk)
    ptb = mk.sbuf(tag + "_ptb", (128, 128), BF16, stack=sstk)
    sgc = mk.sbuf(tag + "_sgc", (64, 192), F32, stack=sstk)
    w0 = mk.sbuf(tag + "_w0", (64, 64), F32, stack=sstk)
    l1 = mk.sbuf(tag + "_l1", (64, 64), F32, stack=sstk)
    wsc = mk.sbuf(tag + "_wscs", (64, 64), F32, stack=sstk)
    grow = mk.sbuf(tag + "_grow", (1, 512), F32, stack=sstk)
    gcol = mk.sbuf(tag + "_gcol", (1, 128), F32, stack=sstk)
    wbf = mk.sbuf(tag + "_wbf", (128, 512), F32, stack=sstk)
    wpre = mk.sbuf(tag + "_wpre", (128, 512), BF16, stack=sstk)
    ps_a = mk.psum(tag + "_psa", (128, 512), F32, sstk)
    ps_b = mk.psum(tag + "_psb", (128, 512), F32, sstk)
    ps_c = mk.psum(tag + "_psc", (128, 512), F32, sstk)

    if need_wqkv:
        wq = out["wqkvT"]
        prev = [None]
        for r in range(24):
            tl = B.dma("gpsimd", wtmp[:],
                       ap(W["L0_attn_b_w_qkv"], r * 128 * 1024, [[1024, 128], [1, 1024]]),
                       after=prev, inc="wld", serial=True)
            prev = []
            for c in range(8):
                tt = B.op("tensor", lambda e, c=c: e.transpose(
                    ps_a[:, 128 * (c % 4):128 * (c % 4) + 128],
                    wtmp[:, 128 * c:128 * (c + 1)], idf[:]),
                    after=[tl] + prev[-1:], inc="pe")
                if r < 16:
                    dst = ap(wq, 3072 * c + 128 * r,
                             [[8 * 3072, 128], [64, 2], [32, 2], [1, 32]])
                    src = ap(ps_a, 128 * (c % 4), [[512, 128], [64, 2], [1, 2], [2, 32]])
                    tv = B.op("vector", lambda e, dst=dst, src=src: e.tensor_copy(dst, src),
                              after=[tt], inc="dv")
                else:
                    tv = B.op("vector", lambda e, c=c, r=r: e.tensor_copy(
                        wq[:, 3072 * c + 128 * r:3072 * c + 128 * (r + 1)],
                        ps_a[:, 128 * (c % 4):128 * (c % 4) + 128]), after=[tt], inc="dv")
                prev.append(tv)

    wbd = out["wbd"]
    th = B.dma("gpsimd", hsb[:], ap(W["had"], 0, [[64, 64], [1, 64]]),
               inc="wld", serial=True)
    tpt = B.dma("gpsimd", ptb[:], ap(W["permT"], 0, [[128, 128], [1, 128]]),
                inc="wld", serial=True)
    tgc = B.op("vector", lambda e: e.memset(gcol[:], 1.0 / 4096.0), inc="dv")
    sstk2 = contextlib.ExitStack()
    sgn2 = mk.sbuf(tag + "_sgn2", (128, 64), F32, stack=sstk2)
    ts1b = B.dma("gpsimd", sgn2[:],
                 ap(W["L0_attn_b_sorf_signs"], 0, [[64, 128], [1, 64]]),
                 inc="wld", serial=True)
    last_wq = prev if need_wqkv else [None]
    tt1 = B.op("tensor", lambda e: e.transpose(ps_a[0:64, 0:128], sgn2[:], idf[:]),
               after=[ts1b] + last_wq, inc="pe")
    tv1 = B.op("vector", lambda e: e.tensor_copy(sgc[0:64, 0:128], ps_a[0:64, 0:128]),
               after=[tt1], inc="dv")
    ts2b = B.dma("gpsimd", sgn2[0:64, :],
                 ap(W["L0_attn_b_sorf_signs"], 128 * 64, [[64, 64], [1, 64]]),
                 after=[tv1], inc="wld", serial=True)
    tt2 = B.op("tensor", lambda e: e.transpose(ps_a[0:64, 128:192], sgn2[0:64, :],
               idf[0:64, 0:64]), after=[ts2b, tv1], inc="pe")
    tv2 = B.op("vector", lambda e: e.tensor_copy(sgc[0:64, 128:192], ps_a[0:64, 128:192]),
               after=[tt2], inc="dv")
    sg_ready = [tv1, tv2]

    prev_chunk = [None]
    for c in range(8):
        tzw = B.op("vector", lambda e: e.memset(wbf[:], 0.0), after=prev_chunk, inc="dv")
        tgl = B.dma("gpsimd", grow[:],
                    ap(W["L0_attn_b_sorf_scales"], 512 * c, [[0, 1], [1, 512]]),
                    after=prev_chunk, inc="wld", serial=True)
        ev_toks = []
        for par in range(2):
            h = 2 * c + par
            for r in range(4):
                col = 12 * h + 3 * r
                t0 = B.op("vector", lambda e, col=col: e.tensor_scalar(
                    w0[:], hsb[:], sgc[0:64, col:col + 1], None, ALU.mult),
                    after=sg_ready + [th] + ev_toks[-1:], inc="dv")
                tT = B.op("tensor", lambda e: e.transpose(
                    ps_b[0:64, 0:64], w0[:], idf[0:64, 0:64]), after=[t0], inc="pe")
                tl1 = B.op("vector", lambda e, col=col: e.tensor_scalar(
                    l1[:], ps_b[0:64, 0:64], sgc[0:64, col + 1:col + 2], None, ALU.mult),
                    after=[tT], inc="dv")
                tm1 = B.op("tensor", lambda e: e.matmul(
                    ps_b[0:64, 64:128], l1[:], hsb[:], start=True, stop=True),
                    after=[tl1], inc="pe")
                te1 = B.op("vector", lambda e: e.tensor_copy(wsc[:], ps_b[0:64, 64:128]),
                           after=[tm1], inc="dv")
                tT2 = B.op("tensor", lambda e: e.transpose(
                    ps_b[0:64, 128:192], wsc[:], idf[0:64, 0:64]), after=[te1], inc="pe")
                tl2 = B.op("vector", lambda e, col=col: e.tensor_scalar(
                    l1[:], ps_b[0:64, 128:192], sgc[0:64, col + 2:col + 3], None, ALU.mult),
                    after=[tT2], inc="dv")
                tm2 = B.op("tensor", lambda e: e.matmul(
                    ps_b[0:64, 192:256], l1[:], hsb[:], start=True, stop=True),
                    after=[tl2], inc="pe")
                tev = B.op("vector", lambda e, par=par, r=r: e.tensor_copy(
                    wbf[64 * par:64 * par + 64, 256 * par + 64 * r:256 * par + 64 * (r + 1)],
                    ps_b[0:64, 192:256]), after=[tm2, tzw], inc="dv")
                ev_toks.append(tev)
        tg = B.op("tensor", lambda e: e.matmul(
            ps_a[:, 0:512], gcol[0:1, :], grow[0:1, :], start=True, stop=True),
            after=[tgl, tgc] + ev_toks, inc="pe")
        tmu = B.op("vector", lambda e: e.tensor_mul(wpre[:], wbf[:], ps_a[:, 0:512]),
                   after=[tg] + ev_toks, inc="dv")
        tpm = B.op("tensor", lambda e: e.matmul(
            ps_c[:, 0:512], ptb[:], wpre[:], start=True, stop=True),
            after=[tmu, tpt], inc="pe")
        tevc = B.op("vector", lambda e, c=c: e.tensor_copy(
            wbd[:, 512 * c:512 * (c + 1)], ps_c[:, 0:512]), after=[tpm], inc="dv")
        prev_chunk = [tevc]
    B.flush()
    sstk2.close()
    sstk.close()
    return out


def attn_b_passA(mk, W, Sin, QROT, BQ, SKV):
    nc, B = mk.nc, mk.B
    tag = "ba"
    TILE = 512
    NSUB = 4
    NT = SEQ // TILE
    NSEQ_L = NSEQ
    if "nt1" in mk.dbg:
        NT, NSEQ_L = 1, 1

    stk = contextlib.ExitStack()
    sb = lambda n, shp, dt: mk.sbuf(tag + n, shp, dt, stack=stk)
    idb = mk.identb

    su = attn_b_setup(mk, W, tag, stk, need_wqkv=True)
    wq, wbd = su["wqkvT"], su["wbd"]
    csb = sb("_csb", (128, 2048), BF16)
    ssb = sb("_ssb", (128, 2048), BF16)
    t1c = B.dma("gpsimd", csb[:], ap(W["tab_csb"], 0, [[2048, 128], [1, 2048]]),
                inc="wld", serial=True)
    t2c = B.dma("gpsimd", ssb[:], ap(W["tab_ssb"], 0, [[2048, 128], [1, 2048]]),
                inc="wld", serial=True)
    B.flush()

    xt = [sb(f"_xt{i}", (128, NSUB * 1024), F32) for i in range(2)]
    h0b = sb("_h0b", (128, NSUB * 1024), BF16)
    h_fm = sb("_hfm", (128, 8 * 512), BF16)
    qk_sb = sb("_qksb", (128, 8 * 512), BF16)
    v65 = sb("_v65", (128, 1040), BF16)
    qrot = sb("_qrot", (128, 3 * 1024), BF16)
    krot_s = sb("_krot", (128, 1024), BF16)
    k_fm = sb("_kfm", (128, 8 * 512), BF16)
    kphi_s = sb("_kphi", (128, 512), BF16)
    sq_scr = sb("_sqscr", (128, 1024), F32)
    nq = sb("_nq", (128, 48), F32)
    stats = sb("_stats", (128, 24), F32)
    rr = sb("_rr", (128, 4), F32)
    skv_sb = sb("_skv", (128, 2080), F32)

    ps_tr = mk.psum(tag + "_pstr", (128, 1024), BF16, stk)
    ps_A = mk.psum(tag + "_psA", (128, 512), F32, stk)
    ps_B = mk.psum(tag + "_psB", (128, 512), F32, stk)
    ps_S = [mk.psum(tag + f"_psS{i}", (128, 455), F32, stk) for i in range(5)]

    def s_slice(h, mc):
        a = 2 * h + mc
        return ps_S[a // 7][:, 65 * (a % 7):65 * (a % 7) + 65]

    wtok = ("wld", B.sems["wld"][1])
    P = {
        "xt_free": [None, None], "h0b_free": wtok, "hfm_free": wtok,
        "qk_free": [wtok, wtok], "v_free": wtok, "qrot_free": wtok, "krot_free": wtok,
        "kfm_free": wtok, "kphi_free": wtok, "sq_free": wtok, "nq_free": wtok,
        "stats_free": wtok, "rr_free": wtok, "tr_free": wtok,
        "A_free": wtok, "B_free": wtok, "S_rd": wtok, "skv_free": wtok,
        "bq_dma": None,
    }
    tz = B.op("vector", lambda e: e.memset(v65[:], 0.0), inc="dv")
    P["v_free"] = tz

    LNE = NSUB * 1024
    for s in range(NSEQ_L):
        tsz = []
        for i in range(5):
            tsz.append(B.op("vector", lambda e, i=i: e.memset(ps_S[i][:], 0.0),
                            after=[P["S_rd"]], inc="dv"))
        s_started = False
        for it in range(NT):
            bi = it % 2
            x_t = xt[bi]
            r0 = s * SEQ + it * TILE
            t_ld = B.dma("sync", ap(x_t, 0, [[LNE, 128], [1024, NSUB], [1, 1024]]),
                         ap(Sin, r0 * 1024, [[1024, 128], [128 * 1024, NSUB], [1, 1024]]),
                         after=[P["xt_free"][bi]], inc=f"ld{bi}")
            tstats = []
            tpr = [t_ld, P["stats_free"]]
            for si in range(NSUB):
                o = 1024 * si
                b1 = B.op("vector", lambda e, o=o, x_t=x_t: e.bn_stats(
                    stats[:, 0:6], x_t[:, o:o + 512]), after=tpr, inc="dv")
                b2 = B.op("vector", lambda e, o=o, x_t=x_t: e.bn_stats(
                    stats[:, 6:12], x_t[:, o + 512:o + 1024]), after=[b1], inc="dv")
                tg = B.op("vector", lambda e, si=si: e.bn_aggr(
                    stats[:, 12 + 2 * si:14 + 2 * si], stats[:, 0:12]),
                    after=[b1, b2], inc="dv")
                tstats.append(tg)
                tpr = [t_ld, tg]
            tl1 = B.op("scalar", lambda e: e.activation(
                out=rr[:, 0:NSUB], in_=ap(stats, 13, [[24, 128], [2, NSUB]]),
                func=AF.Ln, bias=mk.cbias(1e-5), scale=1.0),
                after=tstats + [P["rr_free"]], inc="ac")
            tr1 = B.op("scalar", lambda e: e.activation(
                out=rr[:, 0:NSUB], in_=rr[:, 0:NSUB], func=AF.Exp, scale=-0.5),
                after=[tl1], inc="ac")
            t_h0 = []
            for si in range(NSUB):
                o = 1024 * si
                mu = ap(stats, 12 + 2 * si, [[24, 128], [1, 1]])
                t_h0.append(B.op("vector", lambda e, o=o, si=si, mu=mu, x_t=x_t: e.tensor_scalar(
                    h0b[:, o:o + 1024], x_t[:, o:o + 1024], mu, rr[:, si:si + 1],
                    ALU.subtract, ALU.mult), after=[tr1, P["h0b_free"]], inc="dv"))
            P["stats_free"] = t_h0[-1]
            P["rr_free"] = t_h0[-1]
            P["xt_free"][bi] = t_h0[-1]

            t_hs = []
            prev_tt = [P["tr_free"]]
            for c in range(8):
                tts = []
                for si in range(NSUB):
                    tts.append(B.op("tensor", lambda e, c=c, si=si: e.transpose(
                        ps_tr[:, 128 * si:128 * (si + 1)],
                        h0b[:, 1024 * si + 128 * c:1024 * si + 128 * (c + 1)], idb[:]),
                        after=t_h0 + prev_tt, inc="pe"))
                te = B.op("vector", lambda e, c=c: e.tensor_copy(
                    h_fm[:, 512 * c:512 * (c + 1)], ps_tr[0:128, 0:512]),
                    after=tts + [P["hfm_free"]], inc="dv")
                t_hs.append(te)
                prev_tt = [te]
            P["tr_free"] = t_hs[-1]
            P["h0b_free"] = t_hs[-1]

            pAB = [ps_A, ps_B]
            gAB = [P["A_free"], P["B_free"]]
            t_qk = [None] * NSUB
            t_v = [None] * NSUB
            nmm = 0
            for si in range(NSUB):
                for grp in range(6):
                    pp = pAB[nmm % 2]
                    gd = gAB[nmm % 2]
                    tms = []
                    for kc in range(8):
                        tms.append(B.op("tensor",
                            lambda e, si=si, grp=grp, kc=kc, pp=pp: e.matmul(
                                pp[:], h_fm[:, 512 * kc + 128 * si:512 * kc + 128 * (si + 1)],
                                wq[:, 3072 * kc + 512 * grp:3072 * kc + 512 * (grp + 1)],
                                start=(kc == 0), stop=(kc == 7)),
                            after=(t_hs + [gd] if kc == 0 else []), inc="pe"))
                    if grp < 4:
                        te = B.op("scalar", lambda e, si=si, grp=grp, pp=pp: e.copy(
                            out=qk_sb[:, 2048 * (grp // 2) + 1024 * (si % 2) + 512 * (grp % 2):
                                      2048 * (grp // 2) + 1024 * (si % 2) + 512 * (grp % 2) + 512],
                            in_=pp[:]), after=tms + [P["qk_free"][si % 2]], inc="ac")
                        t_qk[si] = te
                    else:
                        te = B.op("scalar", lambda e, grp=grp, pp=pp: e.copy(
                            out=ap(v65, 65 * 8 * (grp - 4), [[1040, 128], [65, 8], [1, 64]]),
                            in_=pp[:]), after=tms + [P["v_free"]], inc="ac")
                        t_v[si] = te
                    gAB[nmm % 2] = te
                    nmm += 1

                qoff = 1024 * (si % 2)
                koff = 2048 + 1024 * (si % 2)
                jj = ((r0 // 128) + si) % 32
                t_rot = {}
                for part, off in ((0, qoff), (1, koff)):
                    src = qk_sb[:, off:off + 1024]
                    tsq = B.op("scalar", lambda e, src=src: e.activation(
                        out=sq_scr[:], in_=src, func=AF.Square),
                        after=[t_qk[si], P["sq_free"]], inc="ac")
                    trd = B.op("vector", lambda e, part=part, si=si: e.reduce_sum(
                        nq[:, 16 * part:16 * part + 16],
                        sq_scr[:].rearrange("p (h d) -> p h d", d=64), axis=AX.X),
                        after=[tsq, P["nq_free"]], inc="dv")
                    P["sq_free"] = trd
                    tt1 = B.op("vector", lambda e, src=src, jj=jj: e.tensor_tensor(
                        qrot[:, 1024:2048].rearrange("p (h d) -> p h d", d=64),
                        src.rearrange("p (h d) -> p h d", d=64),
                        ap(csb, 64 * jj, [[2048, 128], [0, 16], [1, 64]]), ALU.mult),
                        after=[t_qk[si], P["qrot_free"]], inc="dv")
                    tt2 = B.op("vector", lambda e, src=src, jj=jj: e.tensor_tensor(
                        qrot[:, 2048:3072].rearrange("p (h s d) -> p h s d", d=32, s=2),
                        ap(src.tensor, src.offset + 32, [[src.ap[0][0], 128], [64, 16], [-32, 2], [1, 32]]),
                        ap(ssb, 64 * jj, [[2048, 128], [0, 16], [32, 2], [1, 32]]), ALU.mult),
                        after=[t_qk[si], tt1], inc="dv")
                    dst = qrot[:, 0:1024] if part == 0 else krot_s[:]
                    t3 = B.op("vector", lambda e, dst=dst: e.tensor_add(
                        dst, qrot[:, 1024:2048], qrot[:, 2048:3072]),
                        after=[tt1, tt2, P["qrot_free"] if part == 0 else P["krot_free"]],
                        inc="dv")
                    t_rot[part] = t3
                    if part == 0:
                        P["qrot_free"] = t3
                    else:
                        P["krot_free"] = t3
                P["qk_free"][si % 2] = [t_rot[0], t_rot[1]]
                tbq = B.dma("sync", ap(BQ, (r0 + 128 * si) * 16, [[16, 128], [1, 16]]),
                            nq[:, 0:16], after=[trd, P["bq_dma"]], inc="st")
                P["bq_dma"] = tbq
                tbk = B.op("vector", lambda e: e.tensor_scalar(
                    nq[:, 32:48], nq[:, 16:32], -0.5, mk.cbias(-2.7725887222397811)[0:128, 0:1] if False else -2.7725887222397811,
                    ALU.mult, ALU.add), after=[trd], inc="dv")
                P["nq_free"] = [tbk, tbq]
                tqs = B.dma("sync", ap(QROT, (r0 + 128 * si) * 1024, [[1024, 128], [1, 1024]]),
                            qrot[:, 0:1024], after=[t_rot[0]], inc="sto", serial=True)
                P["qrot_free"] = tqs
                tts = []
                for c in range(8):
                    tts.append(B.op("tensor", lambda e, c=c: e.transpose(
                        ps_tr[:, 128 * c:128 * (c + 1)],
                        krot_s[:, 128 * c:128 * (c + 1)], idb[:]),
                        after=[t_rot[1], P["tr_free"]], inc="pe"))
                tkf = B.op("vector", lambda e, si=si: e.tensor_copy(
                    ap(k_fm, 128 * si, [[8 * 512, 128], [512, 8], [1, 128]]),
                    ps_tr[0:128, 0:1024].rearrange("p (c d) -> p c d", d=128)),
                    after=tts + [P["kfm_free"]], inc="dv")
                P["tr_free"] = tkf
                P["krot_free"] = tkf
                t_s_mm = []
                for hp in range(8):
                    pp = pAB[nmm % 2]
                    gd = gAB[nmm % 2]
                    tkp = B.op("tensor", lambda e, hp=hp, si=si, pp=pp: e.matmul(
                        pp[:], k_fm[:, 512 * hp + 128 * si:512 * hp + 128 * (si + 1)],
                        wbd[:, 512 * hp:512 * (hp + 1)], start=True, stop=True),
                        after=[tkf, gd], inc="pe")
                    texs = []
                    for par in range(2):
                        h = 2 * hp + par
                        texs.append(B.op("scalar", lambda e, pp=pp, par=par, h=h: e.activation(
                            out=kphi_s[:, 256 * par:256 * (par + 1)],
                            in_=pp[:, 256 * par:256 * (par + 1)], func=AF.Exp,
                            bias=nq[:, 32 + h:33 + h]),
                            after=[tkp, tbk, P["kphi_free"]], inc="ac"))
                    gAB[nmm % 2] = texs[-1]
                    nmm += 1
                    smm = []
                    for par in range(2):
                        h = 2 * hp + par
                        for mc in range(2):
                            smm.append(B.op("tensor",
                                lambda e, h=h, par=par, mc=mc: e.matmul(
                                    s_slice(h, mc),
                                    kphi_s[:, 256 * par + 128 * mc:256 * par + 128 * (mc + 1)],
                                    v65[:, 65 * h:65 * h + 65],
                                    start=False, stop=False, skip_group_check=True),
                                after=texs + [t_v[si]] + tsz, inc="pe"))
                    t_s_mm += smm
                    P["kphi_free"] = smm[-1]
                P["S_rd"] = t_s_mm[-1]
                P["v_free"] = t_s_mm[-1]
            P["hfm_free"] = tms[-1]

        tevs = []
        for i in range(5):
            w = 455 if i < 4 else 260
            tevs.append(B.op("vector", lambda e, i=i, w=w: e.tensor_copy(
                skv_sb[:, 455 * i:455 * i + w], ps_S[i][:, 0:w]),
                after=[P["S_rd"], P["skv_free"]], inc="dv"))
        P["S_rd"] = tevs[-1]
        tskv = B.dma("sync", ap(SKV, s * 128 * 2080, [[2080, 128], [1, 2080]]),
                     skv_sb[:], after=tevs, inc="st")
        P["skv_free"] = tskv

    B.wait_all("sync", "st")
    B.wait_all("sync", "sto")
    B.flush()
    stk.close()


def attn_b_passB(mk, W, Sin, QROT, BQ, SKV, Sout):
    nc, B = mk.nc, mk.B
    tag = "bb"
    TILE = 512
    NSUB = 4
    NT = SEQ // TILE
    NSEQ_L = NSEQ
    if "nt1" in mk.dbg:
        NT, NSEQ_L = 1, 1

    stk = contextlib.ExitStack()
    sb = lambda n, shp, dt: mk.sbuf(tag + n, shp, dt, stack=stk)
    idb, idf = mk.identb, mk.identf

    su = attn_b_setup(mk, W, tag, stk, need_wqkv=False)
    wbd = su["wbd"]
    woutT = sb("_woutT", (128, 8 * 1024), BF16)
    sstk = contextlib.ExitStack()
    wtmp = mk.sbuf(tag + "_wt2", (128, 1024), F32, stack=sstk)
    ps_w = mk.psum(tag + "_psw", (128, 512), F32, sstk)
    prev = [None]
    for r in range(8):
        tl = B.dma("gpsimd", wtmp[:],
                   ap(W["L0_attn_b_w_out"], r * 128 * 1024, [[1024, 128], [1, 1024]]),
                   after=prev, inc="wld", serial=True)
        prev = []
        for c in range(8):
            tt = B.op("tensor", lambda e, c=c: e.transpose(
                ps_w[:, 128 * (c % 4):128 * (c % 4) + 128],
                wtmp[:, 128 * c:128 * (c + 1)], idf[:]),
                after=[tl] + prev[-1:], inc="pe")
            tv = B.op("vector", lambda e, c=c, r=r: e.tensor_copy(
                woutT[:, 1024 * c + 128 * r:1024 * c + 128 * (r + 1)],
                ps_w[:, 128 * (c % 4):128 * (c % 4) + 128]), after=[tt], inc="dv")
            prev.append(tv)
    B.flush()
    sstk.close()

    xt = [sb(f"_xt{i}", (128, NSUB * 1024), F32) for i in range(2)]
    qr = [sb(f"_qr{i}", (128, NSUB * 1024), BF16) for i in range(2)]
    bqb = [sb(f"_bq{i}", (128, 64), F32) for i in range(2)]
    skv_b = sb("_skvb", (128, 2080), BF16)
    z_sb = sb("_z", (128, 64), F32)
    q_fm = sb("_qfm", (128, 8 * 512), BF16)
    qphi = sb("_qphi", (128, 4 * 512), BF16)
    ons = sb("_ons", (128, 64), BF16)
    oh_fm = sb("_ohfm", (128, 8 * 512), BF16)
    wsc = sb("_wsc", (128, 4), F32)
    y2 = sb("_y2", (128, NSUB * 1024), F32)

    ps_tr = mk.psum(tag + "_pstr", (128, 1024), BF16, stk)
    ps_A = mk.psum(tag + "_psA", (128, 512), F32, stk)
    ps_B2 = mk.psum(tag + "_psB", (128, 512), F32, stk)
    ps_N0 = mk.psum(tag + "_psN0", (128, 65), F32, stk)
    ps_N1 = mk.psum(tag + "_psN1", (128, 65), F32, stk)
    ps_G = mk.psum(tag + "_psG", (128, 1024), F32, stk)

    wtok = ("wld", B.sems["wld"][1])
    P = {
        "xt_free": [None, None], "qr_free": [None, None], "bq_free": [None, None],
        "z_free": wtok, "qfm_free": wtok, "qphi_free": [wtok, wtok],
        "ons_free": wtok, "oh_free": wtok, "wsc_free": wtok,
        "tr_free": wtok, "A_free": wtok, "B_free": wtok,
        "N_free": [wtok, wtok], "G_free": wtok, "y2_free": wtok, "skv_free": wtok,
    }
    LNE = NSUB * 1024
    for s in range(NSEQ_L):
        tskv = B.dma("gpsimd", skv_b[:], ap(SKV, s * 128 * 2080, [[2080, 128], [1, 2080]]),
                     after=[P["skv_free"]], inc="wld", serial=True)
        for it in range(NT):
            bi = it % 2
            x_t, q_t, bq_t = xt[bi], qr[bi], bqb[bi]
            r0 = s * SEQ + it * TILE
            t_ld = B.dma("sync", ap(x_t, 0, [[LNE, 128], [1024, NSUB], [1, 1024]]),
                         ap(Sin, r0 * 1024, [[1024, 128], [128 * 1024, NSUB], [1, 1024]]),
                         after=[P["xt_free"][bi]], inc=f"ld{bi}")
            t_lq = B.dma("sync", ap(q_t, 0, [[LNE, 128], [1024, NSUB], [1, 1024]]),
                         ap(QROT, r0 * 1024, [[1024, 128], [128 * 1024, NSUB], [1, 1024]]),
                         after=[P["qr_free"][bi]], inc=f"ld{bi}")
            t_lb = B.dma("sync", ap(bq_t, 0, [[64, 128], [16, NSUB], [1, 16]]),
                         ap(BQ, r0 * 16, [[16, 128], [128 * 16, NSUB], [1, 16]]),
                         after=[P["bq_free"][bi]], inc=f"ld{bi}")
            ld_tok = (f"ld{bi}", B.sems[f"ld{bi}"][1])

            tz = B.op("scalar", lambda e, bq_t=bq_t: e.activation(
                out=z_sb[:], in_=bq_t[:], func=AF.Exp,
                bias=mk.cbias(-2.7725887222397811), scale=-0.5),
                after=[ld_tok, P["z_free"]], inc="ac")

            t_qf = []
            prev_tt = [P["tr_free"]]
            for c in range(8):
                tts = []
                for si in range(NSUB):
                    tts.append(B.op("tensor", lambda e, c=c, si=si, q_t=q_t: e.transpose(
                        ps_tr[:, 128 * si:128 * (si + 1)],
                        q_t[:, 1024 * si + 128 * c:1024 * si + 128 * (c + 1)], idb[:]),
                        after=[ld_tok] + prev_tt, inc="pe"))
                te = B.op("vector", lambda e, c=c: e.tensor_copy(
                    q_fm[:, 512 * c:512 * (c + 1)], ps_tr[0:128, 0:512]),
                    after=tts + [P["qfm_free"]], inc="dv")
                t_qf.append(te)
                prev_tt = [te]
            P["tr_free"] = t_qf[-1]
            P["qr_free"][bi] = t_qf[-1]

            pAB = [ps_A, ps_B2]
            pN = [ps_N0, ps_N1]
            t_oh = []
            nmm = 0
            nn = 0
            for hp in range(8):
                t_qp = []
                for j in range(4):
                    pp = pAB[nmm % 2]
                    gd = P["A_free"] if nmm % 2 == 0 else P["B_free"]
                    tm = B.op("tensor", lambda e, hp=hp, j=j, pp=pp: e.matmul(
                        pp[:], wbd[:, 512 * hp + 128 * j:512 * hp + 128 * (j + 1)],
                        q_fm[:, 512 * hp:512 * (hp + 1)], start=True, stop=True),
                        after=t_qf + [gd], inc="pe")
                    tex = B.op("scalar", lambda e, j=j, pp=pp: e.activation(
                        out=qphi[:, 512 * j:512 * (j + 1)], in_=pp[:], func=AF.Exp),
                        after=[tm, P["qphi_free"][j // 2]], inc="ac")
                    if nmm % 2 == 0:
                        P["A_free"] = tex
                    else:
                        P["B_free"] = tex
                    t_qp.append(tex)
                    nmm += 1
                for par in range(2):
                    h = 2 * hp + par
                    for si in range(NSUB):
                        pp = pN[nn % 2]
                        gd = P["N_free"][nn % 2]
                        nmms = []
                        for mc in range(2):
                            a = 2 * h + mc
                            nmms.append(B.op("tensor",
                                lambda e, par=par, mc=mc, si=si, a=a, pp=pp: e.matmul(
                                    pp[:],
                                    qphi[:, 512 * (2 * par + mc) + 128 * si:
                                         512 * (2 * par + mc) + 128 * (si + 1)],
                                    skv_b[:, 65 * a:65 * (a + 1)],
                                    start=(mc == 0), stop=(mc == 1)),
                                after=[t_qp[2 * par + mc], gd] + ([tskv] if nn < 2 else []),
                                inc="pe"))
                        zc = z_sb[:, 16 * si + h:16 * si + h + 1]
                        c1 = B.op("vector", lambda e, pp=pp, zc=zc: e.tensor_scalar(
                            wsc[:, 0:1], pp[:, 64:65], zc, None, ALU.mult),
                            after=nmms + [tz, P["wsc_free"]], inc="dv")
                        c2 = B.op("vector", lambda e: e.tensor_scalar(
                            wsc[:, 1:2], wsc[:, 0:1], 1e-6, None, ALU.max),
                            after=[c1], inc="dv")
                        c3 = B.op("vector", lambda e: e.reciprocal(
                            wsc[:, 2:3], wsc[:, 1:2]), after=[c2], inc="dv")
                        c4 = B.op("vector", lambda e, zc=zc: e.tensor_scalar(
                            wsc[:, 3:4], wsc[:, 2:3], zc, None, ALU.mult),
                            after=[c3], inc="dv")
                        t_on = B.op("vector", lambda e, pp=pp: e.tensor_scalar(
                            ons[:], pp[:, 0:64], wsc[:, 3:4], None, ALU.mult),
                            after=[c4, P["ons_free"]], inc="dv")
                        P["N_free"][nn % 2] = t_on
                        P["wsc_free"] = t_on
                        nn += 1
                        ttr = B.op("tensor", lambda e: e.transpose(
                            ps_tr[0:64, 0:128], ons[:], idb[:]),
                            after=[t_on, P["tr_free"]], inc="pe")
                        tev = B.op("vector", lambda e, par=par, hp=hp, si=si: e.tensor_copy(
                            oh_fm[64 * par:64 * par + 64,
                                  512 * hp + 128 * si:512 * hp + 128 * (si + 1)],
                            ps_tr[0:64, 0:128]), after=[ttr, P["oh_free"]], inc="dv")
                        P["tr_free"] = tev
                        P["ons_free"] = tev
                        t_oh.append(tev)
                P["qphi_free"] = [t_oh[-1], t_oh[-1]]

            t_y2 = []
            for si in range(NSUB):
                toms = []
                for ec in range(8):
                    for ng in range(2):
                        toms.append(B.op("tensor",
                            lambda e, ec=ec, ng=ng, si=si: e.matmul(
                                ps_G[:, 512 * ng:512 * (ng + 1)],
                                oh_fm[:, 512 * ec + 128 * si:512 * ec + 128 * (si + 1)],
                                woutT[:, 1024 * ec + 512 * ng:1024 * ec + 512 * (ng + 1)],
                                start=(ec == 0), stop=(ec == 7)),
                            after=(t_oh + [P["G_free"]] if ec == 0 else []), inc="pe"))
                ty = B.op("vector", lambda e, si=si, x_t=x_t: e.tensor_add(
                    y2[:, 1024 * si:1024 * (si + 1)], x_t[:, 1024 * si:1024 * (si + 1)],
                    ps_G[:]), after=toms + [P["y2_free"]], inc="dv")
                P["G_free"] = ty
                t_y2.append(ty)
            P["xt_free"][bi] = t_y2[-1]
            P["bq_free"][bi] = tz
            P["oh_free"] = [toms[-1], t_y2[-1]]
            P["z_free"] = c4
            tso = B.dma("sync", ap(Sout, r0 * 1024, [[1024, 128], [128 * 1024, NSUB], [1, 1024]]),
                        ap(y2, 0, [[LNE, 128], [1024, NSUB], [1, 1024]]),
                        after=t_y2, inc="st")
            P["y2_free"] = tso
        P["skv_free"] = nmms[-1]

    B.wait_all("sync", "st")
    B.flush()
    stk.close()


_HC = None


def get_hc():
    global _HC
    if _HC is None:
        _HC = host_consts()
    return _HC


def make_in_maps(inputs, extra_streams=None):
    hc = get_hc()
    x = np.ascontiguousarray(np.asarray(inputs["L0_attn_a_x"], dtype=np.float32))
    in_maps = []
    for c in range(NCORES):
        m = {"x": np.ascontiguousarray(x[2 * c:2 * c + 2].reshape(TOK, DM))}
        for k, v in inputs.items():
            if k == "L0_attn_a_x":
                continue
            m[k] = np.ascontiguousarray(np.asarray(v, dtype=np.float32))
        for k, v in hc.items():
            m[k] = v
        if extra_streams:
            for k, v in extra_streams.items():
                m[k] = np.ascontiguousarray(v[c])
        in_maps.append(m)
    return in_maps


def kernel(**inputs):
    mk = build_program()
    in_maps = make_in_maps(inputs)
    res = run_bass_kernel_spmd(mk.nc, in_maps, core_ids=list(range(NCORES)))
    out = np.concatenate([r["OUT"].reshape(2, SEQ, DM) for r in res.results], 0)
    return out


# version 03 (final)
# speedup vs baseline: 1.0218x; 1.0218x over previous
import contextlib
import numpy as np
import concourse.bass as bass
import concourse.mybir as mybir
from concourse.bass_utils import run_bass_kernel_spmd

F32 = mybir.dt.float32
BF16 = mybir.dt.bfloat16
AF = mybir.ActivationFunctionType
ALU = mybir.AluOpType
AX = mybir.AxisListType

NCORES = 8
SEQ = 4096
DM = 1024
NSEQ = 2
TOK = NSEQ * SEQ

ENG_NAMES = ("sync", "tensor", "vector", "scalar", "gpsimd")


def ap(t, off, dims):
    return bass.AP(t, off, [list(d) for d in dims])


class Builder:
    def __init__(self, nc):
        self.nc = nc
        self.sems = {}
        self._stack = contextlib.ExitStack()
        self.reset_queues()

    def reset_queues(self):
        self.q = {e: [] for e in ENG_NAMES}
        self.last_wait = {e: {} for e in ENG_NAMES}

    def sem(self, name, dma=False):
        h = self._stack.enter_context(self.nc.semaphore(name))
        self.sems[name] = [h, 0, 0]
        if not hasattr(self, "dma_sems"):
            self.dma_sems = set()
        if dma:
            self.dma_sems.add(name)

    def op(self, eng, fn, after=(), inc=None, inc_n=1):
        waits = []
        flat = []
        for t in after:
            if t is None:
                continue
            if isinstance(t, list):
                flat.extend(x for x in t if x is not None)
            else:
                flat.append(t)
        for t in flat:
            sname, val = t
            s = self.sems[sname]
            if val <= s[2]:
                continue
            lw = self.last_wait[eng]
            if lw.get(sname, -1) >= val:
                continue
            lw[sname] = val
            waits.append((sname, val))
        tok = None
        if inc is not None:
            s = self.sems[inc]
            s[1] += inc_n
            tok = (inc, s[1])
        self.q[eng].append((waits, fn, inc, inc_n))
        return tok

    def dma(self, queue, out, in_, after=(), inc="dma", serial=False, **kw):
        if serial:
            st = self.sems[inc]
            if st[1] > 0:
                self.op(queue, None, after=[(inc, st[1])])
        return self.op(queue, lambda e: e.dma_start(out=out, in_=in_, **kw),
                       after=after, inc=inc, inc_n=16)

    def wait_all(self, eng, sem):
        s = self.sems[sem]
        if s[1] > s[2]:
            self.op(eng, lambda e: e.wait_ge(s[0], s[1]))

    def flush(self):
        nc = self.nc
        qs = self.q
        sems = self.sems
        with nc.Block() as blk:
            for ename in ENG_NAMES:
                if not qs[ename]:
                    continue
                ops = qs[ename]

                def make(ops=ops):
                    def run(eng):
                        for waits, fn, inc, inc_n in ops:
                            for sname, val in waits:
                                eng.wait_ge(sems[sname][0], val)
                            if fn is None:
                                continue
                            ins = fn(eng)
                            if inc is not None:
                                ins.then_inc(sems[inc][0], inc_n)
                    return run
                getattr(blk, ename)(make())
        for name, s in self.sems.items():
            if name not in getattr(self, "dma_sems", set()):
                s[2] = s[1]
        self.reset_queues()


def host_consts():
    c = {}
    DH, WIN = 64, 512
    half = np.arange(0, DH, 2, dtype=np.float64)
    inv_freq = 1.0 / 10000.0 ** (half / DH)
    t = np.arange(SEQ, dtype=np.float64)
    freqs = t[:, None] * inv_freq[None, :]
    base = (half + 0.4 * DH) / (1.4 * DH)
    power = (t - SEQ // 2) / float(WIN // 2)
    scale = base[None, :] ** power[:, None]
    cos = np.cos(freqs)
    sin = np.sin(freqs)
    signs = np.concatenate([-np.ones(32), np.ones(32)])

    def lay(a):
        return np.ascontiguousarray(
            a.reshape(32, 128, 64).transpose(1, 0, 2).reshape(128, 2048)
        ).astype(np.float32)

    c["tab_csq"] = lay(np.concatenate([cos * scale, cos * scale], 1))
    c["tab_ssq"] = lay(np.concatenate([sin * scale, sin * scale], 1) * signs)
    c["tab_csk"] = lay(np.concatenate([cos / scale, cos / scale], 1))
    c["tab_ssk"] = lay(np.concatenate([sin / scale, sin / scale], 1) * signs)

    fr_b = 10000.0 ** (-np.arange(0, 64, 2, dtype=np.float64) / 64)
    pos = np.arange(SEQ, dtype=np.float64) - 1.0
    ang = pos[:, None] * fr_b[None, :]
    cb = np.cos(ang)
    sb = np.sin(ang)
    cb[0, :] = 1.0
    sb[0, :] = 0.0
    c["tab_csb"] = lay(np.concatenate([cb, cb], 1))
    c["tab_ssb"] = lay(np.concatenate([sb, sb], 1) * signs)

    H = np.array([[1.0]])
    while H.shape[0] < 64:
        H = np.block([[H, H], [H, -H]])
    sig = np.concatenate([np.arange(0, 64, 2), np.arange(1, 64, 2)])
    c["had"] = np.ascontiguousarray(H.astype(np.float32))
    PT = np.zeros((64, 64))
    PT[sig, np.arange(64)] = 1.0
    PT2 = np.zeros((128, 128))
    PT2[0:64, 0:64] = PT
    PT2[64:128, 64:128] = PT
    c["permT"] = np.ascontiguousarray(PT2.astype(np.float32))
    return c


class MK:
    def __init__(self, phases=None, dbg=()):
        self.phases = phases
        self.dbg = set(dbg)
        self.nc = bass.Bass("TRN2", target_bir_lowering=False)
        self.st = contextlib.ExitStack()
        self.B = Builder(self.nc)
        for s in ("pe", "dv", "ac", "pl"):
            self.B.sem(s)
        for s in ("ld0", "ld1", "st", "wld", "sto", "sh0", "sh1", "dma"):
            self.B.sem(s, dma=True)

    def sel(self, p):
        return self.phases is None or p in self.phases

    def sbuf(self, name, shape, dt, stack=None):
        s = stack if stack is not None else self.st
        return s.enter_context(self.nc.sbuf_tensor(name, list(shape), dt))

    def psum(self, name, shape, dt, stack):
        return stack.enter_context(self.nc.psum_tensor(name, list(shape), dt))


WSPECS = {
    "L0_attn_a_ln_w": (1024,), "L0_attn_a_ln_b": (1024,),
    "L0_attn_a_W_in": (1024, 3072), "L0_attn_a_conv_w": (1536, 3),
    "L0_attn_a_A": (16, 16), "L0_attn_a_Bm": (16, 1536),
    "L0_attn_a_Cm": (1536, 16), "L0_attn_a_W_out": (1536, 1024),
    "L0_attn_b_w_qkv": (3072, 1024), "L0_attn_b_w_out": (1024, 1024),
    "L0_attn_b_sorf_signs": (16, 4, 3, 64), "L0_attn_b_sorf_scales": (16, 4, 64),
    "L0_ffn_wqkv": (2, 1024, 1536), "L0_ffn_q_scale": (2, 64),
    "L0_ffn_k_scale": (2, 64), "L0_ffn_wo": (2, 512, 1024),
    "L0_ffn_ln_g": (2, 1024), "L0_ffn_ln_b": (2, 1024),
    "L0_ffn_w1": (2, 1024, 5460), "L0_ffn_b1": (2, 5460),
    "L0_ffn_w2": (2, 2730, 1024), "L0_ffn_b2": (2, 1024),
    "tab_csq": (128, 2048), "tab_ssq": (128, 2048),
    "tab_csk": (128, 2048), "tab_ssk": (128, 2048),
    "tab_csb": (128, 2048), "tab_ssb": (128, 2048),
    "had": (64, 64), "permT": (128, 128),
}


def build_program(phases=None, dbg=()):
    mk = MK(phases, dbg)
    nc, B = mk.nc, mk.B
    sel = mk.sel

    W = {n: nc.dram_tensor(n, list(s), F32, kind="ExternalInput")
         for n, s in WSPECS.items()}
    x_in = nc.dram_tensor("x", [TOK, DM], F32, kind="ExternalInput")

    def stream(name, produced, consumers, dt=F32, shape=(TOK, DM)):
        if phases is None:
            kind = "ExternalOutput" if name == "OUT" else None
        else:
            p_in = sel(produced)
            c_in = any(sel(c) for c in consumers)
            if not p_in and not c_in:
                return None
            if name == "OUT":
                kind = "ExternalOutput"
            elif p_in and c_in:
                kind = "ExternalOutput" if name in mk.dbg else None
            elif p_in:
                kind = "ExternalOutput"
            else:
                kind = "ExternalInput"
        if kind is None:
            return nc.dram_tensor(name, list(shape), dt)
        return nc.dram_tensor(name, list(shape), dt, kind=kind)

    S1 = stream("S1", "p1", ["f0a", "f0d"])
    S2 = stream("S2", "f0a", ["f0b"])
    H1 = stream("H1", "f0a", ["f0b"], BF16)
    S3 = stream("S3", "f0b", ["f0c"])
    S4 = stream("S4", "f0c", ["f0d"])
    H3 = stream("H3", "f0c", ["f0d"], BF16)
    S5 = stream("S5", "f0d", ["ba", "bb"])
    QROT = stream("QROT", "ba", ["bb"], BF16)
    BQ = stream("BQ", "ba", ["bb"], F32, (TOK, 16))
    SKV = stream("SKV", "ba", ["bb"], F32, (NSEQ * 128, 16 * 2 * 65))
    S6 = stream("S6", "bb", ["f1a", "f1d"])
    S7 = stream("S7", "f1a", ["f1b"])
    H7 = stream("H7", "f1a", ["f1b"], BF16)
    S8 = stream("S8", "f1b", ["f1c"])
    S9 = stream("S9", "f1c", ["f1d"])
    H9 = stream("H9", "f1c", ["f1d"], BF16)
    OUT = stream("OUT", "f1d", [])

    mk.cpool = mk.sbuf("cpool", (128, 8), F32)
    CVALS = [1e-5, 1e-10, -2.7725887222397811, 1.0, 0.0, 0.5, -0.5, 2.0]
    mk.cmap = {v: i for i, v in enumerate(CVALS)}
    mk.cbias = lambda v: mk.cpool[:, mk.cmap[v]:mk.cmap[v] + 1]
    for v, i in mk.cmap.items():
        B.op("gpsimd", lambda e, v=v, i=i: e.memset(mk.cpool[:, i:i + 1], v), inc="pl")
    mk.identb = mk.sbuf("identb", (128, 128), BF16)
    mk.identf = mk.sbuf("identf", (128, 128), F32)
    mk.ones = mk.sbuf("onesrow", (1, 2048), F32)
    mk.onesb = mk.sbuf("onesrowb", (1, 2048), BF16)

    def aff_ident(t):
        def f(e, t=t):
            return e.affine_select(out=t[:], in_=t[:], compare_op=ALU.not_equal,
                                   fill=1.0, base=0, pattern=[[-1, 128]],
                                   channel_multiplier=1)
        return f
    t0 = B.op("gpsimd", lambda e: e.memset(mk.identb[:], 0.0), inc="pl")
    B.op("gpsimd", aff_ident(mk.identb), after=[t0], inc="pl")
    t1 = B.op("gpsimd", lambda e: e.memset(mk.identf[:], 0.0), inc="pl")
    B.op("gpsimd", aff_ident(mk.identf), after=[t1], inc="pl")
    B.op("gpsimd", lambda e: e.memset(mk.ones[:], 1.0), inc="pl")
    B.op("gpsimd", lambda e: e.memset(mk.onesb[:], 1.0), inc="pl")
    B.flush()

    if sel("p1"):
        p1_attn_a(mk, W, x_in, S1)
    if sel("f0a"):
        ffn_attn(mk, W, 0, "f0a", S1, S2, H1, outer_ln=True)
    if sel("f0b"):
        ffn_mlp(mk, W, 0, "f0b", S2, H1, S3, extra=None)
    if sel("f0c"):
        ffn_attn(mk, W, 1, "f0c", S3, S4, H3, outer_ln=False)
    if sel("f0d"):
        ffn_mlp(mk, W, 1, "f0d", S4, H3, S5, extra=S1)
    if sel("ba"):
        attn_b_passA(mk, W, S5, QROT, BQ, SKV)
    if sel("bb"):
        attn_b_passB(mk, W, S5, QROT, BQ, SKV, S6)
    if sel("f1a"):
        ffn_attn(mk, W, 0, "f1a", S6, S7, H7, outer_ln=True)
    if sel("f1b"):
        ffn_mlp(mk, W, 0, "f1b", S7, H7, S8, extra=None)
    if sel("f1c"):
        ffn_attn(mk, W, 1, "f1c", S8, S9, H9, outer_ln=False)
    if sel("f1d"):
        ffn_mlp(mk, W, 1, "f1d", S9, H9, OUT, extra=S6)

    return mk



def p1_attn_a(mk, W, x_in, S1):
    nc, B = mk.nc, mk.B
    TILE = 256
    NSUB = TILE // 128
    NT = SEQ // TILE
    ND = 64
    NSEQ_L = NSEQ
    if "nt1" in mk.dbg:
        NT, NSEQ_L = 1, 1
    for f in mk.dbg:
        if f.startswith("ntx"):
            NT, NSEQ_L = int(f[3:]), 1

    stk = contextlib.ExitStack()
    sb = lambda n, shp, dt: mk.sbuf(n, shp, dt, stack=stk)
    idf, idb, ones = mk.identf, mk.identb, mk.ones

    Wp = sb("a_Wp", (128, 8 * 3072), BF16)
    Wo = sb("a_Wo", (128, 12 * 1024), BF16)
    pb = sb("a_pb", (1, 3072), F32)
    bmT = sb("a_bmT", (128, 12 * 16), BF16)
    cmT = sb("a_cmT", (16, 12 * 128), BF16)
    mpow = sb("a_mpow", (16, ND * 16), BF16)
    cw = sb("a_cw", (128, 36), F32)

    skip_setup = "nosetup" in mk.dbg
    sstk = contextlib.ExitStack()
    ssb = lambda n, shp, dt: mk.sbuf(n, shp, dt, stack=sstk)
    wtmp = ssb("a_wtmp", (128, 3072), F32)
    lnw_r = ssb("a_lnwr", (1, 1024), F32)
    lnb_r = ssb("a_lnbr", (1, 1024), F32)
    lnw_c = ssb("a_lnwc", (128, 8), F32)
    lnb_c = ssb("a_lnbc", (128, 8), F32)
    bm_r = ssb("a_bmr", (16, 1536), F32)
    m_sb = ssb("a_msb", (16, 16), F32)
    x_sb = ssb("a_xsb", (16, 16), F32)
    mm_sb = ssb("a_mmsb", (16, 16), F32)
    ps_a = mk.psum("a_ps_a", (128, 512), F32, sstk)
    ps_b = mk.psum("a_ps_b", (128, 512), F32, sstk)
    ps_pb = mk.psum("a_ps_pb", (1, 3072), F32, sstk)

    if skip_setup:
        B.flush()
        sstk.close()
    else:
      tlw = B.dma("gpsimd", lnw_r[:], ap(W["L0_attn_a_ln_w"], 0, [[0, 1], [1, 1024]]), inc="wld", serial=True)
      tlb = B.dma("gpsimd", lnb_r[:], ap(W["L0_attn_a_ln_b"], 0, [[0, 1], [1, 1024]]), inc="wld", serial=True)
      tt_all = []
      for c in range(8):
          tt_all.append(B.op("tensor", lambda e, c=c: e.transpose(
              ps_a[:, c:c + 1], lnw_r[0:1, 128 * c:128 * (c + 1)], idf[0:1, 0:1]),
              after=[tlw], inc="pe"))
          tt_all.append(B.op("tensor", lambda e, c=c: e.transpose(
              ps_b[:, c:c + 1], lnb_r[0:1, 128 * c:128 * (c + 1)], idf[0:1, 0:1]),
              after=[tlb], inc="pe"))
      tw = B.op("vector", lambda e: e.tensor_copy(lnw_c[:], ps_a[:, 0:8]), after=tt_all, inc="dv")
      tb = B.op("vector", lambda e: e.tensor_copy(lnb_c[:], ps_b[:, 0:8]), after=tt_all, inc="dv")

      prev_k = [tw, tb]
      for kc in range(8):
          tl = B.dma("gpsimd", wtmp[:],
                     ap(W["L0_attn_a_W_in"], kc * 128 * 3072, [[3072, 128], [1, 3072]]),
                     after=prev_k, inc="wld", serial=True)
          tv = B.op("vector", lambda e, kc=kc: e.tensor_scalar(
              Wp[:, 3072 * kc:3072 * (kc + 1)], wtmp[:], lnw_c[:, kc:kc + 1], None,
              ALU.mult), after=[tl, tw], inc="dv")
          cur = [tv]
          for ng in range(6):
              cur.append(B.op("tensor", lambda e, kc=kc, ng=ng: e.matmul(
                  ps_pb[0:1, 512 * ng:512 * (ng + 1)], lnb_c[:, kc:kc + 1],
                  wtmp[:, 512 * ng:512 * (ng + 1)], start=(kc == 0), stop=(kc == 7)),
                  after=[tl, tb], inc="pe"))
          prev_k = cur
      tpb = B.op("vector", lambda e: e.tensor_copy(pb[:], ps_pb[:]), after=prev_k, inc="dv")

      two = [B.dma("gpsimd", Wo[:, 1024 * c:1024 * (c + 1)],
                   ap(W["L0_attn_a_W_out"], c * 128 * 1024, [[1024, 128], [1, 1024]]),
                   inc="wld", serial=True) for c in range(12)]
      B.op("vector", lambda e: e.tensor_scalar(Wo[:], Wo[:], 0.5, None, ALU.mult),
           after=two, inc="dv")

      tbm = B.dma("gpsimd", bm_r[:], ap(W["L0_attn_a_Bm"], 0, [[1536, 16], [1, 1536]]), inc="wld", serial=True)
      prev = [tw, tb]
      for c in range(12):
          tt = B.op("tensor", lambda e, c=c: e.transpose(
              ps_a[:, 0:16], bm_r[0:16, 128 * c:128 * (c + 1)], idf[0:16, 0:16]),
              after=[tbm] + prev, inc="pe")
          tv = B.op("vector", lambda e, c=c: e.tensor_scalar(
              bmT[:, 16 * c:16 * (c + 1)], ps_a[:, 0:16], 0.5, None, ALU.mult),
              after=[tt], inc="dv")
          prev = [tv]

      tcm = B.dma("gpsimd", wtmp[:, 0:192],
                  ap(W["L0_attn_a_Cm"], 0, [[16, 128], [2048, 12], [1, 16]]),
                  after=[tpb] + prev_k, inc="wld", serial=True)
      prevc = [tb]
      for c in range(12):
          tt = B.op("tensor", lambda e, c=c: e.transpose(
              ps_b[0:16, 0:128], wtmp[:, 16 * c:16 * (c + 1)], idf[:, 0:128]),
              after=[tcm] + prevc, inc="pe")
          tv = B.op("vector", lambda e, c=c: e.tensor_copy(
              cmT[0:16, 128 * c:128 * (c + 1)], ps_b[0:16, 0:128]), after=[tt], inc="dv")
          prevc = [tv]

      ta = B.dma("gpsimd", m_sb[:], ap(W["L0_attn_a_A"], 0, [[16, 16], [1, 16]]), inc="wld", serial=True)
      tmt = B.op("vector", lambda e: e.tensor_add(m_sb[:], m_sb[:], idf[0:16, 0:16]),
                 after=[ta], inc="dv")
      tmm = B.op("tensor", lambda e: e.transpose(ps_a[0:16, 16:32], m_sb[:], idf[0:16, 0:16]),
                 after=[tmt] + prev, inc="pe")
      tme = B.op("vector", lambda e: e.tensor_copy(mm_sb[:], ps_a[0:16, 16:32]),
                 after=[tmm], inc="dv")
      tx = B.op("vector", lambda e: e.tensor_copy(x_sb[:], idf[0:16, 0:16]), inc="dv")
      prevp = [tx, tme]
      for d in range(ND):
          tt = B.op("tensor", lambda e, d=d: e.transpose(
              ps_b[0:16, 0:16], x_sb[:], idf[0:16, 0:16]), after=prevp, inc="pe")
          te = B.op("vector", lambda e, d=d: e.tensor_copy(
              mpow[0:16, 16 * d:16 * (d + 1)], ps_b[0:16, 0:16]), after=[tt], inc="dv")
          if d < ND - 1:
              tm2 = B.op("tensor", lambda e: e.matmul(
                  ps_a[0:16, 0:16], mm_sb[:], x_sb[:], start=True, stop=True),
                  after=prevp + [tt], inc="pe")
              tx2 = B.op("vector", lambda e: e.tensor_copy(x_sb[:], ps_a[0:16, 0:16]),
                         after=[tm2, te], inc="dv")
              prevp = [tx2, te]
          else:
              prevp = [te]

      tcw = B.dma("gpsimd", cw[:, 0:36],
                  ap(W["L0_attn_a_conv_w"], 0, [[3, 128], [384, 12], [1, 3]]), inc="wld", serial=True)
      B.flush()
      sstk.close()

    xt = [sb(f"a_xt{i}", (128, NSUB * 1024), F32) for i in range(2)]
    xho = sb("a_xho", (128, NSUB * 1024), F32)
    h2n = sb("a_h2n", (128, NSUB * 1024), BF16)
    hfm = sb("a_hfm", (128, 8 * TILE), BF16)
    data = sb("a_data", (128, 12 * TILE), F32)
    dext = sb("a_dext", (128, 12 * (TILE + 2)), F32)
    tnh = sb("a_tnh", (128, 12 * TILE), F32)
    u_sb = sb("a_u", (128, 12 * TILE), BF16)
    gyb = sb("a_gy", (128, 12 * TILE), BF16)
    bu_pad = sb("a_bu", (16, ND + SEQ), BF16)
    st_sb = sb("a_st", (16, TILE), BF16)
    y_scr = sb("a_yscr", (128, TILE), F32)
    tg_scr = sb("a_tg", (128, TILE), F32)
    stats = sb("a_stats", (128, 16), F32)
    rr = sb("a_rr", (128, 2 * NSUB), F32)

    ps_tr = mk.psum("a_pstr", (128, TILE), BF16, stk)
    ps_d0 = mk.psum("a_psd0", (128, TILE), F32, stk)
    ps_d1 = mk.psum("a_psd1", (128, TILE), F32, stk)
    ps_bu = mk.psum("a_psbu", (16, TILE), F32, stk)
    ps_st = mk.psum("a_psst", (16, TILE), F32, stk)
    ps_y = mk.psum("a_psy", (128, TILE), F32, stk)
    ps_o = mk.psum("a_pso", (128, 1024), F32, stk)
    pd = [ps_d0, ps_d1]

    dbg = "p1dbg" in mk.dbg
    if dbg:
        DXHO = nc.dram_tensor("DXHO", [128, NSUB * 1024], F32, kind="ExternalOutput")
        DXT = nc.dram_tensor("DXT", [128, NSUB * 1024], F32, kind="ExternalOutput")
        DSTATS = nc.dram_tensor("DSTATS", [128, 16], F32, kind="ExternalOutput")
        DRR = nc.dram_tensor("DRR", [128, 2 * NSUB], F32, kind="ExternalOutput")
        DH2N = nc.dram_tensor("DH2N", [128, NSUB * 1024], F32, kind="ExternalOutput")
        DDATA = nc.dram_tensor("DDATA", [128, 12 * TILE], F32, kind="ExternalOutput")
        DU = nc.dram_tensor("DU", [128, 12 * TILE], F32, kind="ExternalOutput")
        DBU = nc.dram_tensor("DBU", [16, TILE], F32, kind="ExternalOutput")
        DST = nc.dram_tensor("DST", [16, TILE], F32, kind="ExternalOutput")
        DGY = nc.dram_tensor("DGY", [128, 12 * TILE], F32, kind="ExternalOutput")

    wtok = ("wld", B.sems["wld"][1])
    P = {
        "dbg_xho": None,
        "xt_free": [None, None],
        "xho_free": None,
        "stats_free": None, "rr_free": None, "hfm_free": wtok,
        "pd_free": [wtok, wtok],
        "tr_free": wtok, "y_free": wtok, "tg_free": wtok,
        "o_free": wtok, "bu_rd": wtok, "st_free": wtok,
        "u_free": wtok, "gy_free": wtok, "dext_free": wtok,
        "h2n_free": None, "data_free": wtok, "tnh_free": wtok,
    }

    for s in range(NSEQ_L):
        tz = B.op("vector", lambda e: e.memset(bu_pad[0:16, 0:ND], 0.0),
                  after=[P["bu_rd"]], inc="dv")
        tdz = B.op("vector", lambda e: e.memset(dext[:], 0.0),
                   after=[P["dext_free"]], inc="dv")
        P["dext_free"] = tdz
        for it in range(NT):
            bi = it % 2
            x_t = xt[bi]
            r0 = s * SEQ + it * TILE
            t_ld = B.dma("sync", ap(x_t, 0, [[NSUB * 1024, 128], [1024, NSUB], [1, 1024]]),
                         ap(x_in, r0 * 1024, [[1024, 128], [128 * 1024, NSUB], [1, 1024]]),
                         after=[P["xt_free"][bi]], inc=f"ld{bi}")

            tstats = []
            tprev = [t_ld, P["stats_free"]]
            for si in range(NSUB):
                o = 1024 * si
                b1 = B.op("vector", lambda e, o=o, x_t=x_t: e.bn_stats(
                    stats[:, 0:6], x_t[:, o:o + 512]), after=tprev, inc="dv")
                b2 = B.op("vector", lambda e, o=o, x_t=x_t: e.bn_stats(
                    stats[:, 6:12], x_t[:, o + 512:o + 1024]), after=[t_ld, b1], inc="dv")
                tag = B.op("vector", lambda e, si=si: e.bn_aggr(
                    stats[:, 12 + 2 * si:14 + 2 * si], stats[:, 0:12]),
                    after=[b1, b2], inc="dv")
                tstats.append(tag)
                tprev = [t_ld, tag]
            var_ap = ap(stats, 13, [[16, 128], [2, NSUB]])
            dstat = None
            if dbg and s == 0 and it == 0:
                dstat = B.dma("sync", ap(DSTATS, 0, [[16, 128], [1, 16]]),
                              stats[:], after=tstats, inc="sto", serial=True)
            tl1 = B.op("scalar", lambda e: e.activation(
                out=rr[:, 0:NSUB], in_=var_ap, func=AF.Ln, bias=mk.cbias(1e-5), scale=1.0),
                after=tstats + [P["rr_free"], dstat], inc="ac")
            tr1 = B.op("scalar", lambda e: e.activation(
                out=rr[:, 0:NSUB], in_=rr[:, 0:NSUB], func=AF.Exp, scale=-0.5),
                after=[tl1], inc="ac")
            tl2 = B.op("scalar", lambda e: e.activation(
                out=rr[:, NSUB:2 * NSUB], in_=var_ap, func=AF.Ln, bias=mk.cbias(1e-10),
                scale=1.0 + 1e-5), after=tstats + [P["rr_free"]], inc="ac")
            tr2 = B.op("scalar", lambda e: e.activation(
                out=rr[:, NSUB:2 * NSUB], in_=rr[:, NSUB:2 * NSUB], func=AF.Exp, scale=-0.5), after=[tl2], inc="ac")

            t_ho = []
            for si in range(NSUB):
                o = 1024 * si
                mu = ap(stats, 12 + 2 * si, [[16, 128], [1, 1]])
                ta1 = B.op("vector", lambda e, o=o, si=si, mu=mu, x_t=x_t: e.tensor_scalar(
                    xho[:, o:o + 1024], x_t[:, o:o + 1024], mu, rr[:, si:si + 1],
                    ALU.subtract, ALU.mult), after=[tr1, P["xho_free"]], inc="dv")
                ta2 = B.op("vector", lambda e, o=o, si=si, mu=mu, x_t=x_t: e.tensor_scalar(
                    h2n[:, o:o + 1024], x_t[:, o:o + 1024], mu,
                    rr[:, NSUB + si:NSUB + si + 1], ALU.subtract, ALU.mult),
                    after=[tr2, P["h2n_free"]], inc="dv")
                ta3 = B.op("vector", lambda e, o=o, x_t=x_t: e.tensor_add(
                    xho[:, o:o + 1024], xho[:, o:o + 1024], x_t[:, o:o + 1024]),
                    after=[ta1], inc="dv")
                t_ho += [ta2, ta3]
            P["xt_free"][bi] = t_ho[-1]
            P["stats_free"] = t_ho[-1]
            P["rr_free"] = t_ho[-1]

            if dbg and s == 0 and it == 0:
                d1 = B.dma("sync", ap(DXT, 0, [[NSUB * 1024, 128], [1, NSUB * 1024]]),
                      x_t[:], after=t_ho, inc="sto", serial=True)
                d2 = dstat
                d3 = B.dma("sync", ap(DRR, 0, [[2 * NSUB, 128], [1, 2 * NSUB]]),
                      rr[:], after=t_ho, inc="sto", serial=True)
                d4 = B.dma("sync", ap(DXHO, 0, [[NSUB * 1024, 128], [1, NSUB * 1024]]),
                      xho[:], after=t_ho, inc="sto", serial=True)
                d5 = B.dma("gpsimd", ap(DH2N, 0, [[NSUB * 1024, 128], [1, NSUB * 1024]]),
                      h2n[:], after=t_ho, inc="sto", serial=True)
                P["xt_free"][bi] = d1
                P["stats_free"] = d2
                P["rr_free"] = d3
                P["dbg_xho"] = d4
                P["h2n_free"] = d5

            if "cut1" in mk.dbg:
                continue
            t_tr = []
            tr_guard = [P["data_free"], P["hfm_free"]]
            prev_tt = [P["tr_free"]]
            for c in range(8):
                tts = []
                for si in range(NSUB):
                    tts.append(B.op("tensor", lambda e, c=c, si=si: e.transpose(
                        ps_tr[:, 128 * si:128 * (si + 1)],
                        h2n[:, 1024 * si + 128 * c:1024 * si + 128 * (c + 1)],
                        idb[:]), after=t_ho + prev_tt, inc="pe"))
                te = B.op("vector", lambda e, c=c: e.tensor_copy(
                    hfm[:, TILE * c:TILE * (c + 1)], ps_tr[:, 0:TILE]),
                    after=tts + tr_guard, inc="dv")
                t_tr.append(te)
                prev_tt = [te]
            P["tr_free"] = t_tr[-1]
            P["h2n_free"] = t_tr[-1]

            t_data = []
            for mc in range(12):
                pp = pd[mc % 2]
                guard = [t_data[mc - 2]] if mc >= 2 else [P["pd_free"][mc % 2]]
                tms = []
                for kc in range(8):
                    tms.append(B.op("tensor", lambda e, mc=mc, kc=kc, pp=pp: e.matmul(
                        pp[:], Wp[:, 3072 * kc + 128 * mc:3072 * kc + 128 * (mc + 1)],
                        hfm[:, TILE * kc:TILE * (kc + 1)], start=(kc == 0), stop=False),
                        after=(t_tr + guard if kc == 0 else []), inc="pe"))
                tms.append(B.op("tensor", lambda e, mc=mc, pp=pp: e.matmul(
                    pp[:], pb[0:1, 128 * mc:128 * (mc + 1)], ones[0:1, 0:TILE],
                    start=False, stop=True), inc="pe"))
                te = B.op("scalar", lambda e, mc=mc, pp=pp: e.copy(
                    out=data[:, TILE * mc:TILE * (mc + 1)], in_=pp[:]),
                    after=tms + [P["tnh_free"] if False else None], inc="ac")
                t_data.append(te)

            if dbg and s == 0 and it == 0:
                d6 = B.dma("sync", ap(DDATA, 0, [[12 * TILE, 128], [1, 12 * TILE]]),
                      data[:], after=t_data, inc="sto", serial=True)
                P["tnh_free"] = d6
            tcp = B.op("vector", lambda e: e.tensor_copy(
                ap(dext, 2, [[12 * (TILE + 2), 128], [TILE + 2, 12], [1, TILE]]),
                ap(data, 0, [[12 * TILE, 128], [TILE, 12], [1, TILE]])),
                after=t_data + [P["dext_free"]], inc="dv")
            t_conv = []
            for c in range(12):
                ob = c * (TILE + 2)
                tc1 = B.op("vector", lambda e, ob=ob, c=c: e.tensor_scalar(
                    tnh[:, TILE * c:TILE * (c + 1)], dext[:, ob:ob + TILE],
                    cw[:, 3 * c:3 * c + 1], None, ALU.mult),
                    after=[tcp, P["tnh_free"]], inc="dv")
                tc2 = B.op("vector", lambda e, ob=ob, c=c: e.scalar_tensor_tensor(
                    tnh[:, TILE * c:TILE * (c + 1)], dext[:, ob + 1:ob + 1 + TILE],
                    cw[:, 3 * c + 1:3 * c + 2], tnh[:, TILE * c:TILE * (c + 1)],
                    ALU.mult, ALU.add), after=[tc1], inc="dv")
                tc3 = B.op("vector", lambda e, ob=ob, c=c: e.scalar_tensor_tensor(
                    data[:, TILE * c:TILE * (c + 1)], dext[:, ob + 2:ob + 2 + TILE],
                    cw[:, 3 * c + 2:3 * c + 3], tnh[:, TILE * c:TILE * (c + 1)],
                    ALU.mult, ALU.add), after=[tc2], inc="dv")
                t_conv.append(tc3)
            tcar = B.op("vector", lambda e: e.tensor_copy(
                ap(dext, 0, [[12 * (TILE + 2), 128], [TILE + 2, 12], [1, 2]]),
                ap(dext, TILE, [[12 * (TILE + 2), 128], [TILE + 2, 12], [1, 2]])),
                after=t_conv, inc="dv")
            P["dext_free"] = tcar
            P["data_free"] = t_conv[-1]
            tth = B.op("scalar", lambda e: e.activation(
                out=tnh[:], in_=data[:, 0:12 * TILE], func=AF.Tanh, scale=0.5),
                after=t_conv, inc="ac")
            P["tnh_free"] = tth
            tu = B.op("vector", lambda e: e.scalar_tensor_tensor(
                u_sb[:], tnh[:], 1.0, data[:, 0:12 * TILE], ALU.add, ALU.mult),
                after=[tth, P["u_free"]], inc="dv")
            P["data_free"] = tu
            P["tnh_free"] = tu

            if dbg and s == 0 and it == 0:
                P["u_free"] = B.dma("gpsimd", ap(DU, 0, [[12 * TILE, 128], [1, 12 * TILE]]),
                      u_sb[:], after=[tu], inc="sto", serial=True)
            if "cut2" in mk.dbg:
                continue
            tbu_m = []
            for kc in range(12):
                tbu_m.append(B.op("tensor", lambda e, kc=kc: e.matmul(
                    ps_bu[:], bmT[:, 16 * kc:16 * (kc + 1)],
                    u_sb[:, TILE * kc:TILE * (kc + 1)], start=(kc == 0),
                    stop=(kc == 11)), after=[tu] + ([P["bu_psfree"]] if "bu_psfree" in P else []), inc="pe"))
            o_bu = ND + it * TILE
            tbu = B.op("vector", lambda e, o_bu=o_bu: e.tensor_copy(
                bu_pad[0:16, o_bu:o_bu + TILE], ps_bu[:]), after=tbu_m + [tz], inc="dv")
            P["bu_psfree"] = tbu
            P["u_free"] = tbu_m[-1]

            if dbg and s == 0 and it == 0:
                B.dma("gpsimd", ap(DBU, 0, [[TILE, 16], [1, TILE]]),
                      bu_pad[0:16, o_bu:o_bu + TILE], after=[tbu], inc="sto", serial=True)
            tst_m = []
            for d in range(ND):
                tst_m.append(B.op("tensor", lambda e, d=d, o_bu=o_bu: e.matmul(
                    ps_st[:], mpow[0:16, 16 * d:16 * (d + 1)],
                    bu_pad[0:16, o_bu - d:o_bu - d + TILE],
                    start=(d == 0), stop=(d == ND - 1)),
                    after=[tbu, P["st_free"]], inc="pe"))
            tst = B.op("vector", lambda e: e.tensor_copy(st_sb[:], ps_st[:]),
                       after=tst_m, inc="dv")
            P["st_free"] = tst
            P["bu_rd"] = tst_m[-1]

            if dbg and s == 0 and it == 0:
                P["st_free"] = B.dma("gpsimd", ap(DST, 0, [[TILE, 16], [1, TILE]]),
                      st_sb[:], after=[tst], inc="sto", serial=True)
            t_gy = []
            prev_ye = None
            prev_ge = None
            for mc in range(12):
                ty_m = B.op("tensor", lambda e, mc=mc: e.matmul(
                    ps_y[:], cmT[0:16, 128 * mc:128 * (mc + 1)], st_sb[:],
                    start=True, stop=True), after=[tst, P["y_free"]], inc="pe")
                ty_e = B.op("scalar", lambda e: e.copy(out=y_scr[:], in_=ps_y[:]),
                            after=[ty_m] + t_gy[-1:], inc="ac")
                P["y_free"] = ty_e
                pp = pd[mc % 2]
                guard = [t_gy[mc - 2]] if mc >= 2 else [t_data[10 + mc]]
                tg_m = []
                for kc in range(8):
                    tg_m.append(B.op("tensor", lambda e, mc=mc, kc=kc, pp=pp: e.matmul(
                        pp[:], Wp[:, 3072 * kc + 128 * (12 + mc):3072 * kc + 128 * (13 + mc)],
                        hfm[:, TILE * kc:TILE * (kc + 1)], start=(kc == 0), stop=False),
                        after=(guard if kc == 0 else []), inc="pe"))
                tg_m.append(B.op("tensor", lambda e, mc=mc, pp=pp: e.matmul(
                    pp[:], pb[0:1, 128 * (12 + mc):128 * (13 + mc)], ones[0:1, 0:TILE],
                    start=False, stop=True), inc="pe"))
                tg_e = B.op("scalar", lambda e, pp=pp: e.activation(
                    out=tg_scr[:], in_=pp[:], func=AF.Tanh, scale=0.5),
                    after=tg_m + t_gy[-1:], inc="ac")
                tgy = B.op("vector", lambda e, mc=mc: e.scalar_tensor_tensor(
                    gyb[:, TILE * mc:TILE * (mc + 1)], tg_scr[:], 1.0, y_scr[:],
                    ALU.add, ALU.mult), after=[tg_e, ty_e, P["gy_free"]], inc="dv")
                t_gy.append(tgy)
            P["hfm_free"] = tg_m[-1]
            P["pd_free"] = [t_gy[-2], t_gy[-1]]

            if dbg and s == 0 and it == 0:
                P["gy_free"] = B.dma("gpsimd", ap(DGY, 0, [[12 * TILE, 128], [1, 12 * TILE]]),
                      gyb[:], after=t_gy, inc="sto", serial=True)
            t_nx = []
            for si in range(NSUB):
                to_m = []
                for mc in range(12):
                    for ng in range(2):
                        to_m.append(B.op("tensor", lambda e, mc=mc, ng=ng, si=si: e.matmul(
                            ps_o[:, 512 * ng:512 * (ng + 1)],
                            gyb[:, TILE * mc + 128 * si:TILE * mc + 128 * (si + 1)],
                            Wo[:, 1024 * mc + 512 * ng:1024 * mc + 512 * (ng + 1)],
                            start=(mc == 0), stop=(mc == 11)),
                            after=(t_gy + [P["o_free"]] if mc == 0 else []), inc="pe"))
                tnx = B.op("vector", lambda e, si=si: e.tensor_add(
                    xho[:, 1024 * si:1024 * (si + 1)], xho[:, 1024 * si:1024 * (si + 1)],
                    ps_o[:]), after=to_m + [P.get("dbg_xho")], inc="dv")
                P["o_free"] = tnx
                tso = B.dma("gpsimd", ap(S1, (r0 + 128 * si) * 1024, [[1024, 128], [1, 1024]]),
                            xho[:, 1024 * si:1024 * (si + 1)], after=[tnx], inc="sh0")
                t_nx.append(tso)
            P["xho_free"] = t_nx[-1]
            P["gy_free"] = to_m[-1]

    B.wait_all("gpsimd", "sh0")
    B.wait_all("sync", "sto")
    B.flush()
    stk.close()


def ffn_attn(mk, W, l, tag, Sin, Sout, Hout, outer_ln):
    nc, B = mk.nc, mk.B
    TILE = 512
    NSUB = 4
    NT = SEQ // TILE
    NSEQ_L = NSEQ
    if "nt1" in mk.dbg:
        NT, NSEQ_L = 1, 1

    stk = contextlib.ExitStack()
    sb = lambda n, shp, dt: mk.sbuf(tag + n, shp, dt, stack=stk)
    idb = mk.identb

    wqkv = sb("_wqkv", (128, 8 * 1536), BF16)
    wo = sb("_wo", (128, 4 * 1024), BF16)
    csq = sb("_csq", (128, 2048), BF16)
    ssq = sb("_ssq", (128, 2048), BF16)
    csk = sb("_csk", (128, 2048), BF16)
    ssk = sb("_ssk", (128, 2048), BF16)
    mask_l = sb("_mskl", (128, 128), F32)
    mask_r = sb("_mskr", (128, 128), F32)

    sstk = contextlib.ExitStack()
    qs_row = mk.sbuf(tag + "_qsr", (1, 128), F32, stack=sstk)
    rep = mk.sbuf(tag + "_rep", (128, 128), F32, stack=sstk)
    col8 = mk.sbuf(tag + "_col8", (1, 128), F32, stack=sstk)
    tabf = mk.sbuf(tag + "_tabf", (128, 2048), F32, stack=sstk)
    ps_r = mk.psum(tag + "_psr", (128, 128), F32, sstk)

    for kc in range(8):
        B.dma("gpsimd", wqkv[:, 1536 * kc:1536 * (kc + 1)],
              ap(W["L0_ffn_wqkv"], l * 1024 * 1536 + kc * 128 * 1536,
                 [[1536, 128], [1, 1536]]), inc="wld", serial=True)
    for c in range(4):
        B.dma("gpsimd", wo[:, 1024 * c:1024 * (c + 1)],
              ap(W["L0_ffn_wo"], l * 512 * 1024 + c * 128 * 1024,
                 [[1024, 128], [1, 1024]]), inc="wld", serial=True)
    t_qs = B.dma("gpsimd", qs_row[0:1, 0:64],
                 ap(W["L0_ffn_q_scale"], l * 64, [[0, 1], [1, 64]]), inc="wld", serial=True)
    t_ks = B.dma("gpsimd", qs_row[0:1, 64:128],
                 ap(W["L0_ffn_k_scale"], l * 64, [[0, 1], [1, 64]]), inc="wld", serial=True)
    t_c8 = B.op("vector", lambda e: e.memset(col8[:], 8.0), inc="dv")
    t_r1 = B.op("tensor", lambda e: e.matmul(
        ps_r[:, 0:64], col8[0:1, :], qs_row[0:1, 0:64], start=True, stop=True),
        after=[t_qs, t_c8], inc="pe")
    t_r2 = B.op("tensor", lambda e: e.matmul(
        ps_r[:, 64:128], mk.ones[0:1, 0:128], qs_row[0:1, 64:128], start=True, stop=True),
        after=[t_ks], inc="pe")
    t_re = B.op("vector", lambda e: e.tensor_copy(rep[:], ps_r[:]),
                after=[t_r1, t_r2], inc="dv")

    tv_prev = None
    for name, tab, rcol, rot in (("csq", csq, 0, False), ("ssq", ssq, 0, True),
                                 ("csk", csk, 64, False), ("ssk", ssk, 64, True)):
        tl = B.dma("gpsimd", tabf[:], ap(W["tab_" + name], 0, [[2048, 128], [1, 2048]]),
                   after=[tv_prev], inc="wld", serial=True)
        if rot:
            rv = ap(rep, rcol + 32, [[128, 128], [0, 32], [-32, 2], [1, 32]])
        else:
            rv = ap(rep, rcol, [[128, 128], [0, 32], [1, 64]])
        tv_prev = B.op("vector", lambda e, tab=tab, rv=rv: e.tensor_tensor(
            tab[:].rearrange("p (j d) -> p j d", d=64),
            tabf[:].rearrange("p (j d) -> p j d", d=64), rv, ALU.mult),
            after=[tl, t_re], inc="dv")

    tm0 = B.op("gpsimd", lambda e: e.memset(mask_l[:], 0.0), inc="pl")
    B.op("gpsimd", lambda e: e.affine_select(
        out=mask_l[:], in_=mask_l[:], compare_op=ALU.is_ge, fill=-3e38, base=0,
        pattern=[[1, 128]], channel_multiplier=-1), after=[tm0], inc="pl")
    tm2 = B.op("gpsimd", lambda e: e.memset(mask_r[:], 0.0), inc="pl")
    B.op("gpsimd", lambda e: e.affine_select(
        out=mask_r[:], in_=mask_r[:], compare_op=ALU.is_ge, fill=-3e38, base=0,
        pattern=[[-1, 128]], channel_multiplier=1), after=[tm2], inc="pl")
    B.flush()
    sstk.close()

    xt = [sb(f"_xt{i}", (128, NSUB * 1024), F32) for i in range(2)]
    h0 = sb("_h0", (128, NSUB * 1024), F32)
    h0b = sb("_h0b", (128, NSUB * 1024), BF16)
    sh2b = sb("_sh2b", (128, NSUB * 512), BF16)
    carry_h = sb("_carh", (1, 512), BF16)
    carry_y = sb("_cary", (1, 512), F32)
    hs_fm = sb("_hsfm", (128, 8 * 512), BF16)
    qk_sb = sb("_qksb", (128, 8 * 512), BF16)
    v_tm = sb("_vtm", (128, 8 * 512), BF16)
    k_fm = sb("_kfm", (128, 4 * 1024), BF16)
    q_fm = sb("_qfm", (128, 4 * 512), BF16)
    qrot = sb("_qrot", (128, 1536), BF16)
    sq_scr = sb("_sqscr", (128, 512), F32)
    nrm = sb("_nrm", (128, 16), F32)
    nrmb = sb("_nrmb", (128, 8), BF16)
    sc_sb = sb("_scsb", (128, 2 * 640), BF16)
    an_sb = sb("_ansb", (128, 2 * 640), BF16)
    at_sb = sb("_atsb", (128, 2 * 640), BF16)
    mneg = sb("_mneg", (128, 4), F32)
    dens = sb("_dens", (128, 4), F32)
    o_sb = sb("_osb", (128, 4 * 512), BF16)
    y1 = sb("_y1", (128, NSUB * 1024), F32)
    sh2y = sb("_sh2y", (128, NSUB * 512), F32)
    stats = sb("_stats", (128, 24), F32)
    rr = sb("_rr", (128, 8), F32)

    ps_tr = mk.psum(tag + "_pstr", (128, 1024), BF16, stk)
    ps_A = mk.psum(tag + "_psA", (128, 512), F32, stk)
    ps_B = mk.psum(tag + "_psB", (128, 512), F32, stk)
    ps_E = mk.psum(tag + "_psE", (128, 128), F32, stk)
    ps_E2 = mk.psum(tag + "_psE2", (128, 128), F32, stk)
    ps_F = mk.psum(tag + "_psF", (128, 512), F32, stk)
    ps_G = mk.psum(tag + "_psG", (128, 1024), F32, stk)

    wtok = ("wld", B.sems["wld"][1])
    P = {
        "xt_free": [None, None], "h0_free": wtok, "h0b_free": wtok,
        "sh2b_free": wtok, "hs_free": wtok, "qk_free": wtok,
        "vslot": [wtok] * 8, "kslot": [wtok] * 8, "qfm_free": wtok,
        "qrot_free": wtok, "sq_free": wtok, "nrm_free": wtok, "nrmb_free": wtok,
        "sc_free": [wtok, wtok], "an_free": [wtok, wtok], "at_free": [wtok, wtok],
        "mn_free": wtok, "dn_free": wtok,
        "o_free": wtok, "y1_free": wtok, "sh2y_free": wtok, "h1b_free": wtok,
        "stats_free": wtok, "rr_free": wtok, "tr_free": wtok,
        "A_free": wtok, "B_free": wtok, "E_free": wtok, "F_free": wtok,
        "G_free": wtok, "carh_free": wtok, "cary_free": wtok,
        "krd": wtok, "vrd": wtok,
    }
    tvz = B.op("vector", lambda e: e.memset(v_tm[:], 0.0), inc="dv")
    P["vslot"] = [tvz] * 8

    LNE = NSUB * 1024

    for s in range(NSEQ_L):
        tch = B.op("vector", lambda e: e.memset(carry_h[:], 0.0),
                   after=[P["carh_free"]], inc="dv")
        tcy = B.op("vector", lambda e: e.memset(carry_y[:], 0.0),
                   after=[P["cary_free"]], inc="dv")
        P["carh_free"], P["cary_free"] = tch, tcy
        for it in range(NT):
            bi = it % 2
            x_t = xt[bi]
            r0 = s * SEQ + it * TILE
            t_ld = B.dma("sync", ap(x_t, 0, [[LNE, 128], [1024, NSUB], [1, 1024]]),
                         ap(Sin, r0 * 1024, [[1024, 128], [128 * 1024, NSUB], [1, 1024]]),
                         after=[P["xt_free"][bi]], inc=f"ld{bi}")

            t_h0 = []
            if outer_ln:
                tstats = []
                tpr = [t_ld, P["stats_free"]]
                for si in range(NSUB):
                    o = 1024 * si
                    b1 = B.op("vector", lambda e, o=o, x_t=x_t: e.bn_stats(
                        stats[:, 0:6], x_t[:, o:o + 512]), after=tpr, inc="dv")
                    b2 = B.op("vector", lambda e, o=o, x_t=x_t: e.bn_stats(
                        stats[:, 6:12], x_t[:, o + 512:o + 1024]), after=[b1], inc="dv")
                    tg = B.op("vector", lambda e, si=si: e.bn_aggr(
                        stats[:, 12 + 2 * si:14 + 2 * si], stats[:, 0:12]),
                        after=[b1, b2], inc="dv")
                    tstats.append(tg)
                    tpr = [t_ld, tg]
                tl1 = B.op("scalar", lambda e: e.activation(
                    out=rr[:, 0:NSUB], in_=ap(stats, 13, [[24, 128], [2, NSUB]]),
                    func=AF.Ln, bias=mk.cbias(1e-5), scale=1.0),
                    after=tstats + [P["rr_free"]], inc="ac")
                tr1 = B.op("scalar", lambda e: e.activation(
                    out=rr[:, 0:NSUB], in_=rr[:, 0:NSUB], func=AF.Exp, scale=-0.5),
                    after=[tl1], inc="ac")
                for si in range(NSUB):
                    o = 1024 * si
                    mu = ap(stats, 12 + 2 * si, [[24, 128], [1, 1]])
                    th1 = B.op("vector", lambda e, o=o, si=si, mu=mu, x_t=x_t: e.tensor_scalar(
                        h0[:, o:o + 1024], x_t[:, o:o + 1024], mu, rr[:, si:si + 1],
                        ALU.subtract, ALU.mult), after=[tr1, P["h0_free"]], inc="dv")
                    th2 = B.op("vector", lambda e, o=o: e.tensor_copy(
                        h0b[:, o:o + 1024], h0[:, o:o + 1024]),
                        after=[th1, P["h0b_free"]], inc="dv")
                    t_h0 += [th1, th2]
                P["stats_free"] = t_h0[-1]
                P["rr_free"] = t_h0[-1]
                P["xt_free"][bi] = t_h0[-1]
                h0_src = h0
            else:
                for si in range(NSUB):
                    o = 1024 * si
                    t_h0.append(B.op("vector", lambda e, o=o, x_t=x_t: e.tensor_copy(
                        h0b[:, o:o + 1024], x_t[:, o:o + 1024]),
                        after=[t_ld, P["h0b_free"]], inc="dv"))
                h0_src = x_t
            P["h0b_free"] = None

            if "fa1" in mk.dbg:
                continue
            shd = []
            shd.append(B.dma("gpsimd",
                       bass.AP(sh2b, NSUB * 512, [[NSUB * 512, 127], [512, NSUB], [1, 512]]),
                       bass.AP(h0b, 512, [[NSUB * 1024, 127], [1024, NSUB], [1, 512]]),
                       after=t_h0 + [P["sh2b_free"]], inc="sh0"))
            for si in range(1, NSUB):
                shd.append(B.dma("gpsimd",
                           bass.AP(sh2b, 512 * si, [[NSUB * 512, 1], [1, 512]]),
                           bass.AP(h0b, 127 * NSUB * 1024 + 1024 * (si - 1) + 512,
                                   [[NSUB * 1024, 1], [1, 512]]),
                           after=t_h0, inc="sh0"))
            shd.append(B.dma("gpsimd",
                       bass.AP(sh2b, 0, [[NSUB * 512, 1], [1, 512]]),
                       bass.AP(carry_h, 0, [[512, 1], [1, 512]]),
                       after=t_h0 + [P["carh_free"]], inc="sh0"))
            sh_tok = shd[-1]
            tch = B.dma("gpsimd", carry_h[:],
                        h0b[127:128, (NSUB - 1) * 1024 + 512:NSUB * 1024],
                        after=t_h0 + [sh_tok], inc="sh0")
            P["carh_free"] = tch

            t_hs = []
            prev_tt = [P["tr_free"]]
            for c in range(8):
                tts = []
                for si in range(NSUB):
                    if c < 4:
                        src = h0b[:, 1024 * si + 128 * c:1024 * si + 128 * (c + 1)]
                        dep = t_h0
                    else:
                        src = sh2b[:, 512 * si + 128 * (c - 4):512 * si + 128 * (c - 3)]
                        dep = [sh_tok]
                    tts.append(B.op("tensor", lambda e, src=src, si=si: e.transpose(
                        ps_tr[:, 128 * si:128 * (si + 1)], src, idb[:]),
                        after=dep + prev_tt, inc="pe"))
                te = B.op("vector", lambda e, c=c: e.tensor_copy(
                    hs_fm[:, 512 * c:512 * (c + 1)], ps_tr[0:128, 0:512]),
                    after=tts + [P["hs_free"]], inc="dv")
                t_hs.append(te)
                prev_tt = [te]
            P["tr_free"] = t_hs[-1]
            P["h0b_free"] = t_hs[-1]
            P["sh2b_free"] = t_hs[-1]

            if "fa2" in mk.dbg:
                continue
            pAB = [ps_A, ps_B]
            gAB = [P["A_free"], P["B_free"]]
            t_q, t_k, t_v = [None] * NSUB, [None] * NSUB, [None] * NSUB
            nmm = 0
            for si in range(NSUB):
                for part in range(3):
                    pp = pAB[nmm % 2]
                    gd = gAB[nmm % 2]
                    tms = []
                    for kc in range(8):
                        tms.append(B.op("tensor",
                            lambda e, si=si, part=part, kc=kc, pp=pp: e.matmul(
                                pp[:], hs_fm[:, 512 * kc + 128 * si:512 * kc + 128 * (si + 1)],
                                wqkv[:, 1536 * kc + 512 * part:1536 * kc + 512 * (part + 1)],
                                start=(kc == 0), stop=(kc == 7)),
                            after=(t_hs + [gd] if kc == 0 else []), inc="pe"))
                    if part < 2:
                        te = B.op("scalar", lambda e, si=si, part=part, pp=pp: e.copy(
                            out=qk_sb[:, 2048 * part + 512 * si:2048 * part + 512 * (si + 1)],
                            in_=pp[:]), after=tms + [P["qk_free"]], inc="ac")
                        (t_q if part == 0 else t_k)[si] = te
                    else:
                        slot = ((r0 // 128) + si) % 8
                        te = B.op("scalar", lambda e, pp=pp, slot=slot: e.copy(
                            out=v_tm[:, 512 * slot:512 * (slot + 1)], in_=pp[:]),
                            after=tms + [P["vslot"][slot], P["vrd"]], inc="ac")
                        P["vslot"][slot] = te
                        t_v[si] = te
                    gAB[nmm % 2] = te
                    nmm += 1
            P["A_free"], P["B_free"] = gAB
            P["qk_free"] = t_k[-1]

            if "fa3" in mk.dbg:
                continue
            t_qready = [None] * NSUB
            t_kready = [None] * NSUB
            for part, (cse, sse) in ((0, (csq, ssq)), (1, (csk, ssk))):
                for si in range(NSUB):
                    base = 2048 * part + 512 * si
                    src = qk_sb[:, base:base + 512]
                    dep = [t_q[si] if part == 0 else t_k[si]]
                    tsq = B.op("scalar", lambda e, src=src: e.activation(
                        out=sq_scr[:], in_=src, func=AF.Square),
                        after=dep + [P["sq_free"]], inc="ac")
                    trd = B.op("vector", lambda e: e.reduce_sum(
                        nrm[:, 0:8], sq_scr[:].rearrange("p (h d) -> p h d", d=64),
                        axis=AX.X), after=[tsq, P["nrm_free"]], inc="dv")
                    tmx = B.op("vector", lambda e: e.tensor_scalar(
                        nrm[:, 8:16], nrm[:, 0:8], 1e-24, None, ALU.max),
                        after=[trd], inc="dv")
                    tln = B.op("scalar", lambda e: e.activation(
                        out=nrm[:, 8:16], in_=nrm[:, 8:16], func=AF.Ln),
                        after=[tmx], inc="ac")
                    tex = B.op("scalar", lambda e: e.activation(
                        out=nrmb[:], in_=nrm[:, 8:16], func=AF.Exp, scale=-0.5),
                        after=[tln, P["nrmb_free"]], inc="ac")
                    P["sq_free"] = trd
                    tqn = B.op("vector", lambda e, src=src: e.tensor_tensor(
                        qrot[:, 0:512].rearrange("p (h d) -> p h d", d=64),
                        src.rearrange("p (h d) -> p h d", d=64),
                        ap(nrmb, 0, [[8, 128], [1, 8], [0, 64]]), ALU.mult),
                        after=[tex, P["qrot_free"]], inc="dv")
                    P["nrm_free"] = tln
                    P["nrmb_free"] = tqn
                    jj = ((r0 // 128) + si) % 32
                    t1 = B.op("vector", lambda e, jj=jj, cse=cse: e.tensor_tensor(
                        qrot[:, 512:1024].rearrange("p (h d) -> p h d", d=64),
                        qrot[:, 0:512].rearrange("p (h d) -> p h d", d=64),
                        ap(cse, 64 * jj, [[2048, 128], [0, 8], [1, 64]]), ALU.mult),
                        after=[tqn], inc="dv")
                    t2 = B.op("vector", lambda e, jj=jj, sse=sse: e.tensor_tensor(
                        qrot[:, 1024:1536].rearrange("p (h s d) -> p h s d", d=32, s=2),
                        ap(qrot, 32, [[1536, 128], [64, 8], [-32, 2], [1, 32]]),
                        ap(sse, 64 * jj, [[2048, 128], [0, 8], [32, 2], [1, 32]]), ALU.mult),
                        after=[tqn, t1], inc="dv")
                    t3 = B.op("vector", lambda e: e.tensor_add(
                        qrot[:, 0:512], qrot[:, 512:1024], qrot[:, 1024:1536]),
                        after=[t1, t2], inc="dv")
                    P["qrot_free"] = t3
                    tts = []
                    for c in range(4):
                        tts.append(B.op("tensor", lambda e, c=c: e.transpose(
                            ps_tr[:, 128 * c:128 * (c + 1)],
                            qrot[:, 128 * c:128 * (c + 1)], idb[:]),
                            after=[t3, P["tr_free"]], inc="pe"))
                    if part == 0:
                        tev = B.op("vector", lambda e, si=si: e.tensor_copy(
                            ap(q_fm, 128 * si, [[4 * 512, 128], [512, 4], [1, 128]]),
                            ps_tr[0:128, 0:512].rearrange("p (c d) -> p c d", d=128)),
                            after=tts + [P["qfm_free"]], inc="dv")
                        P["qfm_free"] = tev
                        t_qready[si] = tev
                    else:
                        slot = ((r0 // 128) + si) % 8
                        col = (128 * slot) % 1024
                        tev = B.op("vector", lambda e, col=col: e.tensor_copy(
                            ap(k_fm, col, [[4 * 1024, 128], [1024, 4], [1, 128]]),
                            ps_tr[0:128, 0:512].rearrange("p (c d) -> p c d", d=128)),
                            after=tts + [P["kslot"][slot], P["krd"]], inc="dv")
                        P["kslot"][slot] = tev
                        t_kready[si] = tev
                    P["tr_free"] = tev
            qall = t_qready[-1]

            if "fa4" in mk.dbg:
                continue
            alv = 9
            for f in mk.dbg:
                if f.startswith("alv"):
                    alv = int(f[3:])
            t_o_ev = [None] * 4
            for hp in range(4):
                for si in range(NSUB):
                    g0 = r0 + 128 * si
                    t_sc = {0: [], 1: []}
                    t_mz = []
                    for par in range(2):
                        pmain = pAB[par]
                        pb0 = 64 * par
                        gmain = P["A_free"] if par == 0 else P["B_free"]
                        for jj in range(5):
                            kc_tok = g0 - 512 + 128 * jj
                            slot = ((kc_tok // 128) % 8 + 8) % 8
                            dstp, dcol = (pmain, 128 * jj) if jj < 4 else ((ps_E if par == 0 else ps_E2), 0)
                            if kc_tok < s * SEQ:
                                t_mz.append(B.op("vector",
                                    lambda e, dstp=dstp, dcol=dcol: e.memset(
                                        dstp[:, dcol:dcol + 128], -1e30),
                                    after=[gmain, P["E_free"]], inc="dv"))
                            else:
                                t_sc[par].append((jj, slot))
                    sc_toks = []
                    for par in range(2):
                        pmain = pAB[par]
                        pb0 = 64 * par
                        gmain = P["A_free"] if par == 0 else P["B_free"]
                        for (jj, slot) in t_sc[par]:
                            dstp, dcol = (pmain, 128 * jj) if jj < 4 else ((ps_E if par == 0 else ps_E2), 0)
                            deps = [qall, t_kready[-1], gmain, P["E_free"]] + t_mz
                            sc_toks.append(B.op("tensor",
                                lambda e, hp=hp, si=si, slot=slot, dstp=dstp, dcol=dcol, pb0=pb0: e.matmul(
                                    dstp[:, dcol:dcol + 128],
                                    q_fm[pb0:pb0 + 64, 512 * hp + 128 * si:512 * hp + 128 * (si + 1)],
                                    k_fm[pb0:pb0 + 64, 1024 * hp + 128 * slot:1024 * hp + 128 * (slot + 1)],
                                    start=True, stop=True),
                                after=deps, inc="pe"))
                    done = sc_toks + t_mz
                    if alv < 2:
                        P["A_free"] = done
                        P["B_free"] = done
                        P["E_free"] = done
                        continue
                    head_toks = []
                    for par in range(2):
                        pmain = pAB[par]
                        tml = B.op("vector", lambda e, pmain=pmain, par=par: e.tensor_add(
                            sc_sb[:, 640 * par:640 * par + 128], pmain[:, 0:128],
                            mask_l[:]), after=done + [P["sc_free"][par]], inc="dv")
                        pse = ps_E if par == 0 else ps_E2
                        tmr = B.op("vector", lambda e, par=par, pse=pse: e.tensor_add(
                            sc_sb[:, 640 * par + 512:640 * par + 640],
                            pse[:, 0:128], mask_r[:]),
                            after=done + [P["sc_free"][par]], inc="dv")
                        te1 = B.op("scalar", lambda e, pmain=pmain, par=par: e.copy(
                            out=sc_sb[:, 640 * par + 128:640 * par + 512],
                            in_=pmain[:, 128:512]),
                            after=done + [tml, tmr, P["sc_free"][par]], inc="ac")
                        if par == 0:
                            P["A_free"] = [te1, tml]
                        else:
                            P["B_free"] = [te1, tml]
                        P["E_free"] = tmr
                        if alv < 3:
                            head_toks.append(tmr)
                            continue
                        tmax = B.op("vector", lambda e, par=par: e.reduce_max(
                            mneg[:, par:par + 1], sc_sb[:, 640 * par:640 * (par + 1)],
                            axis=AX.X, negate=True),
                            after=[te1, tml, tmr, P["mn_free"]], inc="dv")
                        if alv < 4:
                            head_toks.append(tmax)
                            continue
                        texp = B.op("scalar", lambda e, par=par: e.activation(
                            out=an_sb[:, 640 * par:640 * (par + 1)],
                            in_=sc_sb[:, 640 * par:640 * (par + 1)], func=AF.Exp,
                            bias=mneg[:, par:par + 1],
                            accum_out=dens[:, par:par + 1]),
                            after=[tmax, P["an_free"][par], P["dn_free"]], inc="ac")
                        P["sc_free"][par] = texp
                        if alv < 5:
                            head_toks.append(texp)
                            P["mn_free"] = texp
                            continue
                        trc = B.op("vector", lambda e, par=par: e.reciprocal(
                            dens[:, 2 + par:3 + par], dens[:, par:par + 1]),
                            after=[texp], inc="dv")
                        tnr = B.op("vector", lambda e, par=par: e.tensor_scalar(
                            an_sb[:, 640 * par:640 * (par + 1)],
                            an_sb[:, 640 * par:640 * (par + 1)],
                            dens[:, 2 + par:3 + par], None, ALU.mult),
                            after=[trc], inc="dv")
                        head_toks.append(tnr)
                        P["mn_free"] = texp
                        P["dn_free"] = tnr
                    if alv < 6:
                        continue
                    for par in range(2):
                        tnr = head_toks[par]
                        tts = []
                        for jj in range(5):
                            tts.append(B.op("tensor", lambda e, par=par, jj=jj: e.transpose(
                                ps_tr[:, 128 * jj:128 * (jj + 1)],
                                an_sb[:, 640 * par + 128 * jj:640 * par + 128 * (jj + 1)],
                                idb[:]), after=[tnr, P["tr_free"]], inc="pe"))
                        tat = B.op("vector", lambda e, par=par: e.tensor_copy(
                            at_sb[:, 640 * par:640 * (par + 1)], ps_tr[0:128, 0:640]),
                            after=tts + [P["at_free"][par]], inc="dv")
                        P["tr_free"] = tat
                        P["an_free"][par] = tat
                        if alv < 7:
                            continue
                        h = 2 * hp + par
                        av = []
                        for jj in range(5):
                            kc_tok = g0 - 512 + 128 * jj
                            slot = ((kc_tok // 128) % 8 + 8) % 8
                            av.append(B.op("tensor",
                                lambda e, h=h, par=par, jj=jj, slot=slot, si=si: e.matmul(
                                    ps_F[64 * par:64 * par + 64, 128 * si:128 * (si + 1)],
                                    v_tm[:, 512 * slot + 64 * h:512 * slot + 64 * (h + 1)],
                                    at_sb[:, 640 * par + 128 * jj:640 * par + 128 * (jj + 1)],
                                    start=(jj == 0), stop=(jj == 4)),
                                after=[tat] + ([P["F_free"]] if jj == 0 else []), inc="pe"))
                        P["at_free"][par] = av[-1]
                        if si == NSUB - 1:
                            t_o_ev[hp] = av[-1]
                if alv < 8:
                    continue
                tev = B.op("vector", lambda e, hp=hp: e.tensor_copy(
                    o_sb[:, 512 * hp:512 * (hp + 1)], ps_F[:]),
                    after=[t_o_ev[hp], P["o_free"]], inc="dv")
                P["F_free"] = tev
                P["o_free"] = tev
                t_o_ev[hp] = tev

            if sc_toks:
                P["krd"] = sc_toks[-1]
            if alv >= 7:
                P["vrd"] = av[-1]

            if "fa5" in mk.dbg or alv < 9:
                continue
            t_y1 = []
            for si in range(NSUB):
                toms = []
                for hp in range(4):
                    for ng in range(2):
                        toms.append(B.op("tensor",
                            lambda e, hp=hp, ng=ng, si=si: e.matmul(
                                ps_G[:, 512 * ng:512 * (ng + 1)],
                                o_sb[:, 512 * hp + 128 * si:512 * hp + 128 * (si + 1)],
                                wo[:, 1024 * hp + 512 * ng:1024 * hp + 512 * (ng + 1)],
                                start=(hp == 0), stop=(hp == 3)),
                            after=(t_o_ev + [P["G_free"]] if hp == 0 else []), inc="pe"))
                ty = B.op("vector", lambda e, si=si, h0_src=h0_src: e.tensor_add(
                    y1[:, 1024 * si:1024 * (si + 1)], h0_src[:, 1024 * si:1024 * (si + 1)],
                    ps_G[:]), after=toms + [P["y1_free"]], inc="dv")
                P["G_free"] = ty
                t_y1.append(ty)
            P["h0_free"] = t_y1[-1]
            if not outer_ln:
                P["xt_free"][bi] = t_y1[-1]
            tso = B.dma("sync", ap(Sout, r0 * 1024, [[1024, 128], [128 * 1024, NSUB], [1, 1024]]),
                        ap(y1, 0, [[LNE, 128], [1024, NSUB], [1, 1024]]),
                        after=t_y1, inc="st")

            shy = []
            shy.append(B.dma("gpsimd",
                       bass.AP(sh2y, NSUB * 512, [[NSUB * 512, 127], [512, NSUB], [1, 512]]),
                       bass.AP(y1, 512, [[NSUB * 1024, 127], [1024, NSUB], [1, 512]]),
                       after=t_y1 + [P["sh2y_free"]], inc="sh1"))
            for si in range(1, NSUB):
                shy.append(B.dma("gpsimd",
                           bass.AP(sh2y, 512 * si, [[NSUB * 512, 1], [1, 512]]),
                           bass.AP(y1, 127 * NSUB * 1024 + 1024 * (si - 1) + 512,
                                   [[NSUB * 1024, 1], [1, 512]]),
                           after=t_y1, inc="sh1"))
            shy.append(B.dma("gpsimd",
                       bass.AP(sh2y, 0, [[NSUB * 512, 1], [1, 512]]),
                       bass.AP(carry_y, 0, [[512, 1], [1, 512]]),
                       after=t_y1 + [P["cary_free"]], inc="sh1"))
            shy_tok = shy[-1]
            tcy = B.dma("gpsimd", carry_y[:],
                        y1[127:128, (NSUB - 1) * 1024 + 512:NSUB * 1024],
                        after=t_y1 + [shy_tok], inc="sh1")
            P["cary_free"] = tcy

            tstats2 = []
            tpr = [shy_tok, P["stats_free"]]
            for si in range(NSUB):
                b1 = B.op("vector", lambda e, si=si: e.bn_stats(
                    stats[:, 0:6], y1[:, 1024 * si:1024 * si + 512]),
                    after=tpr + t_y1, inc="dv")
                b2 = B.op("vector", lambda e, si=si: e.bn_stats(
                    stats[:, 6:12], sh2y[:, 512 * si:512 * (si + 1)]),
                    after=[b1, shy_tok], inc="dv")
                tg = B.op("vector", lambda e, si=si: e.bn_aggr(
                    stats[:, 12 + 2 * si:14 + 2 * si], stats[:, 0:12]),
                    after=[b1, b2], inc="dv")
                tstats2.append(tg)
                tpr = [tg]
            tl2 = B.op("scalar", lambda e: e.activation(
                out=rr[:, 4:4 + NSUB], in_=ap(stats, 13, [[24, 128], [2, NSUB]]),
                func=AF.Ln, bias=mk.cbias(1e-5), scale=1.0),
                after=tstats2 + [P["rr_free"]], inc="ac")
            tr2 = B.op("scalar", lambda e: e.activation(
                out=rr[:, 4:4 + NSUB], in_=rr[:, 4:4 + NSUB], func=AF.Exp, scale=-0.5),
                after=[tl2], inc="ac")
            t_h1 = []
            for si in range(NSUB):
                mu = ap(stats, 12 + 2 * si, [[24, 128], [1, 1]])
                ta = B.op("vector", lambda e, si=si, mu=mu: e.tensor_scalar(
                    h0b[:, 1024 * si:1024 * si + 512], y1[:, 1024 * si:1024 * si + 512],
                    mu, rr[:, 4 + si:5 + si], ALU.subtract, ALU.mult),
                    after=[tr2, P["h1b_free"], P["h0b_free"], P["carh_free"]], inc="dv")
                tb = B.op("vector", lambda e, si=si, mu=mu: e.tensor_scalar(
                    h0b[:, 1024 * si + 512:1024 * (si + 1)], sh2y[:, 512 * si:512 * (si + 1)],
                    mu, rr[:, 4 + si:5 + si], ALU.subtract, ALU.mult),
                    after=[tr2], inc="dv")
                t_h1 += [ta, tb]
            P["stats_free"] = t_h1[-1]
            P["rr_free"] = t_h1[-1]
            P["sh2y_free"] = t_h1[-1]
            P["y1_free"] = (t_h1[-1], tso)[0]
            tso2 = B.dma("sync", ap(Hout, r0 * 1024, [[1024, 128], [128 * 1024, NSUB], [1, 1024]]),
                         ap(h0b, 0, [[LNE, 128], [1024, NSUB], [1, 1024]]),
                         after=t_h1, inc="sto")
            P["h1b_free"] = tso2
            P["h0b_free"] = tso2
            P["y1_free"] = tso

    B.wait_all("sync", "st")
    B.wait_all("sync", "sto")
    B.wait_all("gpsimd", "sh0")
    B.wait_all("gpsimd", "sh1")
    B.flush()
    stk.close()


def ffn_mlp(mk, W, l, tag, Sin, Hin, Sout, extra):
    nc, B = mk.nc, mk.B
    TILE = 256
    NSUB = 2
    NT = SEQ // TILE
    NSEQ_L = NSEQ
    if "nt1" in mk.dbg:
        NT, NSEQ_L = 1, 1
    HID = 2730
    NCH = 22

    stk = contextlib.ExitStack()
    sb = lambda n, shp, dt: mk.sbuf(tag + n, shp, dt, stack=stk)
    idb, ones = mk.identb, mk.ones

    w1p = sb("_w1p", (128, 8 * 5460), BF16)
    w2a = sb("_w2a", (128, NCH * 1024), BF16)
    b1c = sb("_b1c", (128, 2 * NCH), F32)

    sstk = contextlib.ExitStack()
    wtmp = mk.sbuf(tag + "_wtmp", (128, 5460), F32, stack=sstk)
    lg_r = mk.sbuf(tag + "_lgr", (1, 1024), F32, stack=sstk)
    lb_r = mk.sbuf(tag + "_lbr", (1, 1024), F32, stack=sstk)
    lg_c = mk.sbuf(tag + "_lgc", (128, 8), F32, stack=sstk)
    lb_c = mk.sbuf(tag + "_lbc", (128, 8), F32, stack=sstk)
    b1row = mk.sbuf(tag + "_b1row", (1, 5504), F32, stack=sstk)
    ps_a = mk.psum(tag + "_ps_a", (128, 512), F32, sstk)
    ps_b = mk.psum(tag + "_ps_b", (128, 512), F32, sstk)
    ps_c = mk.psum(tag + "_ps_c", (1, 512), F32, sstk)

    idf = mk.identf
    tlg = B.dma("gpsimd", lg_r[:], ap(W["L0_ffn_ln_g"], l * 1024, [[0, 1], [1, 1024]]),
                inc="wld", serial=True)
    tlb = B.dma("gpsimd", lb_r[:], ap(W["L0_ffn_ln_b"], l * 1024, [[0, 1], [1, 1024]]),
                inc="wld", serial=True)
    ttr = []
    for c in range(8):
        ttr.append(B.op("tensor", lambda e, c=c: e.transpose(
            ps_a[:, c:c + 1], lg_r[0:1, 128 * c:128 * (c + 1)], idf[0:1, 0:1]),
            after=[tlg], inc="pe"))
        ttr.append(B.op("tensor", lambda e, c=c: e.transpose(
            ps_b[:, c:c + 1], lb_r[0:1, 128 * c:128 * (c + 1)], idf[0:1, 0:1]),
            after=[tlb], inc="pe"))
    tw = B.op("vector", lambda e: e.tensor_copy(lg_c[:], ps_a[:, 0:8]), after=ttr, inc="dv")
    tb = B.op("vector", lambda e: e.tensor_copy(lb_c[:], ps_b[:, 0:8]), after=ttr, inc="dv")
    tb1 = B.dma("gpsimd", b1row[0:1, 0:5460],
                ap(W["L0_ffn_b1"], l * 5460, [[0, 1], [1, 5460]]), inc="wld", serial=True)
    prev_k = [tw, tb, tb1]
    for kc in range(8):
        tl = B.dma("gpsimd", wtmp[:],
                   ap(W["L0_ffn_w1"], l * 1024 * 5460 + kc * 128 * 5460,
                      [[5460, 128], [1, 5460]]), after=prev_k, inc="wld", serial=True)
        tv = B.op("vector", lambda e, kc=kc: e.tensor_scalar(
            w1p[:, 5460 * kc:5460 * (kc + 1)], wtmp[:], lg_c[:, kc:kc + 1], None,
            ALU.mult), after=[tl, tw], inc="dv")
        cur = [tv]
        for ng in range(11):
            w = min(512, 5460 - 512 * ng)
            tm = B.op("tensor", lambda e, kc=kc, ng=ng, w=w: e.matmul(
                ps_c[0:1, 0:w], lb_c[:, kc:kc + 1],
                wtmp[:, 512 * ng:512 * ng + w], start=True, stop=True),
                after=[tl, tb] + (cur[-1:] if ng > 0 else []), inc="pe")
            ta = B.op("vector", lambda e, ng=ng, w=w: e.tensor_add(
                b1row[0:1, 512 * ng:512 * ng + w], b1row[0:1, 512 * ng:512 * ng + w],
                ps_c[0:1, 0:w]), after=[tm], inc="dv")
            cur.append(ta)
        prev_k = cur
    tcol = []
    for h in range(2):
        for mc in range(NCH):
            off = 2730 * h + 128 * mc
            w = min(128, 2730 - 128 * mc)
            tt = B.op("tensor", lambda e, off=off, w=w: e.transpose(
                ps_a[0:w, 20:21], b1row[0:1, off:off + w], idf[0:1, 0:1]),
                after=prev_k + tcol[-1:], inc="pe")
            tv = B.op("vector", lambda e, h=h, mc=mc, w=w: e.tensor_copy(
                b1c[0:w, NCH * h + mc:NCH * h + mc + 1], ps_a[0:w, 20:21]),
                after=[tt], inc="dv")
            tcol.append(tv)
    tz = B.op("vector", lambda e: e.memset(w2a[:, (NCH - 1) * 1024:NCH * 1024], 0.0), inc="dv")
    for kc in range(NCH):
        rows = min(128, HID - 128 * kc)
        B.dma("gpsimd", w2a[0:rows, 1024 * kc:1024 * (kc + 1)],
              ap(W["L0_ffn_w2"], l * HID * 1024 + kc * 128 * 1024,
                 [[1024, rows], [1, 1024]]),
              after=[tz] if kc == NCH - 1 else [], inc="wld", serial=True)
    B.dma("gpsimd", w2a[42:43, (NCH - 1) * 1024:NCH * 1024],
          ap(W["L0_ffn_b2"], l * 1024, [[0, 1], [1, 1024]]),
          after=[tz], inc="wld", serial=True)
    B.flush()
    sstk.close()

    hb = [sb(f"_hb{i}", (128, NSUB * 1024), BF16) for i in range(2)]
    yb = [sb(f"_yb{i}", (128, NSUB * 1024), F32) for i in range(2)]
    xb = [sb(f"_xb{i}", (128, NSUB * 1024), F32) for i in range(2)] if extra is not None else None
    hfm = sb("_hfm", (128, 8 * TILE), BF16)
    gel = sb("_gel", (128, TILE), F32)
    agb = sb("_agb", (128, NCH * TILE), BF16)
    y2 = sb("_y2", (128, NSUB * 1024), F32)

    ps_tr = mk.psum(tag + "_pstr", (128, TILE), BF16, stk)
    ps_g0 = mk.psum(tag + "_psg0", (128, TILE), F32, stk)
    ps_g1 = mk.psum(tag + "_psg1", (128, TILE), F32, stk)
    ps_a0 = mk.psum(tag + "_psa0", (128, TILE), F32, stk)
    ps_a1 = mk.psum(tag + "_psa1", (128, TILE), F32, stk)
    ps_o = mk.psum(tag + "_pso", (128, 1024), F32, stk)
    pg = [ps_g0, ps_g1]
    pa = [ps_a0, ps_a1]

    wtok = ("wld", B.sems["wld"][1])
    P = {
        "h_free": [None, None], "y_free": [None, None], "x_free": [None, None],
        "hfm_free": wtok, "tr_free": wtok, "gel_free": wtok,
        "pg_free": [wtok, wtok], "pa_free": [wtok, wtok],
        "ag_free": wtok, "o_free": wtok, "y2_free": wtok,
    }
    tz1 = B.op("vector", lambda e: e.memset(
        agb[:, (NCH - 1) * TILE:NCH * TILE], 0.0), inc="dv")
    tz2 = B.dma("gpsimd", agb[42:43, (NCH - 1) * TILE:NCH * TILE],
                mk.ones[0:1, 0:TILE], after=[tz1], inc="wld", serial=True)
    P["ag_free"] = tz2

    LNE = NSUB * 1024

    for s in range(NSEQ_L):
        for it in range(NT):
            bi = it % 2
            h_t, y_t = hb[bi], yb[bi]
            r0 = s * SEQ + it * TILE
            t_lh = B.dma("sync", ap(h_t, 0, [[LNE, 128], [1024, NSUB], [1, 1024]]),
                         ap(Hin, r0 * 1024, [[1024, 128], [128 * 1024, NSUB], [1, 1024]]),
                         after=[P["h_free"][bi]], inc=f"ld{bi}")
            t_ly = B.dma("sync", ap(y_t, 0, [[LNE, 128], [1024, NSUB], [1, 1024]]),
                         ap(Sin, r0 * 1024, [[1024, 128], [128 * 1024, NSUB], [1, 1024]]),
                         after=[P["y_free"][bi]], inc=f"ld{bi}")
            if extra is not None:
                x_t = xb[bi]
                t_lx = B.dma("sync", ap(x_t, 0, [[LNE, 128], [1024, NSUB], [1, 1024]]),
                             ap(extra, r0 * 1024, [[1024, 128], [128 * 1024, NSUB], [1, 1024]]),
                             after=[P["x_free"][bi]], inc=f"ld{bi}")
            ld_tok = (f"ld{bi}", B.sems[f"ld{bi}"][1])

            t_hs = []
            prev_tt = [P["tr_free"]]
            for c in range(8):
                tts = []
                for si in range(NSUB):
                    tts.append(B.op("tensor", lambda e, c=c, si=si, h_t=h_t: e.transpose(
                        ps_tr[:, 128 * si:128 * (si + 1)],
                        h_t[:, 1024 * si + 128 * c:1024 * si + 128 * (c + 1)], idb[:]),
                        after=[ld_tok] + prev_tt, inc="pe"))
                te = B.op("vector", lambda e, c=c: e.tensor_copy(
                    hfm[:, TILE * c:TILE * (c + 1)], ps_tr[0:128, 0:TILE]),
                    after=tts + [P["hfm_free"]], inc="dv")
                t_hs.append(te)
                prev_tt = [te]
            P["tr_free"] = t_hs[-1]
            P["h_free"][bi] = t_hs[-1]

            t_ag = []
            for mc in range(NCH):
                wdt = min(128, HID - 128 * mc)
                ppg, ppa = pg[mc % 2], pa[mc % 2]
                gg, ga = P["pg_free"][mc % 2], P["pa_free"][mc % 2]
                tg_m = []
                ta_m = []
                for kc in range(8):
                    tg_m.append(B.op("tensor",
                        lambda e, mc=mc, kc=kc, ppg=ppg, wdt=wdt: e.matmul(
                            ppg[0:wdt, :],
                            w1p[:, 5460 * kc + 2730 + 128 * mc:5460 * kc + 2730 + 128 * mc + wdt],
                            hfm[:, TILE * kc:TILE * (kc + 1)],
                            start=(kc == 0), stop=(kc == 7)),
                        after=(t_hs + [gg] if kc == 0 else []), inc="pe"))
                for kc in range(8):
                    ta_m.append(B.op("tensor",
                        lambda e, mc=mc, kc=kc, ppa=ppa, wdt=wdt: e.matmul(
                            ppa[0:wdt, :],
                            w1p[:, 5460 * kc + 128 * mc:5460 * kc + 128 * mc + wdt],
                            hfm[:, TILE * kc:TILE * (kc + 1)],
                            start=(kc == 0), stop=(kc == 7)),
                        after=(t_hs + [ga] if kc == 0 else []), inc="pe"))
                tge = B.op("scalar", lambda e, mc=mc, ppg=ppg, wdt=wdt: e.activation(
                    out=gel[0:wdt, :], in_=ppg[0:wdt, :], func=AF.Gelu,
                    bias=b1c[0:wdt, NCH + mc:NCH + mc + 1]),
                    after=tg_m + [P["gel_free"]], inc="ac")
                P["pg_free"][mc % 2] = tge
                tag = B.op("vector", lambda e, mc=mc, ppa=ppa, wdt=wdt: e.scalar_tensor_tensor(
                    agb[0:wdt, TILE * mc:TILE * (mc + 1)], ppa[0:wdt, :],
                    b1c[0:wdt, mc:mc + 1], gel[0:wdt, :], ALU.add, ALU.mult),
                    after=ta_m + [tge, P["ag_free"]], inc="dv")
                P["pa_free"][mc % 2] = tag
                P["gel_free"] = tag
                t_ag.append(tag)
            P["hfm_free"] = ta_m[-1]

            t_y2 = []
            for si in range(NSUB):
                toms = []
                for kc in range(NCH):
                    rows = min(128, HID - 128 * kc)
                    if kc == NCH - 1:
                        rows = 43
                    for ng in range(2):
                        toms.append(B.op("tensor",
                            lambda e, kc=kc, ng=ng, si=si, rows=rows: e.matmul(
                                ps_o[:, 512 * ng:512 * (ng + 1)],
                                agb[0:rows, TILE * kc + 128 * si:TILE * kc + 128 * (si + 1)],
                                w2a[0:rows, 1024 * kc + 512 * ng:1024 * kc + 512 * (ng + 1)],
                                start=(kc == 0), stop=(kc == NCH - 1)),
                            after=(t_ag + [P["o_free"]] if kc == 0 else []), inc="pe"))
                ty = B.op("vector", lambda e, si=si, y_t=y_t: e.tensor_add(
                    y2[:, 1024 * si:1024 * (si + 1)], y_t[:, 1024 * si:1024 * (si + 1)],
                    ps_o[:]), after=toms + [ld_tok, P["y2_free"]], inc="dv")
                if extra is not None:
                    ty = B.op("vector", lambda e, si=si, x_t=x_t: e.tensor_add(
                        y2[:, 1024 * si:1024 * (si + 1)], y2[:, 1024 * si:1024 * (si + 1)],
                        x_t[:, 1024 * si:1024 * (si + 1)]), after=[ty], inc="dv")
                P["o_free"] = ty
                t_y2.append(ty)
            P["y_free"][bi] = t_y2[-1]
            if extra is not None:
                P["x_free"][bi] = t_y2[-1]
            P["ag_free"] = toms[-1]
            tso = B.dma("sync", ap(Sout, r0 * 1024, [[1024, 128], [128 * 1024, NSUB], [1, 1024]]),
                        ap(y2, 0, [[LNE, 128], [1024, NSUB], [1, 1024]]),
                        after=t_y2, inc="st")
            P["y2_free"] = tso

    B.wait_all("sync", "st")
    B.flush()
    stk.close()


def attn_b_setup(mk, W, tag, stk, need_wqkv):
    nc, B = mk.nc, mk.B
    idf = mk.identf
    out = {}
    if need_wqkv:
        out["wqkvT"] = mk.sbuf(tag + "_wqkvT", (128, 8 * 3072), BF16, stack=stk)
    out["wbd"] = mk.sbuf(tag + "_wbd", (128, 8 * 512), BF16, stack=stk)
    sstk = contextlib.ExitStack()
    wtmp = mk.sbuf(tag + "_wtmp", (128, 1024), F32, stack=sstk)
    hsb = mk.sbuf(tag + "_hsb", (64, 64), F32, stack=sstk)
    ptb = mk.sbuf(tag + "_ptb", (128, 128), BF16, stack=sstk)
    sgc = mk.sbuf(tag + "_sgc", (64, 192), F32, stack=sstk)
    w0 = mk.sbuf(tag + "_w0", (64, 64), F32, stack=sstk)
    l1 = mk.sbuf(tag + "_l1", (64, 64), F32, stack=sstk)
    wsc = mk.sbuf(tag + "_wscs", (64, 64), F32, stack=sstk)
    grow = mk.sbuf(tag + "_grow", (1, 512), F32, stack=sstk)
    gcol = mk.sbuf(tag + "_gcol", (1, 128), F32, stack=sstk)
    wbf = mk.sbuf(tag + "_wbf", (128, 512), F32, stack=sstk)
    wpre = mk.sbuf(tag + "_wpre", (128, 512), BF16, stack=sstk)
    ps_a = mk.psum(tag + "_psa", (128, 512), F32, sstk)
    ps_b = mk.psum(tag + "_psb", (128, 512), F32, sstk)
    ps_c = mk.psum(tag + "_psc", (128, 512), F32, sstk)

    if need_wqkv:
        wq = out["wqkvT"]
        prev = [None]
        for r in range(24):
            tl = B.dma("gpsimd", wtmp[:],
                       ap(W["L0_attn_b_w_qkv"], r * 128 * 1024, [[1024, 128], [1, 1024]]),
                       after=prev, inc="wld", serial=True)
            prev = []
            for c in range(8):
                tt = B.op("tensor", lambda e, c=c: e.transpose(
                    ps_a[:, 128 * (c % 4):128 * (c % 4) + 128],
                    wtmp[:, 128 * c:128 * (c + 1)], idf[:]),
                    after=[tl] + prev[-1:], inc="pe")
                if r < 16:
                    dst = ap(wq, 3072 * c + 128 * r,
                             [[8 * 3072, 128], [64, 2], [32, 2], [1, 32]])
                    src = ap(ps_a, 128 * (c % 4), [[512, 128], [64, 2], [1, 2], [2, 32]])
                    tv = B.op("vector", lambda e, dst=dst, src=src: e.tensor_copy(dst, src),
                              after=[tt], inc="dv")
                else:
                    tv = B.op("vector", lambda e, c=c, r=r: e.tensor_copy(
                        wq[:, 3072 * c + 128 * r:3072 * c + 128 * (r + 1)],
                        ps_a[:, 128 * (c % 4):128 * (c % 4) + 128]), after=[tt], inc="dv")
                prev.append(tv)

    wbd = out["wbd"]
    th = B.dma("gpsimd", hsb[:], ap(W["had"], 0, [[64, 64], [1, 64]]),
               inc="wld", serial=True)
    tpt = B.dma("gpsimd", ptb[:], ap(W["permT"], 0, [[128, 128], [1, 128]]),
                inc="wld", serial=True)
    tgc = B.op("vector", lambda e: e.memset(gcol[:], 1.0 / 4096.0), inc="dv")
    sstk2 = contextlib.ExitStack()
    sgn2 = mk.sbuf(tag + "_sgn2", (128, 64), F32, stack=sstk2)
    ts1b = B.dma("gpsimd", sgn2[:],
                 ap(W["L0_attn_b_sorf_signs"], 0, [[64, 128], [1, 64]]),
                 inc="wld", serial=True)
    last_wq = prev if need_wqkv else [None]
    tt1 = B.op("tensor", lambda e: e.transpose(ps_a[0:64, 0:128], sgn2[:], idf[:]),
               after=[ts1b] + last_wq, inc="pe")
    tv1 = B.op("vector", lambda e: e.tensor_copy(sgc[0:64, 0:128], ps_a[0:64, 0:128]),
               after=[tt1], inc="dv")
    ts2b = B.dma("gpsimd", sgn2[0:64, :],
                 ap(W["L0_attn_b_sorf_signs"], 128 * 64, [[64, 64], [1, 64]]),
                 after=[tv1], inc="wld", serial=True)
    tt2 = B.op("tensor", lambda e: e.transpose(ps_a[0:64, 128:192], sgn2[0:64, :],
               idf[0:64, 0:64]), after=[ts2b, tv1], inc="pe")
    tv2 = B.op("vector", lambda e: e.tensor_copy(sgc[0:64, 128:192], ps_a[0:64, 128:192]),
               after=[tt2], inc="dv")
    sg_ready = [tv1, tv2]

    prev_chunk = [None]
    for c in range(8):
        tzw = B.op("vector", lambda e: e.memset(wbf[:], 0.0), after=prev_chunk, inc="dv")
        tgl = B.dma("gpsimd", grow[:],
                    ap(W["L0_attn_b_sorf_scales"], 512 * c, [[0, 1], [1, 512]]),
                    after=prev_chunk, inc="wld", serial=True)
        ev_toks = []
        for par in range(2):
            h = 2 * c + par
            for r in range(4):
                col = 12 * h + 3 * r
                t0 = B.op("vector", lambda e, col=col: e.tensor_scalar(
                    w0[:], hsb[:], sgc[0:64, col:col + 1], None, ALU.mult),
                    after=sg_ready + [th] + ev_toks[-1:], inc="dv")
                tT = B.op("tensor", lambda e: e.transpose(
                    ps_b[0:64, 0:64], w0[:], idf[0:64, 0:64]), after=[t0], inc="pe")
                tl1 = B.op("vector", lambda e, col=col: e.tensor_scalar(
                    l1[:], ps_b[0:64, 0:64], sgc[0:64, col + 1:col + 2], None, ALU.mult),
                    after=[tT], inc="dv")
                tm1 = B.op("tensor", lambda e: e.matmul(
                    ps_b[0:64, 64:128], l1[:], hsb[:], start=True, stop=True),
                    after=[tl1], inc="pe")
                te1 = B.op("vector", lambda e: e.tensor_copy(wsc[:], ps_b[0:64, 64:128]),
                           after=[tm1], inc="dv")
                tT2 = B.op("tensor", lambda e: e.transpose(
                    ps_b[0:64, 128:192], wsc[:], idf[0:64, 0:64]), after=[te1], inc="pe")
                tl2 = B.op("vector", lambda e, col=col: e.tensor_scalar(
                    l1[:], ps_b[0:64, 128:192], sgc[0:64, col + 2:col + 3], None, ALU.mult),
                    after=[tT2], inc="dv")
                tm2 = B.op("tensor", lambda e: e.matmul(
                    ps_b[0:64, 192:256], l1[:], hsb[:], start=True, stop=True),
                    after=[tl2], inc="pe")
                tev = B.op("vector", lambda e, par=par, r=r: e.tensor_copy(
                    wbf[64 * par:64 * par + 64, 256 * par + 64 * r:256 * par + 64 * (r + 1)],
                    ps_b[0:64, 192:256]), after=[tm2, tzw], inc="dv")
                ev_toks.append(tev)
        tg = B.op("tensor", lambda e: e.matmul(
            ps_a[:, 0:512], gcol[0:1, :], grow[0:1, :], start=True, stop=True),
            after=[tgl, tgc] + ev_toks, inc="pe")
        tmu = B.op("vector", lambda e: e.tensor_mul(wpre[:], wbf[:], ps_a[:, 0:512]),
                   after=[tg] + ev_toks, inc="dv")
        tpm = B.op("tensor", lambda e: e.matmul(
            ps_c[:, 0:512], ptb[:], wpre[:], start=True, stop=True),
            after=[tmu, tpt], inc="pe")
        tevc = B.op("vector", lambda e, c=c: e.tensor_copy(
            wbd[:, 512 * c:512 * (c + 1)], ps_c[:, 0:512]), after=[tpm], inc="dv")
        prev_chunk = [tevc]
    B.flush()
    sstk2.close()
    sstk.close()
    return out


def attn_b_passA(mk, W, Sin, QROT, BQ, SKV):
    nc, B = mk.nc, mk.B
    tag = "ba"
    TILE = 512
    NSUB = 4
    NT = SEQ // TILE
    NSEQ_L = NSEQ
    if "nt1" in mk.dbg:
        NT, NSEQ_L = 1, 1

    stk = contextlib.ExitStack()
    sb = lambda n, shp, dt: mk.sbuf(tag + n, shp, dt, stack=stk)
    idb = mk.identb

    su = attn_b_setup(mk, W, tag, stk, need_wqkv=True)
    wq, wbd = su["wqkvT"], su["wbd"]
    csb = sb("_csb", (128, 2048), BF16)
    ssb = sb("_ssb", (128, 2048), BF16)
    t1c = B.dma("gpsimd", csb[:], ap(W["tab_csb"], 0, [[2048, 128], [1, 2048]]),
                inc="wld", serial=True)
    t2c = B.dma("gpsimd", ssb[:], ap(W["tab_ssb"], 0, [[2048, 128], [1, 2048]]),
                inc="wld", serial=True)
    B.flush()

    xt = [sb(f"_xt{i}", (128, NSUB * 1024), F32) for i in range(2)]
    h0b = sb("_h0b", (128, NSUB * 1024), BF16)
    h_fm = sb("_hfm", (128, 8 * 512), BF16)
    qk_sb = sb("_qksb", (128, 8 * 512), BF16)
    v65 = sb("_v65", (128, 1040), BF16)
    qrot = sb("_qrot", (128, 3 * 1024), BF16)
    krot_s = sb("_krot", (128, 1024), BF16)
    k_fm = sb("_kfm", (128, 8 * 512), BF16)
    kphi_s = sb("_kphi", (128, 512), BF16)
    sq_scr = sb("_sqscr", (128, 1024), F32)
    nq = sb("_nq", (128, 48), F32)
    stats = sb("_stats", (128, 24), F32)
    rr = sb("_rr", (128, 4), F32)
    skv_sb = sb("_skv", (128, 2080), F32)

    ps_tr = mk.psum(tag + "_pstr", (128, 1024), BF16, stk)
    ps_A = mk.psum(tag + "_psA", (128, 512), F32, stk)
    ps_B = mk.psum(tag + "_psB", (128, 512), F32, stk)
    ps_S = [mk.psum(tag + f"_psS{i}", (128, 455), F32, stk) for i in range(5)]

    def s_slice(h, mc):
        a = 2 * h + mc
        return ps_S[a // 7][:, 65 * (a % 7):65 * (a % 7) + 65]

    wtok = ("wld", B.sems["wld"][1])
    P = {
        "xt_free": [None, None], "h0b_free": wtok, "hfm_free": wtok,
        "qk_free": [wtok, wtok], "v_free": wtok, "qrot_free": wtok, "krot_free": wtok,
        "kfm_free": wtok, "kphi_free": wtok, "sq_free": wtok, "nq_free": wtok,
        "stats_free": wtok, "rr_free": wtok, "tr_free": wtok,
        "A_free": wtok, "B_free": wtok, "S_rd": wtok, "skv_free": wtok,
        "bq_dma": None,
    }
    tz = B.op("vector", lambda e: e.memset(v65[:], 0.0), inc="dv")
    P["v_free"] = tz

    LNE = NSUB * 1024
    for s in range(NSEQ_L):
        tsz = []
        for i in range(5):
            tsz.append(B.op("vector", lambda e, i=i: e.memset(ps_S[i][:], 0.0),
                            after=[P["S_rd"]], inc="dv"))
        s_started = False
        for it in range(NT):
            bi = it % 2
            x_t = xt[bi]
            r0 = s * SEQ + it * TILE
            t_ld = B.dma("sync", ap(x_t, 0, [[LNE, 128], [1024, NSUB], [1, 1024]]),
                         ap(Sin, r0 * 1024, [[1024, 128], [128 * 1024, NSUB], [1, 1024]]),
                         after=[P["xt_free"][bi]], inc=f"ld{bi}")
            tstats = []
            tpr = [t_ld, P["stats_free"]]
            for si in range(NSUB):
                o = 1024 * si
                b1 = B.op("vector", lambda e, o=o, x_t=x_t: e.bn_stats(
                    stats[:, 0:6], x_t[:, o:o + 512]), after=tpr, inc="dv")
                b2 = B.op("vector", lambda e, o=o, x_t=x_t: e.bn_stats(
                    stats[:, 6:12], x_t[:, o + 512:o + 1024]), after=[b1], inc="dv")
                tg = B.op("vector", lambda e, si=si: e.bn_aggr(
                    stats[:, 12 + 2 * si:14 + 2 * si], stats[:, 0:12]),
                    after=[b1, b2], inc="dv")
                tstats.append(tg)
                tpr = [t_ld, tg]
            tl1 = B.op("scalar", lambda e: e.activation(
                out=rr[:, 0:NSUB], in_=ap(stats, 13, [[24, 128], [2, NSUB]]),
                func=AF.Ln, bias=mk.cbias(1e-5), scale=1.0),
                after=tstats + [P["rr_free"]], inc="ac")
            tr1 = B.op("scalar", lambda e: e.activation(
                out=rr[:, 0:NSUB], in_=rr[:, 0:NSUB], func=AF.Exp, scale=-0.5),
                after=[tl1], inc="ac")
            t_h0 = []
            for si in range(NSUB):
                o = 1024 * si
                mu = ap(stats, 12 + 2 * si, [[24, 128], [1, 1]])
                t_h0.append(B.op("vector", lambda e, o=o, si=si, mu=mu, x_t=x_t: e.tensor_scalar(
                    h0b[:, o:o + 1024], x_t[:, o:o + 1024], mu, rr[:, si:si + 1],
                    ALU.subtract, ALU.mult), after=[tr1, P["h0b_free"]], inc="dv"))
            P["stats_free"] = t_h0[-1]
            P["rr_free"] = t_h0[-1]
            P["xt_free"][bi] = t_h0[-1]

            t_hs = []
            prev_tt = [P["tr_free"]]
            for c in range(8):
                tts = []
                for si in range(NSUB):
                    tts.append(B.op("tensor", lambda e, c=c, si=si: e.transpose(
                        ps_tr[:, 128 * si:128 * (si + 1)],
                        h0b[:, 1024 * si + 128 * c:1024 * si + 128 * (c + 1)], idb[:]),
                        after=t_h0 + prev_tt, inc="pe"))
                te = B.op("vector", lambda e, c=c: e.tensor_copy(
                    h_fm[:, 512 * c:512 * (c + 1)], ps_tr[0:128, 0:512]),
                    after=tts + [P["hfm_free"]], inc="dv")
                t_hs.append(te)
                prev_tt = [te]
            P["tr_free"] = t_hs[-1]
            P["h0b_free"] = t_hs[-1]

            pAB = [ps_A, ps_B]
            gAB = [P["A_free"], P["B_free"]]
            t_qk = [None] * NSUB
            t_v = [None] * NSUB
            nmm = 0
            for si in range(NSUB):
                for grp in range(6):
                    pp = pAB[nmm % 2]
                    gd = gAB[nmm % 2]
                    tms = []
                    for kc in range(8):
                        tms.append(B.op("tensor",
                            lambda e, si=si, grp=grp, kc=kc, pp=pp: e.matmul(
                                pp[:], h_fm[:, 512 * kc + 128 * si:512 * kc + 128 * (si + 1)],
                                wq[:, 3072 * kc + 512 * grp:3072 * kc + 512 * (grp + 1)],
                                start=(kc == 0), stop=(kc == 7)),
                            after=(t_hs + [gd] if kc == 0 else []), inc="pe"))
                    if grp < 4:
                        te = B.op("scalar", lambda e, si=si, grp=grp, pp=pp: e.copy(
                            out=qk_sb[:, 2048 * (grp // 2) + 1024 * (si % 2) + 512 * (grp % 2):
                                      2048 * (grp // 2) + 1024 * (si % 2) + 512 * (grp % 2) + 512],
                            in_=pp[:]), after=tms + [P["qk_free"][si % 2]], inc="ac")
                        t_qk[si] = te
                    else:
                        te = B.op("scalar", lambda e, grp=grp, pp=pp: e.copy(
                            out=ap(v65, 65 * 8 * (grp - 4), [[1040, 128], [65, 8], [1, 64]]),
                            in_=pp[:]), after=tms + [P["v_free"]], inc="ac")
                        t_v[si] = te
                    gAB[nmm % 2] = te
                    nmm += 1

                qoff = 1024 * (si % 2)
                koff = 2048 + 1024 * (si % 2)
                jj = ((r0 // 128) + si) % 32
                t_rot = {}
                for part, off in ((0, qoff), (1, koff)):
                    src = qk_sb[:, off:off + 1024]
                    tsq = B.op("scalar", lambda e, src=src: e.activation(
                        out=sq_scr[:], in_=src, func=AF.Square),
                        after=[t_qk[si], P["sq_free"]], inc="ac")
                    trd = B.op("vector", lambda e, part=part, si=si: e.reduce_sum(
                        nq[:, 16 * part:16 * part + 16],
                        sq_scr[:].rearrange("p (h d) -> p h d", d=64), axis=AX.X),
                        after=[tsq, P["nq_free"]], inc="dv")
                    P["sq_free"] = trd
                    tt1 = B.op("vector", lambda e, src=src, jj=jj: e.tensor_tensor(
                        qrot[:, 1024:2048].rearrange("p (h d) -> p h d", d=64),
                        src.rearrange("p (h d) -> p h d", d=64),
                        ap(csb, 64 * jj, [[2048, 128], [0, 16], [1, 64]]), ALU.mult),
                        after=[t_qk[si], P["qrot_free"]], inc="dv")
                    tt2 = B.op("vector", lambda e, src=src, jj=jj: e.tensor_tensor(
                        qrot[:, 2048:3072].rearrange("p (h s d) -> p h s d", d=32, s=2),
                        ap(src.tensor, src.offset + 32, [[src.ap[0][0], 128], [64, 16], [-32, 2], [1, 32]]),
                        ap(ssb, 64 * jj, [[2048, 128], [0, 16], [32, 2], [1, 32]]), ALU.mult),
                        after=[t_qk[si], tt1], inc="dv")
                    dst = qrot[:, 0:1024] if part == 0 else krot_s[:]
                    t3 = B.op("vector", lambda e, dst=dst: e.tensor_add(
                        dst, qrot[:, 1024:2048], qrot[:, 2048:3072]),
                        after=[tt1, tt2, P["qrot_free"] if part == 0 else P["krot_free"]],
                        inc="dv")
                    t_rot[part] = t3
                    if part == 0:
                        P["qrot_free"] = t3
                    else:
                        P["krot_free"] = t3
                P["qk_free"][si % 2] = [t_rot[0], t_rot[1]]
                tbq = B.dma("sync", ap(BQ, (r0 + 128 * si) * 16, [[16, 128], [1, 16]]),
                            nq[:, 0:16], after=[trd, P["bq_dma"]], inc="st")
                P["bq_dma"] = tbq
                tbk = B.op("vector", lambda e: e.tensor_scalar(
                    nq[:, 32:48], nq[:, 16:32], -0.5, mk.cbias(-2.7725887222397811)[0:128, 0:1] if False else -2.7725887222397811,
                    ALU.mult, ALU.add), after=[trd], inc="dv")
                P["nq_free"] = [tbk, tbq]
                tqs = B.dma("sync", ap(QROT, (r0 + 128 * si) * 1024, [[1024, 128], [1, 1024]]),
                            qrot[:, 0:1024], after=[t_rot[0]], inc="sto")
                P["qrot_free"] = tqs
                tts = []
                for c in range(8):
                    tts.append(B.op("tensor", lambda e, c=c: e.transpose(
                        ps_tr[:, 128 * c:128 * (c + 1)],
                        krot_s[:, 128 * c:128 * (c + 1)], idb[:]),
                        after=[t_rot[1], P["tr_free"]], inc="pe"))
                tkf = B.op("vector", lambda e, si=si: e.tensor_copy(
                    ap(k_fm, 128 * si, [[8 * 512, 128], [512, 8], [1, 128]]),
                    ps_tr[0:128, 0:1024].rearrange("p (c d) -> p c d", d=128)),
                    after=tts + [P["kfm_free"]], inc="dv")
                P["tr_free"] = tkf
                P["krot_free"] = tkf
                t_s_mm = []
                for hp in range(8):
                    pp = pAB[nmm % 2]
                    gd = gAB[nmm % 2]
                    tkp = B.op("tensor", lambda e, hp=hp, si=si, pp=pp: e.matmul(
                        pp[:], k_fm[:, 512 * hp + 128 * si:512 * hp + 128 * (si + 1)],
                        wbd[:, 512 * hp:512 * (hp + 1)], start=True, stop=True),
                        after=[tkf, gd], inc="pe")
                    texs = []
                    for par in range(2):
                        h = 2 * hp + par
                        texs.append(B.op("scalar", lambda e, pp=pp, par=par, h=h: e.activation(
                            out=kphi_s[:, 256 * par:256 * (par + 1)],
                            in_=pp[:, 256 * par:256 * (par + 1)], func=AF.Exp,
                            bias=nq[:, 32 + h:33 + h]),
                            after=[tkp, tbk, P["kphi_free"]], inc="ac"))
                    gAB[nmm % 2] = texs[-1]
                    nmm += 1
                    smm = []
                    for par in range(2):
                        h = 2 * hp + par
                        for mc in range(2):
                            smm.append(B.op("tensor",
                                lambda e, h=h, par=par, mc=mc: e.matmul(
                                    s_slice(h, mc),
                                    kphi_s[:, 256 * par + 128 * mc:256 * par + 128 * (mc + 1)],
                                    v65[:, 65 * h:65 * h + 65],
                                    start=False, stop=False, skip_group_check=True),
                                after=texs + [t_v[si]] + tsz, inc="pe"))
                    t_s_mm += smm
                    P["kphi_free"] = smm[-1]
                P["S_rd"] = t_s_mm[-1]
                P["v_free"] = t_s_mm[-1]
            P["hfm_free"] = tms[-1]

        tevs = []
        for i in range(5):
            w = 455 if i < 4 else 260
            tevs.append(B.op("vector", lambda e, i=i, w=w: e.tensor_copy(
                skv_sb[:, 455 * i:455 * i + w], ps_S[i][:, 0:w]),
                after=[P["S_rd"], P["skv_free"]], inc="dv"))
        P["S_rd"] = tevs[-1]
        tskv = B.dma("sync", ap(SKV, s * 128 * 2080, [[2080, 128], [1, 2080]]),
                     skv_sb[:], after=tevs, inc="st")
        P["skv_free"] = tskv

    B.wait_all("sync", "st")
    B.wait_all("sync", "sto")
    B.flush()
    stk.close()


def attn_b_passB(mk, W, Sin, QROT, BQ, SKV, Sout):
    nc, B = mk.nc, mk.B
    tag = "bb"
    TILE = 512
    NSUB = 4
    NT = SEQ // TILE
    NSEQ_L = NSEQ
    if "nt1" in mk.dbg:
        NT, NSEQ_L = 1, 1

    stk = contextlib.ExitStack()
    sb = lambda n, shp, dt: mk.sbuf(tag + n, shp, dt, stack=stk)
    idb, idf = mk.identb, mk.identf

    su = attn_b_setup(mk, W, tag, stk, need_wqkv=False)
    wbd = su["wbd"]
    woutT = sb("_woutT", (128, 8 * 1024), BF16)
    sstk = contextlib.ExitStack()
    wtmp = mk.sbuf(tag + "_wt2", (128, 1024), F32, stack=sstk)
    ps_w = mk.psum(tag + "_psw", (128, 512), F32, sstk)
    prev = [None]
    for r in range(8):
        tl = B.dma("gpsimd", wtmp[:],
                   ap(W["L0_attn_b_w_out"], r * 128 * 1024, [[1024, 128], [1, 1024]]),
                   after=prev, inc="wld", serial=True)
        prev = []
        for c in range(8):
            tt = B.op("tensor", lambda e, c=c: e.transpose(
                ps_w[:, 128 * (c % 4):128 * (c % 4) + 128],
                wtmp[:, 128 * c:128 * (c + 1)], idf[:]),
                after=[tl] + prev[-1:], inc="pe")
            tv = B.op("vector", lambda e, c=c, r=r: e.tensor_copy(
                woutT[:, 1024 * c + 128 * r:1024 * c + 128 * (r + 1)],
                ps_w[:, 128 * (c % 4):128 * (c % 4) + 128]), after=[tt], inc="dv")
            prev.append(tv)
    B.flush()
    sstk.close()

    xt = [sb(f"_xt{i}", (128, NSUB * 1024), F32) for i in range(2)]
    qr = [sb(f"_qr{i}", (128, NSUB * 1024), BF16) for i in range(2)]
    bqb = [sb(f"_bq{i}", (128, 64), F32) for i in range(2)]
    skv_b = sb("_skvb", (128, 2080), BF16)
    z_sb = sb("_z", (128, 64), F32)
    q_fm = sb("_qfm", (128, 8 * 512), BF16)
    qphi = sb("_qphi", (128, 4 * 512), BF16)
    ons = sb("_ons", (128, 64), BF16)
    oh_fm = sb("_ohfm", (128, 8 * 512), BF16)
    wsc = sb("_wsc", (128, 4), F32)
    y2 = sb("_y2", (128, NSUB * 1024), F32)

    ps_tr = mk.psum(tag + "_pstr", (128, 1024), BF16, stk)
    ps_A = mk.psum(tag + "_psA", (128, 512), F32, stk)
    ps_B2 = mk.psum(tag + "_psB", (128, 512), F32, stk)
    ps_N0 = mk.psum(tag + "_psN0", (128, 65), F32, stk)
    ps_N1 = mk.psum(tag + "_psN1", (128, 65), F32, stk)
    ps_G = mk.psum(tag + "_psG", (128, 1024), F32, stk)

    wtok = ("wld", B.sems["wld"][1])
    P = {
        "xt_free": [None, None], "qr_free": [None, None], "bq_free": [None, None],
        "z_free": wtok, "qfm_free": wtok, "qphi_free": [wtok, wtok],
        "ons_free": wtok, "oh_free": wtok, "wsc_free": wtok,
        "tr_free": wtok, "A_free": wtok, "B_free": wtok,
        "N_free": [wtok, wtok], "G_free": wtok, "y2_free": wtok, "skv_free": wtok,
    }
    LNE = NSUB * 1024
    for s in range(NSEQ_L):
        tskv = B.dma("gpsimd", skv_b[:], ap(SKV, s * 128 * 2080, [[2080, 128], [1, 2080]]),
                     after=[P["skv_free"]], inc="wld", serial=True)
        for it in range(NT):
            bi = it % 2
            x_t, q_t, bq_t = xt[bi], qr[bi], bqb[bi]
            r0 = s * SEQ + it * TILE
            t_ld = B.dma("sync", ap(x_t, 0, [[LNE, 128], [1024, NSUB], [1, 1024]]),
                         ap(Sin, r0 * 1024, [[1024, 128], [128 * 1024, NSUB], [1, 1024]]),
                         after=[P["xt_free"][bi]], inc=f"ld{bi}")
            t_lq = B.dma("sync", ap(q_t, 0, [[LNE, 128], [1024, NSUB], [1, 1024]]),
                         ap(QROT, r0 * 1024, [[1024, 128], [128 * 1024, NSUB], [1, 1024]]),
                         after=[P["qr_free"][bi]], inc=f"ld{bi}")
            t_lb = B.dma("sync", ap(bq_t, 0, [[64, 128], [16, NSUB], [1, 16]]),
                         ap(BQ, r0 * 16, [[16, 128], [128 * 16, NSUB], [1, 16]]),
                         after=[P["bq_free"][bi]], inc=f"ld{bi}")
            ld_tok = (f"ld{bi}", B.sems[f"ld{bi}"][1])

            tz = B.op("scalar", lambda e, bq_t=bq_t: e.activation(
                out=z_sb[:], in_=bq_t[:], func=AF.Exp,
                bias=mk.cbias(-2.7725887222397811), scale=-0.5),
                after=[ld_tok, P["z_free"]], inc="ac")

            t_qf = []
            prev_tt = [P["tr_free"]]
            for c in range(8):
                tts = []
                for si in range(NSUB):
                    tts.append(B.op("tensor", lambda e, c=c, si=si, q_t=q_t: e.transpose(
                        ps_tr[:, 128 * si:128 * (si + 1)],
                        q_t[:, 1024 * si + 128 * c:1024 * si + 128 * (c + 1)], idb[:]),
                        after=[ld_tok] + prev_tt, inc="pe"))
                te = B.op("vector", lambda e, c=c: e.tensor_copy(
                    q_fm[:, 512 * c:512 * (c + 1)], ps_tr[0:128, 0:512]),
                    after=tts + [P["qfm_free"]], inc="dv")
                t_qf.append(te)
                prev_tt = [te]
            P["tr_free"] = t_qf[-1]
            P["qr_free"][bi] = t_qf[-1]

            pAB = [ps_A, ps_B2]
            pN = [ps_N0, ps_N1]
            t_oh = []
            nmm = 0
            nn = 0
            for hp in range(8):
                t_qp = []
                for j in range(4):
                    pp = pAB[nmm % 2]
                    gd = P["A_free"] if nmm % 2 == 0 else P["B_free"]
                    tm = B.op("tensor", lambda e, hp=hp, j=j, pp=pp: e.matmul(
                        pp[:], wbd[:, 512 * hp + 128 * j:512 * hp + 128 * (j + 1)],
                        q_fm[:, 512 * hp:512 * (hp + 1)], start=True, stop=True),
                        after=t_qf + [gd], inc="pe")
                    tex = B.op("scalar", lambda e, j=j, pp=pp: e.activation(
                        out=qphi[:, 512 * j:512 * (j + 1)], in_=pp[:], func=AF.Exp),
                        after=[tm, P["qphi_free"][j // 2]], inc="ac")
                    if nmm % 2 == 0:
                        P["A_free"] = tex
                    else:
                        P["B_free"] = tex
                    t_qp.append(tex)
                    nmm += 1
                for par in range(2):
                    h = 2 * hp + par
                    for si in range(NSUB):
                        pp = pN[nn % 2]
                        gd = P["N_free"][nn % 2]
                        nmms = []
                        for mc in range(2):
                            a = 2 * h + mc
                            nmms.append(B.op("tensor",
                                lambda e, par=par, mc=mc, si=si, a=a, pp=pp: e.matmul(
                                    pp[:],
                                    qphi[:, 512 * (2 * par + mc) + 128 * si:
                                         512 * (2 * par + mc) + 128 * (si + 1)],
                                    skv_b[:, 65 * a:65 * (a + 1)],
                                    start=(mc == 0), stop=(mc == 1)),
                                after=[t_qp[2 * par + mc], gd] + ([tskv] if nn < 2 else []),
                                inc="pe"))
                        zc = z_sb[:, 16 * si + h:16 * si + h + 1]
                        c1 = B.op("vector", lambda e, pp=pp, zc=zc: e.tensor_scalar(
                            wsc[:, 0:1], pp[:, 64:65], zc, None, ALU.mult),
                            after=nmms + [tz, P["wsc_free"]], inc="dv")
                        c2 = B.op("vector", lambda e: e.tensor_scalar(
                            wsc[:, 1:2], wsc[:, 0:1], 1e-6, None, ALU.max),
                            after=[c1], inc="dv")
                        c3 = B.op("vector", lambda e: e.reciprocal(
                            wsc[:, 2:3], wsc[:, 1:2]), after=[c2], inc="dv")
                        c4 = B.op("vector", lambda e, zc=zc: e.tensor_scalar(
                            wsc[:, 3:4], wsc[:, 2:3], zc, None, ALU.mult),
                            after=[c3], inc="dv")
                        t_on = B.op("vector", lambda e, pp=pp: e.tensor_scalar(
                            ons[:], pp[:, 0:64], wsc[:, 3:4], None, ALU.mult),
                            after=[c4, P["ons_free"]], inc="dv")
                        P["N_free"][nn % 2] = t_on
                        P["wsc_free"] = t_on
                        nn += 1
                        ttr = B.op("tensor", lambda e: e.transpose(
                            ps_tr[0:64, 0:128], ons[:], idb[:]),
                            after=[t_on, P["tr_free"]], inc="pe")
                        tev = B.op("vector", lambda e, par=par, hp=hp, si=si: e.tensor_copy(
                            oh_fm[64 * par:64 * par + 64,
                                  512 * hp + 128 * si:512 * hp + 128 * (si + 1)],
                            ps_tr[0:64, 0:128]), after=[ttr, P["oh_free"]], inc="dv")
                        P["tr_free"] = tev
                        P["ons_free"] = tev
                        t_oh.append(tev)
                P["qphi_free"] = [t_oh[-1], t_oh[-1]]

            t_y2 = []
            for si in range(NSUB):
                toms = []
                for ec in range(8):
                    for ng in range(2):
                        toms.append(B.op("tensor",
                            lambda e, ec=ec, ng=ng, si=si: e.matmul(
                                ps_G[:, 512 * ng:512 * (ng + 1)],
                                oh_fm[:, 512 * ec + 128 * si:512 * ec + 128 * (si + 1)],
                                woutT[:, 1024 * ec + 512 * ng:1024 * ec + 512 * (ng + 1)],
                                start=(ec == 0), stop=(ec == 7)),
                            after=(t_oh + [P["G_free"]] if ec == 0 else []), inc="pe"))
                ty = B.op("vector", lambda e, si=si, x_t=x_t: e.tensor_add(
                    y2[:, 1024 * si:1024 * (si + 1)], x_t[:, 1024 * si:1024 * (si + 1)],
                    ps_G[:]), after=toms + [P["y2_free"]], inc="dv")
                P["G_free"] = ty
                t_y2.append(ty)
            P["xt_free"][bi] = t_y2[-1]
            P["bq_free"][bi] = tz
            P["oh_free"] = [toms[-1], t_y2[-1]]
            P["z_free"] = c4
            tso = B.dma("sync", ap(Sout, r0 * 1024, [[1024, 128], [128 * 1024, NSUB], [1, 1024]]),
                        ap(y2, 0, [[LNE, 128], [1024, NSUB], [1, 1024]]),
                        after=t_y2, inc="st")
            P["y2_free"] = tso
        P["skv_free"] = nmms[-1]

    B.wait_all("sync", "st")
    B.flush()
    stk.close()


_HC = None


def get_hc():
    global _HC
    if _HC is None:
        _HC = host_consts()
    return _HC


def make_in_maps(inputs, extra_streams=None):
    hc = get_hc()
    x = np.ascontiguousarray(np.asarray(inputs["L0_attn_a_x"], dtype=np.float32))
    in_maps = []
    for c in range(NCORES):
        m = {"x": np.ascontiguousarray(x[2 * c:2 * c + 2].reshape(TOK, DM))}
        for k, v in inputs.items():
            if k == "L0_attn_a_x":
                continue
            m[k] = np.ascontiguousarray(np.asarray(v, dtype=np.float32))
        for k, v in hc.items():
            m[k] = v
        if extra_streams:
            for k, v in extra_streams.items():
                m[k] = np.ascontiguousarray(v[c])
        in_maps.append(m)
    return in_maps


def kernel(**inputs):
    mk = build_program()
    in_maps = make_in_maps(inputs)
    res = run_bass_kernel_spmd(mk.nc, in_maps, core_ids=list(range(NCORES)))
    out = np.concatenate([r["OUT"].reshape(2, SEQ, DM) for r in res.results], 0)
    return out
